# version 01
# baseline (speedup 1.0000x reference)
import numpy as np
import ml_dtypes as md
import concourse.bass as bass
import concourse.mybir as mybir
from concourse.bass_core_rust import IndirectOffsetOnAxis
from concourse.bass_utils import run_bass_kernel_spmd

F32 = mybir.dt.float32
BF16 = mybir.dt.bfloat16
I32 = mybir.dt.int32
AF = mybir.ActivationFunctionType
OP = mybir.AluOpType
AX = mybir.AxisListType

B, T, D = 2, 8192, 1024
NTOK = 2048
NQB = 4
QB = 512
NCH = 8
NH, HS, KVL = 16, 64, 128
EPS = 1e-5
MOE_E, MOE_H, MOE_M = 8, 512, 2048
LORA_R = 32

bf = md.bfloat16


def _core_blocks(c):
    b, i = c // 4, c % 4
    return [16 * b + i, 16 * b + 7 - i, 16 * b + 8 + i, 16 * b + 15 - i]


def _block_owner(g):
    b, gg = g // 16, g % 16
    if gg <= 3:
        i, j = gg, 0
    elif gg <= 7:
        i, j = 7 - gg, 1
    elif gg <= 11:
        i, j = gg - 8, 2
    else:
        i, j = 15 - gg, 3
    return 4 * b + i, j


class Ctr:
    def __init__(self, nc, name):
        self.h = nc.alloc_semaphore(name)
        self.v = 0


class Arena:
    def __init__(self, nc, name, n, dtype):
        self.t = nc.alloc_sbuf_tensor(name, [128, n], dtype)
        self.n = n
        self.off = 0
        self.marks = []

    def take(self, n):
        assert self.off + n <= self.n, f"arena {self.t.name} overflow {self.off}+{n}>{self.n}"
        off = self.off
        self.off += n
        return self.t[:, off:off + n]

    def push(self):
        self.marks.append(self.off)

    def pop(self):
        self.off = self.marks.pop()


def build_program(taps=(), upto=None):
    nc = bass.Bass("TRN2", target_bir_lowering=False)

    def inp(name, shape, dtype=F32):
        return nc.dram_tensor(name, shape, dtype, kind="ExternalInput")

    xT_in = inp("xT", [D, NTOK])
    w_dqT = inp("w_dqT", [D, KVL])
    w_dq = inp("w_dq", [KVL, D])
    w_uq = inp("w_uq", [D, KVL])
    w_uqT = inp("w_uqT", [KVL, D])
    w_dkvT = inp("w_dkvT", [D, KVL])
    w_uk = inp("w_uk", [D, KVL])
    w_uv = inp("w_uv", [D, KVL])
    w_oT = inp("w_oT", [D, D])
    sw1T = inp("sw1T", [D, MOE_H])
    sw2T = inp("sw2T", [D, MOE_H])
    sw3T = inp("sw3T", [MOE_H, D])
    sb1 = inp("sb1", [MOE_H, 1])
    sb2 = inp("sb2", [MOE_H, 1])
    sb3 = inp("sb3", [D, 1])
    s_ln_g = inp("s_ln_g", [MOE_H, 1])
    s_ln_b = inp("s_ln_b", [MOE_H, 1])
    gwT = inp("gwT", [D, 16])
    gb = inp("gb", [16, 1])
    exp_bias = inp("exp_bias", [MOE_E, 1])
    ew1T = inp("ew1T", [D, MOE_H])
    ew2T = inp("ew2T", [D, MOE_H])
    ew3T = inp("ew3T", [MOE_H, D])
    eb1 = inp("eb1", [1, MOE_H])
    eb2 = inp("eb2", [1, MOE_H])
    eb3 = inp("eb3", [1, D])
    sel_ln_g = inp("sel_ln_g", [1, MOE_H])
    sel_ln_b = inp("sel_ln_b", [1, MOE_H])
    idx_b = inp("idx_b", [MOE_M], I32)
    idx_t = inp("idx_t", [MOE_M], I32)
    core_meta = inp("core_meta", [128, 4], I32)
    my_onehot = inp("my_onehot", [128, 8])
    q0s = inp("q0s", [128, 4])
    ktpen = inp("ktpen", [128, 256])
    meta_f = inp("meta_f", [128, 4])
    w_qkvT = inp("w_qkvT", [D, D])
    w_l1oT = inp("w_l1oT", [D, D])
    qwT = inp("qwT", [6, D, D], I32)
    scT = inp("scT", [6, D, D])
    l1_biases = inp("l1_biases", [6, D])
    laT = inp("laT", [6, D, LORA_R])
    lbT = inp("lbT", [6, LORA_R, D])
    ln_g = inp("ln_g", [3, D])
    ln_b = inp("ln_b", [3, D])

    y_out = nc.dram_tensor("y", [D, NTOK], F32, kind="ExternalOutput")

    identb = nc.inline_tensor(np.eye(128, dtype=bf), "identb")
    identf = nc.inline_tensor(np.eye(128, dtype=np.float32), "identf")
    ones_col_b = nc.inline_tensor(np.ones((128, 2), dtype=bf), "ones_col_b")
    ones_row_b = nc.inline_tensor(np.ones((1, 128), dtype=bf), "ones_row_b")
    ones_col_f = nc.inline_tensor(np.ones((128, 1), dtype=np.float32), "ones_col_f")
    ones_row_f = nc.inline_tensor(np.ones((1, 128), dtype=np.float32), "ones_row_f")
    jcols0 = nc.inline_tensor((np.arange(128, dtype=np.float32)[:, None] +
                               128.0 * np.arange(16, dtype=np.float32)[None, :]).copy(), "jcols0")
    bigcol0 = nc.inline_tensor(np.full((128, 1), 65536.0, np.float32), "bigcol0")
    colmap0 = nc.inline_tensor(
        np.broadcast_to(np.arange(QB, dtype=np.float32)[None, :], (128, QB)).copy(), "colmap0")
    rowmap0 = nc.inline_tensor(
        (np.arange(128, dtype=np.float32)[:, None] +
         128.0 * np.arange(64, dtype=np.float32)[None, :]).copy(), "rowmap0")

    ckv_contrib = nc.dram_tensor("ckv_contrib", [2 * NTOK, KVL], BF16)
    ckv_ag = nc.dram_tensor("ckv_ag", [4 * 2 * NTOK, KVL], BF16)
    qkv_contrib = nc.dram_tensor("qkv_contrib", [2 * NTOK, D], BF16)
    qkv_ag = nc.dram_tensor("qkv_ag", [4 * 2 * NTOK, D], BF16)
    xm_tm_dram = nc.dram_tensor("xm_tm_dram", [NTOK + 128, D], BF16)
    xmoe_contrib = nc.dram_tensor("xmoe_contrib", [MOE_M, D], BF16)
    xmoe_all = nc.dram_tensor("xmoe_all", [MOE_M, D], BF16)
    delta_contrib = nc.dram_tensor("delta_contrib", [MOE_M, D], F32)
    delta_all = nc.dram_tensor("delta_all", [MOE_M + 128, D], F32)
    own_scale = nc.dram_tensor("own_scale", [NTOK + 128, 1], F32)
    own_delta = nc.dram_tensor("own_delta", [NTOK, D], F32)
    own_inv = nc.dram_tensor("own_inv", [NTOK + 128, 1], F32)
    wq_dram = nc.dram_tensor("wq_dram", [6, D, D], BF16)

    xres = nc.alloc_sbuf_tensor("xres", [128, NCH * NTOK], F32)
    A = Arena(nc, "argA", 16384, BF16)
    Bb = Arena(nc, "argB", 16384, BF16)
    C = Arena(nc, "argC", 16384, BF16)
    Dd = Arena(nc, "argD", 4096, BF16)
    E = Arena(nc, "argE", 8192, F32)
    Si = Arena(nc, "argSi", 256, I32)

    idb_s = nc.alloc_sbuf_tensor("idb_s", [128, 128], BF16)
    idf_s = nc.alloc_sbuf_tensor("idf_s", [128, 128], F32)
    oc_b = nc.alloc_sbuf_tensor("oc_b", [128, 2], BF16)
    or_b = nc.alloc_sbuf_tensor("or_b", [1, 128], BF16)
    oc_f = nc.alloc_sbuf_tensor("oc_f", [128, 1], F32)
    or_f = nc.alloc_sbuf_tensor("or_f", [1, 128], F32)
    colmap_s = nc.alloc_sbuf_tensor("colmap_s", [128, QB], F32)
    rowmap_s = nc.alloc_sbuf_tensor("rowmap_s", [128, 64], F32)
    q0s_s = nc.alloc_sbuf_tensor("q0s_s", [128, 4], F32)
    ktpen_s = nc.alloc_sbuf_tensor("ktpen_s", [128, 256], F32)
    meta_s = nc.alloc_sbuf_tensor("meta_s", [128, 4], I32)
    metaf_s = nc.alloc_sbuf_tensor("metaf_s", [128, 4], F32)
    onehot_s = nc.alloc_sbuf_tensor("onehot_s", [128, 8], F32)

    pb = [nc.alloc_psum_tensor(f"pb{i}", [128, 512], F32) for i in range(7)]
    pbh = nc.alloc_psum_tensor("pbh", [128, 1024], BF16)

    sd = Ctr(nc, "sd")
    sg = Ctr(nc, "sg")
    s1 = Ctr(nc, "s1")
    s2 = Ctr(nc, "s2")
    s3 = Ctr(nc, "s3")
    s4 = Ctr(nc, "s4")
    sv = Ctr(nc, "sv")
    sco = Ctr(nc, "sco")

    tap_tensors = {}

    def tap_fm(name, src, nch=NCH, width=NTOK, dtype=F32):
        if name not in taps:
            return
        t = nc.dram_tensor("tap_" + name, [nch * 128, width], dtype, kind="ExternalOutput")
        tap_tensors[name] = t
        with nc.Block() as blk:
            @blk.sync
            def _(sy):
                for c in range(nch):
                    sy.dma_start(out=t.ap()[c * 128:(c + 1) * 128, :],
                                 in_=src[:, c * width:(c + 1) * width]).then_inc(sd.h, 16)
                sd.v += nch * 16
                sy.wait_ge(sd.h, sd.v)

    def tap_copy_dram(name, src_dram, shape, dtype):
        if name not in taps:
            return
        t = nc.dram_tensor("tap_" + name, shape, dtype, kind="ExternalOutput")
        tap_tensors[name] = t
        with nc.Block() as blk:
            @blk.sync
            def _(sy):
                sy.dma_start(out=t.ap(), in_=src_dram.ap()).then_inc(sd.h, 16)
                sd.v += 16
                sy.wait_ge(sd.h, sd.v)

    with nc.Block() as blk:
        @blk.sync
        def _(sy):
            for c in range(NCH):
                sy.dma_start(out=xres[:, c * NTOK:(c + 1) * NTOK],
                             in_=xT_in.ap()[c * 128:(c + 1) * 128, :]).then_inc(sd.h, 16)
            sy.dma_start(out=idb_s[:], in_=identb.ap()).then_inc(sd.h, 16)
            sy.dma_start(out=idf_s[:], in_=identf.ap()).then_inc(sd.h, 16)
            sy.dma_start(out=oc_b[:], in_=ones_col_b.ap()).then_inc(sd.h, 16)
            sy.dma_start(out=or_b[:], in_=ones_row_b.ap()).then_inc(sd.h, 16)
            sy.dma_start(out=oc_f[:], in_=ones_col_f.ap()).then_inc(sd.h, 16)
            sy.dma_start(out=meta_s[:], in_=core_meta.ap()).then_inc(sd.h, 16)
            sy.dma_start(out=metaf_s[:], in_=meta_f.ap()).then_inc(sd.h, 16)
            sy.dma_start(out=onehot_s[:], in_=my_onehot.ap()).then_inc(sd.h, 16)
            sy.dma_start(out=or_f[:], in_=ones_row_f.ap()).then_inc(sd.h, 16)
            sy.dma_start(out=colmap_s[:], in_=colmap0.ap()).then_inc(sd.h, 16)
            sy.dma_start(out=rowmap_s[:], in_=rowmap0.ap()).then_inc(sd.h, 16)
            sy.dma_start(out=q0s_s[:], in_=q0s.ap()).then_inc(sd.h, 16)
            sy.dma_start(out=ktpen_s[:], in_=ktpen.ap()).then_inc(sd.h, 16)
            sd.v += (NCH + 13) * 16
            sy.wait_ge(sd.h, sd.v)

    def emit_ln(src, dst_b, w=NTOK, dst_f=None, nch=NCH, cs=None, t0=0):
        E.push(); Dd.push()
        r0t = E.take(w)
        r1t = E.take(w)
        r2t = E.take(w)
        sq = E.take(512)
        tsc = E.take(512)
        abb = Dd.take(2 * w)
        nqb = w // QB
        if cs is None:
            cs = w
        srow, qrow = r0t[0:1, :], r1t[0:1, :]
        arow = r2t[0:1, :]

        with nc.Block() as blk:
            b1, b2, b3 = s1.v, s2.v, s3.v

            @blk.scalar
            def _(sc):
                for qb in range(nqb):
                    for c in range(nch):
                        i = qb * nch + c
                        if i > 0:
                            sc.wait_ge(s1.h, b1 + 2 * i)
                        sc.activation(sq[:], src[:, c * cs + t0 + qb * QB: c * cs + t0 + qb * QB + QB],
                                      AF.Square).then_inc(s3.h, 1)
                s3.v += nqb * nch

            @blk.tensor
            def _(te):
                for qb in range(nqb):
                    for c in range(nch):
                        i = qb * nch + c
                        if c == 0 and qb > 0:
                            te.wait_ge(s2.h, b2 + 2 * qb)
                        te.matmul(pb[0][0:1, :], oc_f[:, 0:1],
                                  src[:, c * cs + t0 + qb * QB: c * cs + t0 + qb * QB + QB],
                                  start=(c == 0), stop=(c == nch - 1)).then_inc(s1.h, 1)
                        te.wait_ge(s3.h, b3 + i + 1)
                        te.matmul(pb[1][0:1, :], oc_f[:, 0:1], sq[:],
                                  start=(c == 0), stop=(c == nch - 1)).then_inc(s1.h, 1)
                s1.v += 2 * nqb * nch

            @blk.vector
            def _(ve):
                for qb in range(nqb):
                    ve.wait_ge(s1.h, b1 + 2 * nch * (qb + 1))
                    ve.tensor_copy(srow[0:1, qb * QB:(qb + 1) * QB], pb[0][0:1, :]).then_inc(s2.h, 1)
                    ve.tensor_copy(qrow[0:1, qb * QB:(qb + 1) * QB], pb[1][0:1, :]).then_inc(s2.h, 1)
                s2.v += 2 * nqb

        dd = float(nch * 128)
        with nc.Block() as blk:
            @blk.vector
            def _(ve):
                ve.tensor_scalar(arow, qrow, 1.0 / dd, float(EPS), OP.mult, OP.add)
        with nc.Block() as blk:
            @blk.vector
            def _(ve):
                ve.tensor_scalar(qrow, srow, 1.0 / dd, None, OP.mult)
        with nc.Block() as blk:
            @blk.vector
            def _(ve):
                ve.tensor_tensor(srow, qrow, qrow, OP.mult)
        with nc.Block() as blk:
            @blk.vector
            def _(ve):
                ve.tensor_tensor(srow, arow, srow, OP.subtract)
        with nc.Block() as blk:
            @blk.scalar
            def _(sc):
                sc.activation(arow, srow, AF.Sqrt)
        with nc.Block() as blk:
            @blk.vector
            def _(ve):
                ve.reciprocal(srow, arow)
        with nc.Block() as blk:
            @blk.vector
            def _(ve):
                ve.tensor_tensor(arow, qrow, srow, OP.mult)
        with nc.Block() as blk:
            @blk.vector
            def _(ve):
                ve.tensor_copy(abb[0:1, 0:w], srow)
                ve.tensor_copy(abb[0:1, w:2 * w], arow)

        with nc.Block() as blk:
            b1, b2 = s1.v, s2.v

            @blk.tensor
            def _(te):
                for qb in range(nqb):
                    if qb > 0:
                        te.wait_ge(s2.h, b2 + qb)
                    te.matmul(pb[2][:, :], or_b[0:1, :], abb[0:1, qb * QB:(qb + 1) * QB],
                              start=True, stop=True).then_inc(s1.h, 1)
                    te.matmul(pb[3][:, :], or_b[0:1, :], abb[0:1, w + qb * QB: w + (qb + 1) * QB],
                              start=True, stop=True).then_inc(s1.h, 1)
                s1.v += 2 * nqb

            @blk.vector
            def _(ve):
                for qb in range(nqb):
                    ve.wait_ge(s1.h, b1 + 2 * qb + 2)
                    for c in range(nch):
                        sl = slice(c * w + qb * QB, c * w + qb * QB + QB)
                        ssl = slice(c * cs + t0 + qb * QB, c * cs + t0 + qb * QB + QB)
                        ve.tensor_tensor(tsc[:], src[:, ssl], pb[2][:, :], OP.mult)
                        if dst_f is not None:
                            ve.tensor_tensor(dst_f[:, sl], tsc[:], pb[3][:, :], OP.subtract)
                            bi = ve.tensor_copy(dst_b[:, sl], dst_f[:, sl])
                        else:
                            bi = ve.tensor_tensor(dst_b[:, sl], tsc[:], pb[3][:, :], OP.subtract)
                        if c == nch - 1:
                            bi.then_inc(s2.h, 1)
                s2.v += nqb
        E.pop(); Dd.pop()

    def emit_wload(wdram, dst, nblk, cols, rows=128):
        wap = wdram if isinstance(wdram, bass.AP) else wdram.ap()
        if wap.tensor.dtype == BF16:
            with nc.Block() as blk:
                @blk.sync
                def _(sy):
                    for i in range(nblk):
                        sy.dma_start(out=dst[0:rows, i * cols:(i + 1) * cols],
                                     in_=wap[i * rows:(i + 1) * rows, :]).then_inc(sd.h, 16)
                    sd.v += nblk * 16
                    sy.wait_ge(sd.h, sd.v)
            return
        E.push()
        stg = [E.take(cols) for _ in range(2)]
        with nc.Block() as blk:
            b4 = s4.v
            bd = sd.v

            @blk.sync
            def _(sy):
                for i in range(nblk):
                    if i >= 2:
                        sy.wait_ge(s4.h, b4 + i - 1)
                    sy.dma_start(out=stg[i % 2][0:rows, :],
                                 in_=wap[i * rows:(i + 1) * rows, :]).then_inc(sd.h, 16)
                sd.v += nblk * 16

            @blk.gpsimd
            def _(gp):
                for i in range(nblk):
                    gp.wait_ge(sd.h, bd + 16 * (i + 1))
                    gp.tensor_copy(dst[0:rows, i * cols:(i + 1) * cols],
                                   stg[i % 2][0:rows, :]).then_inc(s4.h, 1)
                s4.v += nblk
        E.pop()

    A.push()
    xn = A.take(NCH * NTOK)
    emit_ln(xres[:, :], xn)
    tap_fm("xn", xn, dtype=BF16)

    def emit_store_y():
        with nc.Block() as blk:
            @blk.sync
            def _(sy):
                for c in range(NCH):
                    sy.dma_start(out=y_out.ap()[c * 128:(c + 1) * 128, :],
                                 in_=xres[:, c * NTOK:(c + 1) * NTOK]).then_inc(sd.h, 16)
                sd.v += NCH * 16
                sy.wait_ge(sd.h, sd.v)

    if upto == "P1":
        emit_store_y()
        return nc, tap_tensors

    keff_b = C.take(NCH * KVL)
    v1_b = C.take(NCH * KVL)
    C.push()
    wdkvT_b = C.take(NCH * KVL)
    wdqT_b = C.take(NCH * KVL)
    wuqT_b = C.take(D)
    wAux_b = C.take(NCH * KVL)
    wuk_b = C.take(NCH * KVL)
    wdq_b = C.take(D)
    c1_b = C.take(KVL)

    emit_wload(w_dkvT, wdkvT_b, NCH, KVL)
    emit_wload(w_dqT, wdqT_b, NCH, KVL)
    emit_wload(w_uqT, wuqT_b, 1, D)
    emit_wload(w_uq, wAux_b, NCH, KVL)
    emit_wload(w_uk, wuk_b, NCH, KVL)
    emit_wload(w_dq, wdq_b, 1, D)

    with nc.Block() as blk:
        b1 = s1.v

        @blk.tensor
        def _(te):
            for i in range(NCH):
                te.matmul(pb[0][:, 0:KVL], wAux_b[:, i * KVL:(i + 1) * KVL],
                          wuk_b[:, i * KVL:(i + 1) * KVL],
                          start=(i == 0), stop=(i == NCH - 1)).then_inc(s1.h, 1)
            s1.v += NCH

        @blk.vector
        def _(ve):
            ve.wait_ge(s1.h, b1 + NCH)
            ve.tensor_copy(c1_b[:, :], pb[0][:, 0:KVL])

    with nc.Block() as blk:
        b1, b2 = s1.v, s2.v

        @blk.tensor
        def _(te):
            for c in range(NCH):
                if c >= 2:
                    te.wait_ge(s2.h, b2 + c - 1)
                te.matmul(pb[c % 2][:, 0:KVL], wdq_b[:, c * KVL:(c + 1) * KVL],
                          c1_b[:, :], start=True, stop=True).then_inc(s1.h, 1)
            s1.v += NCH

        @blk.vector
        def _(ve):
            for c in range(NCH):
                ve.wait_ge(s1.h, b1 + c + 1)
                ve.tensor_copy(keff_b[:, c * KVL:(c + 1) * KVL],
                               pb[c % 2][:, 0:KVL]).then_inc(s2.h, 1)
            s2.v += NCH

    emit_wload(w_uv, wAux_b, NCH, KVL)
    Bb.push()
    woT_b = Bb.take(NCH * D)
    emit_wload(w_oT, woT_b, NCH, D)
    with nc.Block() as blk:
        b1, b2 = s1.v, s2.v

        @blk.tensor
        def _(te):
            for j in range(NCH):
                if j >= 2:
                    te.wait_ge(s2.h, b2 + j - 1)
                for c in range(NCH):
                    te.matmul(pb[j % 2][:, 0:KVL], wAux_b[:, c * KVL:(c + 1) * KVL],
                              woT_b[:, c * D + j * 128: c * D + j * 128 + 128],
                              start=(c == 0), stop=(c == NCH - 1)).then_inc(s1.h, 1)
            s1.v += NCH * NCH

        @blk.vector
        def _(ve):
            for j in range(NCH):
                ve.wait_ge(s1.h, b1 + NCH * (j + 1))
                ve.tensor_copy(v1_b[:, j * KVL:(j + 1) * KVL],
                               pb[j % 2][:, 0:KVL]).then_inc(s2.h, 1)
            s2.v += NCH
    Bb.pop()

    if upto == "P2":
        tap_fm("keff", keff_b, nch=1, width=NCH * KVL, dtype=BF16)
        tap_fm("v1", v1_b, nch=1, width=NCH * KVL, dtype=BF16)
        emit_store_y()
        return nc, tap_tensors

    Dd.push()
    ckv_own = Dd.take(NTOK)
    ckv_own_tm = Dd.take(NTOK)
    with nc.Block() as blk:
        b1, b2 = s1.v, s2.v

        @blk.tensor
        def _(te):
            for qb in range(NQB):
                if qb >= 2:
                    te.wait_ge(s2.h, b2 + qb - 1)
                for c in range(NCH):
                    te.matmul(pb[qb % 2][:, :], wdkvT_b[:, c * KVL:(c + 1) * KVL],
                              xn[:, c * NTOK + qb * QB: c * NTOK + qb * QB + QB],
                              start=(c == 0), stop=(c == NCH - 1)).then_inc(s1.h, 1)
            s1.v += NQB * NCH

        @blk.vector
        def _(ve):
            for qb in range(NQB):
                ve.wait_ge(s1.h, b1 + NCH * (qb + 1))
                ve.tensor_copy(ckv_own[:, qb * QB:(qb + 1) * QB],
                               pb[qb % 2][:, :]).then_inc(s2.h, 1)
            s2.v += NQB

    with nc.Block() as blk:
        b1, b2 = s1.v, s2.v

        @blk.tensor
        def _(te):
            for i in range(16):
                if i >= 1:
                    te.wait_ge(s2.h, b2 + i)
                te.transpose(pbh[:, (i % 2) * 128:(i % 2) * 128 + 128],
                             ckv_own[:, i * 128:(i + 1) * 128], idb_s[:]).then_inc(s1.h, 1)
            s1.v += 16

        @blk.vector
        def _(ve):
            for i in range(16):
                ve.wait_ge(s1.h, b1 + i + 1)
                ve.tensor_copy(ckv_own_tm[:, i * 128:(i + 1) * 128],
                               pbh[:, (i % 2) * 128:(i % 2) * 128 + 128]).then_inc(s2.h, 1)
            s2.v += 16

    with nc.Block() as blk:
        @blk.sync
        def _(sy):
            fmv = bass.AP(ckv_contrib, 0, [[NTOK, 128], [1, NTOK]])
            sy.dma_start(out=fmv, in_=ckv_own[:, :]).then_inc(sd.h, 16)
            for i in range(16):
                sy.dma_start(out=ckv_contrib.ap()[NTOK + i * 128: NTOK + (i + 1) * 128, :],
                             in_=ckv_own_tm[:, i * 128:(i + 1) * 128]).then_inc(sd.h, 16)
            sd.v += 17 * 16
            sy.wait_ge(sd.h, sd.v)
    Dd.pop()

    if upto == "P3a":
        emit_store_y()
        return nc, tap_tensors

    groups4 = [[0, 1, 2, 3], [4, 5, 6, 7]]
    with nc.Block() as blk:
        @blk.gpsimd
        def _(gp):
            gp.collective_compute("AllGather", OP.bypass, replica_groups=groups4,
                                  ins=[ckv_contrib.ap()], outs=[ckv_ag.ap()]).then_inc(sco.h, 1)
            sco.v += 1
            gp.wait_ge(sco.h, sco.v)

    if upto == "P3b":
        emit_store_y()
        return nc, tap_tensors

    Bb.push()
    qT = Bb.take(NCH * NTOK)
    Dd.push()
    t1_b = Dd.take(NTOK)
    with nc.Block() as blk:
        b1, b2 = s1.v, s2.v

        @blk.tensor
        def _(te):
            for qb in range(NQB):
                if qb >= 2:
                    te.wait_ge(s2.h, b2 + qb - 1)
                for c in range(NCH):
                    te.matmul(pb[qb % 2][:, :], wdqT_b[:, c * KVL:(c + 1) * KVL],
                              xn[:, c * NTOK + qb * QB: c * NTOK + qb * QB + QB],
                              start=(c == 0), stop=(c == NCH - 1)).then_inc(s1.h, 1)
            s1.v += NQB * NCH

        @blk.vector
        def _(ve):
            for qb in range(NQB):
                ve.wait_ge(s1.h, b1 + NCH * (qb + 1))
                ve.tensor_copy(t1_b[:, qb * QB:(qb + 1) * QB],
                               pb[qb % 2][:, :]).then_inc(s2.h, 1)
            s2.v += NQB

    with nc.Block() as blk:
        b1, b2 = s1.v, s2.v

        @blk.tensor
        def _(te):
            n = 0
            for c in range(NCH):
                for qb in range(NQB):
                    if n >= 2:
                        te.wait_ge(s2.h, b2 + n - 1)
                    te.matmul(pb[n % 2][:, :], wuqT_b[:, c * 128:(c + 1) * 128],
                              t1_b[:, qb * QB:(qb + 1) * QB],
                              start=True, stop=True).then_inc(s1.h, 1)
                    n += 1
            s1.v += NCH * NQB

        @blk.vector
        def _(ve):
            n = 0
            for c in range(NCH):
                for qb in range(NQB):
                    ve.wait_ge(s1.h, b1 + n + 1)
                    ve.tensor_copy(qT[:, c * NTOK + qb * QB: c * NTOK + qb * QB + QB],
                                   pb[n % 2][:, :]).then_inc(s2.h, 1)
                    n += 1
            s2.v += NCH * NQB
    Dd.pop()
    C.pop()

    A.pop()
    A.push()
    ckv_fm = A.take(8 * 1024)
    ckv_tm = A.take(8 * 1024)
    with nc.Block() as blk:
        @blk.sync
        def _(sy):
            for gg in range(16):
                cc, slot = _block_owner(gg)
                cc = cc % 4
                base = cc * 2 * NTOK
                fmv = bass.AP(ckv_ag, base * KVL + slot * QB, [[NTOK, 128], [1, QB]])
                sy.dma_start(out=ckv_fm[:, gg * QB:(gg + 1) * QB], in_=fmv).then_inc(sd.h, 16)
                for j in range(4):
                    kt = gg * 4 + j
                    r0 = base + NTOK + slot * QB + j * 128
                    sy.dma_start(out=ckv_tm[:, kt * 128:(kt + 1) * 128],
                                 in_=ckv_ag.ap()[r0:r0 + 128, :]).then_inc(sd.h, 16)
            sd.v += 16 * 5 * 16
            sy.wait_ge(sd.h, sd.v)

    if upto == "P3":
        tap_fm("ckv_fm", ckv_fm, nch=1, width=8192, dtype=BF16)
        tap_fm("ckv_tm", ckv_tm, nch=1, width=8192, dtype=BF16)
        tap_fm("qT", qT, nch=1, width=NCH * NTOK, dtype=BF16)
        emit_store_y()
        return nc, tap_tensors

    Dd.push()
    qlh = Dd.take(2 * NTOK)
    E.push()
    mrun = E.take(8 * QB)
    permax = E.take(8)
    negm8 = E.take(8)
    biast = E.take(8)
    biastx = E.take(512)
    recipf = E.take(1024)
    rects = E.take(1024)
    C.push()
    pt = [C.take(QB), C.take(QB)]
    ptm = [C.take(QB), C.take(QB)]
    ctxe = [C.take(QB), C.take(QB)]
    scrb = C.take(128)
    negmrow = C.take(16)

    UB = [16, 32, 48, 64]
    MB = [(0, 16), (16, 32), (32, 48), (48, 64)]

    for hp in range(NH // 2):
        g0, g1 = 2 * hp, 2 * hp + 1
        with nc.Block() as blk:
            b1, b2, bv = s1.v, s2.v, sv.v
            ntile = 2 * sum(UB)

            @blk.tensor
            def _(te):
                n = 0
                for h01 in range(2):
                    for qb in range(NQB):
                        if n >= 2:
                            te.wait_ge(s2.h, b2 + n - 1)
                        te.matmul(pb[n % 2][:, :],
                                  keff_b[h01 * 64:(h01 + 1) * 64, hp * 128:(hp + 1) * 128],
                                  qT[h01 * 64:(h01 + 1) * 64,
                                     hp * NTOK + qb * QB: hp * NTOK + (qb + 1) * QB],
                                  start=True, stop=True).then_inc(s1.h, 1)
                        n += 1
                a = 0
                for qb in range(NQB):
                    for kt in range(UB[qb]):
                        for h01 in range(2):
                            if a < 4:
                                te.wait_ge(s2.h, b2 + 8)
                            else:
                                te.wait_ge(s2.h, b2 + 8 + a - 3)
                            te.matmul(pb[a % 4][:, :],
                                      ckv_fm[:, kt * 128:(kt + 1) * 128],
                                      qlh[:, h01 * NTOK + qb * QB: h01 * NTOK + (qb + 1) * QB],
                                      start=True, stop=True).then_inc(s1.h, 1)
                            a += 1
                s1.v += 8 + ntile
                te.wait_ge(s2.h, b2 + 8 + ntile + 1)
                te.transpose(pbh[:, 0:128], scrb[:, 0:128], idb_s[:]).then_inc(s1.h, 1)
                te.wait_ge(s2.h, b2 + 8 + ntile + 2)
                te.transpose(pbh[:, 128:256], scrb[:, 0:128], idb_s[:]).then_inc(s1.h, 1)
                te.wait_ge(s2.h, b2 + 8 + ntile + 3)
                te.matmul(pb[4][:, 0:16], or_b[0:1, :], negmrow[0:1, 0:16],
                          start=True, stop=True).then_inc(s1.h, 1)
                s1.v += 3

            @blk.vector
            def _(ve):
                for n in range(8):
                    h01, qb = divmod(n, NQB)
                    ve.wait_ge(s1.h, b1 + n + 1)
                    ve.tensor_copy(qlh[:, h01 * NTOK + qb * QB: h01 * NTOK + (qb + 1) * QB],
                                   pb[n % 2][:, :]).then_inc(s2.h, 1)
                a = 0
                for qb in range(NQB):
                    for kt in range(UB[qb]):
                        for h01 in range(2):
                            slot = h01 * 4 + qb
                            ve.wait_ge(s1.h, b1 + 8 + a + 1)
                            if kt == 0:
                                ve.tensor_copy(mrun[:, slot * QB:(slot + 1) * QB],
                                               pb[a % 4][:, :]).then_inc(s2.h, 1)
                            else:
                                ve.tensor_tensor(mrun[:, slot * QB:(slot + 1) * QB],
                                                 pb[a % 4][:, :],
                                                 mrun[:, slot * QB:(slot + 1) * QB],
                                                 OP.max).then_inc(s2.h, 1)
                            a += 1
                s2.v += 8 + ntile
                for k in range(8):
                    bi = ve.reduce_max(permax[:, k:k + 1], mrun[:, k * QB:(k + 1) * QB], axis=AX.X)
                bi.then_inc(sv.h, 1)
                ve.wait_ge(sv.h, bv + 1)
                ve.tensor_scalar(scrb[:, 0:8], permax[:, 0:8], 0.125, None, OP.mult).then_inc(s2.h, 1)
                ve.wait_ge(s1.h, b1 + 8 + ntile + 1)
                ve.reduce_max(negm8[0:8, 0:1], pbh[0:8, 0:128], axis=AX.X, negate=True).then_inc(sv.h, 1)
                ve.wait_ge(sv.h, bv + 2)
                ve.tensor_copy(scrb[0:8, 0:1], negm8[0:8, 0:1]).then_inc(s2.h, 1)
                ve.wait_ge(s1.h, b1 + 8 + ntile + 2)
                ve.tensor_copy(negmrow[0:1, 0:16], pbh[0:1, 128:144]).then_inc(s2.h, 1)
                ve.wait_ge(s1.h, b1 + 8 + ntile + 3)
                ve.tensor_copy(biast[:, 0:8], pb[4][:, 0:8]).then_inc(sv.h, 1)
                ve.wait_ge(sv.h, bv + 3)
                for h01 in range(2):
                    for qb in range(NQB):
                        slot = h01 * 4 + qb
                        ve.tensor_scalar(biastx[:, slot * 64: slot * 64 + UB[qb]],
                                         ktpen_s[:, qb * 64: qb * 64 + UB[qb]],
                                         biast[:, slot:slot + 1], None, OP.add)
                for qb in range(NQB):
                    ve.tensor_scalar(mrun[:, qb * QB:(qb + 1) * QB], colmap_s[:, :],
                                     q0s_s[:, qb:qb + 1], None, OP.add)
                ve.engine_nop().then_inc(s2.h, 1)
                s2.v += 4
                sv.v += 3

        with nc.Block() as blk:
            b1, b2, b3, b4 = s1.v, s2.v, s3.v, s4.v
            tiles = []
            for qb in range(NQB):
                for kt in range(UB[qb]):
                    for h01 in range(2):
                        tiles.append((qb, kt, h01, MB[qb][0] <= kt < MB[qb][1]))
            ntl = len(tiles)
            pe_after_B = [0] * ntl
            pe_after_sum = [0] * ntl
            act_after = [0] * ntl
            pool_after = [0] * ntl
            epi_sum_free = {}
            epi_ctx_free = {}
            epi_out_free = {}
            pe_c = act_c = pool_c = dve_c = 0
            for t, (qb, kt, h01, mskd) in enumerate(tiles):
                pe_c += 1
                pe_after_B[t] = pe_c
                act_c += 1
                act_after[t] = act_c
                if mskd:
                    pool_c += 1
                    pool_after[t] = pool_c
                pe_c += 2
                pe_after_sum[t] = pe_c
                if t + 1 == ntl or tiles[t + 1][0] != qb:
                    epi_sum_free[qb] = dve_c + 1
                    epi_ctx_free[qb] = (dve_c + 4, dve_c + 5)
                    epi_out_free[qb] = dve_c + 6
                    dve_c += 6
                    pe_c += 4

            @blk.tensor
            def _(te):
                pe_n = 0
                for t, (qb, kt, h01, mskd) in enumerate(tiles):
                    slot = h01 * 4 + qb
                    if t >= 2:
                        te.wait_ge(s3.h, b3 + act_after[t - 2])
                    te.matmul(pb[4 + t % 2][:, :],
                              ckv_fm[:, kt * 128:(kt + 1) * 128],
                              qlh[:, h01 * NTOK + qb * QB: h01 * NTOK + (qb + 1) * QB],
                              start=True, stop=True).then_inc(s1.h, 1)
                    pe_n += 1
                    if mskd:
                        te.wait_ge(s4.h, b4 + pool_after[t])
                    else:
                        te.wait_ge(s3.h, b3 + act_after[t])
                    if kt == 0 and qb > 0:
                        te.wait_ge(s2.h, b2 + epi_ctx_free[qb - 1][h01])
                    rhs = ptm[t % 2] if mskd else pt[t % 2]
                    te.matmul(pb[h01][:, :], ckv_tm[:, kt * 128:(kt + 1) * 128], rhs,
                              start=(kt == 0), stop=(kt == UB[qb] - 1)).then_inc(s1.h, 1)
                    if kt == 0 and qb > 0 and h01 == 0:
                        te.wait_ge(s2.h, b2 + epi_sum_free[qb - 1])
                    te.matmul(pb[2][0:1, :] if h01 == 0 else pb[2][32:33, :],
                              oc_b[:, 0:1], rhs,
                              start=(kt == 0), stop=(kt == UB[qb] - 1),
                              tile_position=(0, 32) if h01 == 1 else None).then_inc(s1.h, 1)
                    pe_n += 2
                    if t + 1 == ntl or tiles[t + 1][0] != qb:
                        te.wait_ge(s2.h, b2 + epi_sum_free[qb] + 0)
                        te.matmul(pb[3][:, :], or_f[0:1, :], recipf[0:1, 0:QB],
                                  start=True, stop=True).then_inc(s1.h, 1)
                        te.wait_ge(s2.h, b2 + epi_sum_free[qb] + 1)
                        te.matmul(pb[3][:, :], or_f[0:1, :], recipf[0:1, QB:2 * QB],
                                  start=True, stop=True).then_inc(s1.h, 1)
                        te.wait_ge(s2.h, b2 + epi_ctx_free[qb][0])
                        te.matmul(pb[6][0:64, :], v1_b[:, g0 * 64:(g0 + 1) * 64], ctxe[0],
                                  start=True, stop=True).then_inc(s1.h, 1)
                        te.wait_ge(s2.h, b2 + epi_ctx_free[qb][1])
                        te.matmul(pb[6][64:128, :], v1_b[:, g1 * 64:(g1 + 1) * 64], ctxe[1],
                                  start=True, stop=True,
                                  tile_position=(0, 64)).then_inc(s1.h, 1)
                        pe_n += 4
                s1.v += pe_n

            @blk.scalar
            def _(sc):
                for t, (qb, kt, h01, mskd) in enumerate(tiles):
                    slot = h01 * 4 + qb
                    sc.wait_ge(s1.h, b1 + pe_after_B[t])
                    sc.activation(pt[t % 2], pb[4 + t % 2][:, :], AF.Exp,
                                  bias=biastx[:, slot * 64 + kt: slot * 64 + kt + 1],
                                  scale=0.125).then_inc(s3.h, 1)
                s3.v += ntl

            @blk.vector
            def _(ve):
                for t, (qb, kt, h01, mskd) in enumerate(tiles):
                    if mskd:
                        ve.wait_ge(s3.h, b3 + act_after[t])
                        ve.scalar_tensor_tensor(ptm[t % 2], mrun[:, qb * QB:(qb + 1) * QB],
                                                rowmap_s[:, kt:kt + 1], pt[t % 2],
                                                OP.is_ge, OP.mult).then_inc(s4.h, 1)
                    if t + 1 == ntl or tiles[t + 1][0] != qb:
                        ve.wait_ge(s1.h, b1 + pe_after_sum[t])
                        ve.reciprocal(recipf[0:1, 0:QB], pb[2][0:1, :])
                        ve.reciprocal(recipf[0:1, QB:2 * QB], pb[2][32:33, :]).then_inc(sv.h, 1)
                        ve.wait_ge(sv.h, sv.v + 1)
                        sv.v += 1
                        ve.engine_nop().then_inc(s2.h, 1)
                        mm_base = b1 + pe_after_sum[t]
                        ve.wait_ge(s1.h, mm_base + 1)
                        ve.tensor_copy(rects.tensor if False else rects[0:128, 0:QB], pb[3][:, :]).then_inc(s2.h, 1)
                        ve.wait_ge(s1.h, mm_base + 2)
                        ve.tensor_copy(rects[0:128, QB:2 * QB], pb[3][:, :]).then_inc(s2.h, 1)
                        ve.tensor_tensor(ctxe[0], pb[0][:, :], rects[0:128, 0:QB], OP.mult).then_inc(s2.h, 1)
                        ve.tensor_tensor(ctxe[1], pb[1][:, :], rects[0:128, QB:2 * QB], OP.mult).then_inc(s2.h, 1)
                        ve.wait_ge(s1.h, mm_base + 4)
                        ve.tensor_tensor(xres[:, hp * NTOK + qb * QB: hp * NTOK + (qb + 1) * QB],
                                         pb[6][:, :],
                                         xres[:, hp * NTOK + qb * QB: hp * NTOK + (qb + 1) * QB],
                                         OP.add).then_inc(s2.h, 1)
                s2.v += dve_c
                s4.v += pool_c

    C.pop()
    Dd.pop()
    E.pop()

    if upto == "P5":
        tap_fm("xattn", xres[:, :], dtype=F32)
        emit_store_y()
        return nc, tap_tensors

    A.pop()
    A.push()
    xn2 = A.take(NCH * NTOK)
    emit_ln(xres[:, :], xn2)

    if upto == "P6":
        tap_fm("xn2", xn2, dtype=BF16)
        emit_store_y()
        return nc, tap_tensors

    Dd.push()
    stg2 = Dd.take(2 * 128)
    with nc.Block() as blk:
        b1, b2, bd = s1.v, s2.v, sd.v

        @blk.tensor
        def _(te):
            n = 0
            for i in range(16):
                for c in range(NCH):
                    if n >= 1:
                        te.wait_ge(s2.h, b2 + n)
                    te.transpose(pbh[:, (n % 2) * 128:(n % 2) * 128 + 128],
                                 xn2[:, c * NTOK + i * 128: c * NTOK + (i + 1) * 128],
                                 idb_s[:]).then_inc(s1.h, 1)
                    n += 1
            s1.v += 128

        @blk.vector
        def _(ve):
            n = 0
            for i in range(16):
                for c in range(NCH):
                    ve.wait_ge(s1.h, b1 + n + 1)
                    if n >= 2:
                        ve.wait_ge(sd.h, bd + 16 * (n - 1))
                    ve.tensor_copy(stg2[:, (n % 2) * 128:(n % 2) * 128 + 128],
                                   pbh[:, (n % 2) * 128:(n % 2) * 128 + 128]).then_inc(s2.h, 1)
                    n += 1
            s2.v += 128

        @blk.sync
        def _(sy):
            n = 0
            for i in range(16):
                for c in range(NCH):
                    sy.wait_ge(s2.h, b2 + n + 1)
                    sy.dma_start(out=xm_tm_dram.ap()[i * 128:(i + 1) * 128, c * 128:(c + 1) * 128],
                                 in_=stg2[:, (n % 2) * 128:(n % 2) * 128 + 128]).then_inc(sd.h, 16)
                    n += 1
            sd.v += 128 * 16
    Dd.pop()

    if upto == "P7a":
        emit_store_y()
        return nc, tap_tensors

    Bb.pop()
    Bb.push()
    sh_pre = Bb.take(4 * NTOK)
    sh = Bb.take(4 * NTOK)
    C.off = 0
    C.marks.clear()
    C.push()
    sw1T_b = C.take(NCH * MOE_H)
    sw2T_b = C.take(NCH * MOE_H)
    abb2 = C.take(2 * NTOK)
    sqb = C.take(512)
    t1s = C.take(512)
    emit_wload(sw1T, sw1T_b, NCH, MOE_H)
    emit_wload(sw2T, sw2T_b, NCH, MOE_H)
    sb12_s = nc.alloc_sbuf_tensor("sb12_s", [128, 8], F32)
    slng_s = nc.alloc_sbuf_tensor("slng_s", [128, 8], F32)
    with nc.Block() as blk:
        @blk.sync
        def _(sy):
            for hh in range(4):
                sy.dma_start(out=sb12_s[:, hh:hh + 1], in_=sb1.ap()[hh * 128:(hh + 1) * 128, :]).then_inc(sd.h, 16)
                sy.dma_start(out=sb12_s[:, 4 + hh:5 + hh], in_=sb2.ap()[hh * 128:(hh + 1) * 128, :]).then_inc(sd.h, 16)
                sy.dma_start(out=slng_s[:, hh:hh + 1], in_=s_ln_g.ap()[hh * 128:(hh + 1) * 128, :]).then_inc(sd.h, 16)
                sy.dma_start(out=slng_s[:, 4 + hh:5 + hh], in_=s_ln_b.ap()[hh * 128:(hh + 1) * 128, :]).then_inc(sd.h, 16)
            sd.v += 16 * 16
            sy.wait_ge(sd.h, sd.v)

    with nc.Block() as blk:
        b1, b2, b3 = s1.v, s2.v, s3.v

        @blk.tensor
        def _(te):
            n = 0
            for hh in range(4):
                for qb in range(NQB):
                    if n > 0:
                        te.wait_ge(s2.h, b2 + n)
                    for c in range(NCH):
                        te.matmul(pb[0][:, :], sw1T_b[:, c * MOE_H + hh * 128: c * MOE_H + (hh + 1) * 128],
                                  xn2[:, c * NTOK + qb * QB: c * NTOK + qb * QB + QB],
                                  start=(c == 0), stop=(c == NCH - 1)).then_inc(s1.h, 1)
                    for c in range(NCH):
                        te.matmul(pb[1][:, :], sw2T_b[:, c * MOE_H + hh * 128: c * MOE_H + (hh + 1) * 128],
                                  xn2[:, c * NTOK + qb * QB: c * NTOK + qb * QB + QB],
                                  start=(c == 0), stop=(c == NCH - 1)).then_inc(s1.h, 1)
                    n += 1
            s1.v += 16 * 16

        @blk.scalar
        def _(sc):
            n = 0
            for hh in range(4):
                for qb in range(NQB):
                    sc.wait_ge(s1.h, b1 + 16 * n + 8)
                    if n > 0:
                        sc.wait_ge(s2.h, b2 + n)
                    sc.activation(t1s[:, :], pb[0][:, :], AF.Silu,
                                  bias=sb12_s[:, hh:hh + 1]).then_inc(s3.h, 1)
                    n += 1
            s3.v += 16

        @blk.vector
        def _(ve):
            n = 0
            for hh in range(4):
                for qb in range(NQB):
                    ve.wait_ge(s1.h, b1 + 16 * (n + 1))
                    ve.wait_ge(s3.h, b3 + n + 1)
                    ve.scalar_tensor_tensor(sh_pre[:, hh * NTOK + qb * QB: hh * NTOK + qb * QB + QB],
                                            pb[1][:, :], sb12_s[:, 4 + hh:5 + hh], t1s[:, :],
                                            OP.add, OP.mult).then_inc(s2.h, 1)
                    n += 1
            s2.v += 16

    E.push()
    r0t = E.take(NTOK)
    r1t = E.take(NTOK)
    r2t = E.take(NTOK)
    sqq = E.take(512)
    sqq2 = E.take(512)
    sr0, sr1, sr2 = r0t[0:1, :], r1t[0:1, :], r2t[0:1, :]
    with nc.Block() as blk:
        b1, b2, b3 = s1.v, s2.v, s3.v

        @blk.scalar
        def _(sc):
            for qb in range(NQB):
                for hh in range(4):
                    i = qb * 4 + hh
                    if i > 0:
                        sc.wait_ge(s1.h, b1 + 2 * i)
                    sc.activation(sqb[:, :], sh_pre[:, hh * NTOK + qb * QB: hh * NTOK + qb * QB + QB],
                                  AF.Square).then_inc(s3.h, 1)
            s3.v += 16

        @blk.tensor
        def _(te):
            for qb in range(NQB):
                for hh in range(4):
                    i = qb * 4 + hh
                    if hh == 0 and qb > 0:
                        te.wait_ge(s2.h, b2 + 2 * qb)
                    te.matmul(pb[0][0:1, :], oc_b[:, 0:1],
                              sh_pre[:, hh * NTOK + qb * QB: hh * NTOK + qb * QB + QB],
                              start=(hh == 0), stop=(hh == 3)).then_inc(s1.h, 1)
                    te.wait_ge(s3.h, b3 + i + 1)
                    te.matmul(pb[1][0:1, :], oc_b[:, 0:1], sqb[:, :],
                              start=(hh == 0), stop=(hh == 3)).then_inc(s1.h, 1)
            s1.v += 32

        @blk.vector
        def _(ve):
            for qb in range(NQB):
                ve.wait_ge(s1.h, b1 + 8 * (qb + 1))
                ve.tensor_copy(sr0[0:1, qb * QB:(qb + 1) * QB], pb[0][0:1, :]).then_inc(s2.h, 1)
                ve.tensor_copy(sr1[0:1, qb * QB:(qb + 1) * QB], pb[1][0:1, :]).then_inc(s2.h, 1)
            s2.v += 8

    ddh = float(MOE_H)
    with nc.Block() as blk:
        @blk.vector
        def _(ve):
            ve.tensor_scalar(sr2, sr1, 1.0 / ddh, float(EPS), OP.mult, OP.add)
    with nc.Block() as blk:
        @blk.vector
        def _(ve):
            ve.tensor_scalar(sr1, sr0, 1.0 / ddh, None, OP.mult)
    with nc.Block() as blk:
        @blk.vector
        def _(ve):
            ve.tensor_tensor(sr0, sr1, sr1, OP.mult)
    with nc.Block() as blk:
        @blk.vector
        def _(ve):
            ve.tensor_tensor(sr0, sr2, sr0, OP.subtract)
    with nc.Block() as blk:
        @blk.scalar
        def _(sc):
            sc.activation(sr2, sr0, AF.Sqrt)
    with nc.Block() as blk:
        @blk.vector
        def _(ve):
            ve.reciprocal(sr0, sr2)
    with nc.Block() as blk:
        @blk.vector
        def _(ve):
            ve.tensor_tensor(sr2, sr1, sr0, OP.mult)
    with nc.Block() as blk:
        @blk.vector
        def _(ve):
            ve.tensor_copy(abb2[0:1, 0:NTOK], sr0)
            ve.tensor_copy(abb2[0:1, NTOK:2 * NTOK], sr2)
    with nc.Block() as blk:
        b1, b2 = s1.v, s2.v

        @blk.tensor
        def _(te):
            for qb in range(NQB):
                if qb > 0:
                    te.wait_ge(s2.h, b2 + qb)
                te.matmul(pb[2][:, :], or_b[0:1, :], abb2[0:1, qb * QB:(qb + 1) * QB],
                          start=True, stop=True).then_inc(s1.h, 1)
                te.matmul(pb[3][:, :], or_b[0:1, :], abb2[0:1, NTOK + qb * QB: NTOK + (qb + 1) * QB],
                          start=True, stop=True).then_inc(s1.h, 1)
            s1.v += 8

        @blk.vector
        def _(ve):
            for qb in range(NQB):
                ve.wait_ge(s1.h, b1 + 2 * qb + 2)
                for hh in range(4):
                    sl = slice(hh * NTOK + qb * QB, hh * NTOK + qb * QB + QB)
                    ve.tensor_tensor(sqq[:, :], sh_pre[:, sl], pb[2][:, :], OP.mult)
                    ve.tensor_tensor(sqq2[:, :], sqq[:, :], pb[3][:, :], OP.subtract)
                    bi = ve.tensor_scalar(sh[:, sl], sqq2[:, :], slng_s[:, hh:hh + 1],
                                          slng_s[:, 4 + hh:5 + hh], OP.mult, OP.add)
                    if hh == 3:
                        bi.then_inc(s2.h, 1)
            s2.v += NQB
    E.pop()
    C.pop()

    if upto == "P7g":
        tap_fm("sh", sh, nch=4, width=NTOK, dtype=BF16)
        emit_store_y()
        return nc, tap_tensors

    lidx = Si.take(16)
    itmp = Si.take(16 * 5)
    ib_t = itmp[:, 0:16]
    it_t = itmp[:, 16:32]
    gs_t = itmp[:, 32:48]
    w_t = itmp[:, 48:64]
    e_t = itmp[:, 64:80]
    with nc.Block() as blk:
        @blk.sync
        def _(sy):
            for i in range(16):
                sy.dma_start(out=ib_t[:, i:i + 1], in_=bass.AP(idx_b, i * 128, [[1, 128], [1, 1]])).then_inc(sd.h, 16)
                sy.dma_start(out=it_t[:, i:i + 1], in_=bass.AP(idx_t, i * 128, [[1, 128], [1, 1]])).then_inc(sd.h, 16)
            sd.v += 32 * 16
            sy.wait_ge(sd.h, sd.v)

    def vstep(fn):
        with nc.Block() as blk:
            @blk.vector
            def _(ve):
                fn(ve)

    vstep(lambda ve: ve.tensor_scalar(gs_t, ib_t, 16, None, OP.mult))
    vstep(lambda ve: ve.tensor_scalar(w_t, it_t, 9, None, OP.arith_shift_right))
    vstep(lambda ve: ve.tensor_tensor(e_t, gs_t, w_t, OP.add))
    vstep(lambda ve: ve.tensor_scalar(w_t, w_t, 9, None, OP.arith_shift_left))
    vstep(lambda ve: ve.tensor_tensor(w_t, it_t, w_t, OP.subtract))
    fscr = E.take(16 * 4)
    gsf = fscr[:, 0:16]
    accf = fscr[:, 16:32]
    ef = fscr[:, 32:48]
    sumf = fscr[:, 48:64]
    vstep(lambda ve: ve.tensor_copy(gsf, e_t))
    vstep(lambda ve: ve.tensor_copy(accf, w_t))
    vstep(lambda ve: ve.tensor_scalar(sumf, accf, 0.0, None, OP.mult))
    for j in range(4):
        def fj(ve, j=j):
            ve.tensor_scalar(ef, gsf, metaf_s[:, j:j + 1], None, OP.is_equal)
        vstep(fj)
        def fj2(ve, j=j):
            ve.tensor_tensor(sumf, ef, sumf, OP.add)
        vstep(fj2)
        if j > 0:
            def fj3(ve, j=j):
                ve.tensor_scalar(ef, ef, float(j * 512), None, OP.mult)
            vstep(fj3)
            def fj4(ve, j=j):
                ve.tensor_tensor(accf, ef, accf, OP.add)
            vstep(fj4)
    vstep(lambda ve: ve.tensor_scalar(ef, sumf, -float(1 << 20), float(1 << 20), OP.mult, OP.add))
    vstep(lambda ve: ve.tensor_tensor(accf, ef, accf, OP.add))
    vstep(lambda ve: ve.tensor_copy(lidx, accf))
    lidxc = Si.take(16)
    vstep(lambda ve: ve.tensor_scalar(lidxc, lidx, NTOK, None, OP.min))

    Dd.push()
    gbuf = Dd.take(1024)
    gbuf2 = Dd.take(1024)
    with nc.Block() as blk:
        b2, bg, bd = s2.v, sg.v, sd.v

        @blk.gpsimd
        def _(gp):
            for i in range(16):
                if i > 0:
                    gp.wait_ge(s2.h, b2 + i)
                gp.indirect_dma_start(
                    gbuf[:, :], None,
                    xm_tm_dram.ap(), IndirectOffsetOnAxis(lidxc[:, i:i + 1], 0),
                ).then_inc(sg.h, 16)
                sg.v += 16
                gp.wait_ge(sg.h, sg.v)

        @blk.vector
        def _(ve):
            for i in range(16):
                ve.wait_ge(sg.h, bg + 16 * (i + 1))
                if i > 0:
                    ve.wait_ge(sd.h, bd + 16 * i)
                ve.tensor_scalar(gbuf2[:, :], gbuf[:, :], sumf[:, i:i + 1],
                                 None, OP.mult).then_inc(s2.h, 1)
            s2.v += 16

        @blk.sync
        def _(sy):
            for i in range(16):
                sy.wait_ge(s2.h, b2 + i + 1)
                sy.dma_start(out=xmoe_contrib.ap()[i * 128:(i + 1) * 128, :],
                             in_=gbuf2[:, :]).then_inc(sd.h, 16)
            sd.v += 16 * 16
            sy.wait_ge(sd.h, sd.v)
    Dd.pop()

    groups8 = [[0, 1, 2, 3, 4, 5, 6, 7]]
    with nc.Block() as blk:
        @blk.gpsimd
        def _(gp):
            gp.collective_compute("AllReduce", OP.add, replica_groups=groups8,
                                  ins=[xmoe_contrib.ap()], outs=[xmoe_all.ap()]).then_inc(sco.h, 1)
            sco.v += 1
            gp.wait_ge(sco.h, sco.v)

    if upto == "P7c":
        tap_copy_dram("xmoe", xmoe_all, [MOE_M, D], BF16)
        emit_store_y()
        return nc, tap_tensors

    xm_fm = C.take(NCH * MOE_M)
    Dd.push()
    stg3 = Dd.take(2 * 1024)
    with nc.Block() as blk:
        b1, b2, bd = s1.v, s2.v, sd.v

        @blk.sync
        def _(sy):
            for i in range(16):
                if i >= 2:
                    sy.wait_ge(s1.h, b1 + 8 * (i - 1))
                sy.dma_start(out=stg3[:, (i % 2) * 1024:(i % 2) * 1024 + 1024],
                             in_=xmoe_all.ap()[i * 128:(i + 1) * 128, :]).then_inc(sd.h, 16)
            sd.v += 16 * 16

        @blk.tensor
        def _(te):
            n = 0
            for i in range(16):
                te.wait_ge(sd.h, bd + 16 * (i + 1))
                for c in range(NCH):
                    if n >= 1:
                        te.wait_ge(s2.h, b2 + n)
                    te.transpose(pbh[:, (n % 2) * 128:(n % 2) * 128 + 128],
                                 stg3[:, (i % 2) * 1024 + c * 128: (i % 2) * 1024 + (c + 1) * 128],
                                 idb_s[:]).then_inc(s1.h, 1)
                    n += 1
            s1.v += 128

        @blk.vector
        def _(ve):
            n = 0
            for i in range(16):
                for c in range(NCH):
                    ve.wait_ge(s1.h, b1 + n + 1)
                    ve.tensor_copy(xm_fm[:, c * MOE_M + i * 128: c * MOE_M + (i + 1) * 128],
                                   pbh[:, (n % 2) * 128:(n % 2) * 128 + 128]).then_inc(s2.h, 1)
                    n += 1
            s2.v += 128
    Dd.pop()

    A.pop()
    A.push()
    vb8t = A.take(MOE_M)
    gw_b = A.take(NCH * 16)
    ebrow_b = A.take(2 * MOE_H + D)
    sw3T_b = A.take(4 * D)
    ew3T_b = A.take(4 * D)
    E.push()
    sigf = E.take(MOE_M)
    gmisc = E.take(128)
    swv_t = E.take(16)
    ftile_t = E.take(16)
    dbuf = E.take(1024)
    selg_b = E.take(512)
    selb_b = E.take(512)
    erow_stg = E.take(1024)
    scb = E.take(512)
    sb3_s = nc.alloc_sbuf_tensor("sb3_s", [128, 8], F32)
    gb_s = nc.alloc_sbuf_tensor("gb_s", [16, 1], F32)
    jcols_s = nc.alloc_sbuf_tensor("jcols_s", [128, 16], F32)
    bigc_s = nc.alloc_sbuf_tensor("bigc_s", [128, 1], F32)
    invx = nc.alloc_sbuf_tensor("invx", [128, 16], I32)
    invf = nc.alloc_sbuf_tensor("invf", [128, 16], F32)
    expb_s = nc.alloc_sbuf_tensor("expb_s", [8, 1], F32)

    emit_wload(gwT, gw_b, NCH, 16)
    emit_wload(sw3T, sw3T_b, 4, D)
    emit_wload(ew3T, ew3T_b, 4, D)
    ew1T_b = Bb.t[:, 0:NCH * MOE_H]
    ew2T_b = Bb.t[:, NCH * MOE_H:2 * NCH * MOE_H]
    emit_wload(ew1T, ew1T_b, NCH, MOE_H)
    emit_wload(ew2T, ew2T_b, NCH, MOE_H)

    with nc.Block() as blk:
        bd = sd.v

        @blk.sync
        def _(sy):
            sy.dma_start(out=gb_s[:], in_=gb.ap()).then_inc(sd.h, 16)
            sy.dma_start(out=expb_s[:], in_=exp_bias.ap()).then_inc(sd.h, 16)
            sy.dma_start(out=erow_stg[0:1, 0:512], in_=eb1.ap()).then_inc(sd.h, 16)
            sy.dma_start(out=erow_stg[0:1, 512:1024], in_=eb2.ap()).then_inc(sd.h, 16)
            sy.dma_start(out=dbuf[0:1, 0:1024], in_=eb3.ap()).then_inc(sd.h, 16)
            sy.dma_start(out=selg_b[0:1, 0:512], in_=sel_ln_g.ap()).then_inc(sd.h, 16)
            sy.dma_start(out=selb_b[0:1, 0:512], in_=sel_ln_b.ap()).then_inc(sd.h, 16)
            for oc in range(8):
                sy.dma_start(out=sb3_s[:, oc:oc + 1],
                             in_=sb3.ap()[oc * 128:(oc + 1) * 128, :]).then_inc(sd.h, 16)
            sd.v += 15 * 16
            sy.wait_ge(sd.h, sd.v)

        @blk.vector
        def _(ve):
            ve.wait_ge(sd.h, bd + 15 * 16)
            ve.tensor_copy(ebrow_b[0:1, 0:1024], erow_stg[0:1, 0:1024])
            ve.tensor_copy(ebrow_b[0:1, 1024:2048], dbuf[0:1, 0:1024])

    with nc.Block() as blk:
        b1 = s1.v

        @blk.tensor
        def _(te):
            te.matmul(pb[2][:, :], or_f[0:1, :], selg_b[0:1, 0:512], start=True, stop=True).then_inc(s1.h, 1)
            te.matmul(pb[3][:, :], or_f[0:1, :], selb_b[0:1, 0:512], start=True, stop=True).then_inc(s1.h, 1)
            s1.v += 2

        @blk.vector
        def _(ve):
            ve.wait_ge(s1.h, b1 + 2)
            ve.tensor_copy(selg_b[:, :], pb[2][:, :])
            ve.tensor_copy(selb_b[:, :], pb[3][:, :])

    with nc.Block() as blk:
        b1, b3 = s1.v, s3.v

        @blk.tensor
        def _(te):
            for qb in range(NQB):
                if qb >= 2:
                    te.wait_ge(s3.h, b3 + qb - 1)
                for c in range(NCH):
                    te.matmul(pb[qb % 2][0:16, :], gw_b[:, c * 16:(c + 1) * 16],
                              xm_fm[:, c * MOE_M + qb * QB: c * MOE_M + qb * QB + QB],
                              start=(c == 0), stop=(c == NCH - 1)).then_inc(s1.h, 1)
            s1.v += 32

        @blk.scalar
        def _(sc):
            for qb in range(NQB):
                sc.wait_ge(s1.h, b1 + 8 * (qb + 1))
                sc.activation(sigf[0:16, qb * QB:(qb + 1) * QB], pb[qb % 2][0:16, :],
                              AF.Sigmoid, bias=gb_s[0:16, 0:1]).then_inc(s3.h, 1)
            s3.v += 4

    vf = E.take(MOE_M)
    with nc.Block() as blk:
        @blk.vector
        def _(ve):
            ve.tensor_copy(vf[0:16, :], sigf[0:16, :]).then_inc(sv.h, 1)
            sv.v += 1
            ve.wait_ge(sv.h, sv.v)
            ve.tensor_copy(vf[32:40, :], sigf[0:8, :]).then_inc(sv.h, 1)
            sv.v += 1
            ve.wait_ge(sv.h, sv.v)
            ve.tensor_scalar(vf[0:8, :], sigf[0:8, :], expb_s[0:8, 0:1], None, OP.add)

    gt = gmisc
    with nc.Block() as blk:
        b1, b2, b3 = s1.v, s2.v, s3.v

        @blk.tensor
        def _(te):
            for i in range(16):
                if i >= 1:
                    te.wait_ge(s2.h, b2 + 3 * (i - 1) + 1)
                te.transpose(pb[5][:, 0:128],
                             vf[:, i * 128:(i + 1) * 128], idf_s[:]).then_inc(s1.h, 1)
            s1.v += 16

        @blk.vector
        def _(ve):
            for i in range(16):
                ve.wait_ge(s1.h, b1 + i + 1)
                ve.tensor_copy(gt[:, 0:48], pb[5][:, 0:48]).then_inc(s2.h, 1)
                ve.engine_nop().then_inc(sv.h, 1)
                ve.wait_ge(sv.h, sv.v + 1); sv.v += 1
                ve.reduce_max(gt[:, 48:49], gt[:, 0:8], axis=AX.X).then_inc(sv.h, 1)
                ve.wait_ge(sv.h, sv.v + 1); sv.v += 1
                ve.tensor_scalar(gt[:, 49:57], gt[:, 0:8], gt[:, 48:49], None, OP.is_equal).then_inc(sv.h, 1)
                ve.wait_ge(sv.h, sv.v + 1); sv.v += 1
                ve.tensor_tensor(gt[:, 57:65], gt[:, 49:57], gt[:, 32:40], OP.mult).then_inc(sv.h, 1)
                ve.wait_ge(sv.h, sv.v + 1); sv.v += 1
                ve.reduce_sum(gt[:, 65:66], gt[:, 57:65], axis=AX.X).then_inc(sv.h, 1)
                ve.wait_ge(sv.h, sv.v + 1); sv.v += 1
                ve.tensor_tensor(gt[:, 66:74], gt[:, 49:57], onehot_s[:, :], OP.mult).then_inc(sv.h, 1)
                ve.wait_ge(sv.h, sv.v + 1); sv.v += 1
                ve.reduce_sum(gt[:, 74:75], gt[:, 66:74], axis=AX.X).then_inc(sv.h, 1)
                ve.wait_ge(sv.h, sv.v + 1); sv.v += 1
                ve.tensor_tensor(gt[:, 75:76], gt[:, 8:9], gt[:, 65:66], OP.subtract).then_inc(s2.h, 1)
                ve.wait_ge(s3.h, b3 + 2 * (i + 1))
                ve.tensor_tensor(ftile_t[:, i:i + 1], gt[:, 76:77], gt[:, 74:75], OP.mult).then_inc(s2.h, 1)
            s2.v += 48

        @blk.scalar
        def _(sc):
            for i in range(16):
                sc.wait_ge(s2.h, b2 + 3 * i + 2)
                sc.activation(swv_t[:, i:i + 1], gt[:, 75:76], AF.Sigmoid).then_inc(s3.h, 1)
                sc.activation(gt[:, 76:77], gt[:, 75:76], AF.Sigmoid, scale=-1.0).then_inc(s3.h, 1)
            s3.v += 32

    if upto == "P7f":
        tap_fm("gating", swv_t, nch=1, width=16, dtype=F32)
        tap_fm("gatingf", ftile_t, nch=1, width=16, dtype=F32)
        emit_store_y()
        return nc, tap_tensors

    Dd.push()
    t4 = Dd.take(512)
    z1 = Dd.take(512)
    sqb4 = Dd.take(512)
    z2 = Dd.take(512)
    z3 = Dd.take(512)
    z4 = Dd.take(512)
    zTt = Dd.take(512)
    est = E.take(8)
    for i in range(16):
        with nc.Block() as blk:
            b1, b2, b3, bv = s1.v, s2.v, s3.v, sv.v

            @blk.tensor
            def _(te):
                for c in range(NCH):
                    te.matmul(pb[0][:, :], xm_fm[:, c * MOE_M + i * 128: c * MOE_M + (i + 1) * 128],
                              ew1T_b[:, c * MOE_H:(c + 1) * MOE_H],
                              start=(c == 0), stop=False).then_inc(s1.h, 1)
                te.matmul(pb[0][:, :], or_b[0:1, :], ebrow_b[0:1, 0:512],
                          start=False, stop=True).then_inc(s1.h, 1)
                for c in range(NCH):
                    te.matmul(pb[1][:, :], xm_fm[:, c * MOE_M + i * 128: c * MOE_M + (i + 1) * 128],
                              ew2T_b[:, c * MOE_H:(c + 1) * MOE_H],
                              start=(c == 0), stop=False).then_inc(s1.h, 1)
                te.matmul(pb[1][:, :], or_b[0:1, :], ebrow_b[0:1, 512:1024],
                          start=False, stop=True).then_inc(s1.h, 1)
                s1.v += 18
                for k in range(4):
                    te.wait_ge(s2.h, b2 + 11 + k)
                    te.transpose(pbh[:, (k % 2) * 128:(k % 2) * 128 + 128],
                                 z3[:, k * 128:(k + 1) * 128], idb_s[:]).then_inc(s1.h, 1)
                s1.v += 4
                te.wait_ge(s2.h, b2 + 15)
                for oh in range(2):
                    for k in range(4):
                        te.matmul(pb[2 + oh][:, :], zTt[:, k * 128:(k + 1) * 128],
                                  ew3T_b[:, k * D + oh * 512: k * D + oh * 512 + 512],
                                  start=(k == 0), stop=False).then_inc(s1.h, 1)
                    te.matmul(pb[2 + oh][:, :], or_b[0:1, :],
                              ebrow_b[0:1, 1024 + oh * 512: 1024 + (oh + 1) * 512],
                              start=False, stop=True).then_inc(s1.h, 1)
                s1.v += 10

            @blk.scalar
            def _(sc):
                sc.wait_ge(s1.h, b1 + 9)
                sc.activation(t4[:, :], pb[0][:, :], AF.Silu).then_inc(s3.h, 1)
                sc.wait_ge(s2.h, b2 + 1)
                sc.activation(sqb4[:, :], z1[:, :], AF.Square).then_inc(s3.h, 1)
                sc.wait_ge(s2.h, b2 + 7)
                sc.activation(est[:, 6:7], est[:, 5:6], AF.Sqrt).then_inc(s3.h, 1)
                s3.v += 3

            @blk.vector
            def _(ve):
                ve.wait_ge(s1.h, b1 + 18)
                ve.wait_ge(s3.h, b3 + 1)
                ve.tensor_tensor(z1[:, :], t4[:, :], pb[1][:, :], OP.mult).then_inc(s2.h, 1)
                ve.reduce_sum(est[:, 1:2], z1[:, :], axis=AX.X).then_inc(s2.h, 1)
                ve.wait_ge(s2.h, b2 + 2)
                ve.tensor_scalar(est[:, 2:3], est[:, 1:2], 1.0 / 512.0, None, OP.mult).then_inc(s2.h, 1)
                ve.wait_ge(s3.h, b3 + 2)
                ve.reduce_sum(est[:, 0:1], sqb4[:, :], axis=AX.X).then_inc(s2.h, 1)
                ve.wait_ge(s2.h, b2 + 4)
                ve.tensor_scalar(est[:, 3:4], est[:, 0:1], 1.0 / 512.0, float(EPS), OP.mult, OP.add).then_inc(s2.h, 1)
                ve.wait_ge(s2.h, b2 + 5)
                ve.tensor_tensor(est[:, 4:5], est[:, 2:3], est[:, 2:3], OP.mult).then_inc(s2.h, 1)
                ve.wait_ge(s2.h, b2 + 6)
                ve.tensor_tensor(est[:, 5:6], est[:, 3:4], est[:, 4:5], OP.subtract).then_inc(s2.h, 1)
                ve.wait_ge(s3.h, b3 + 3)
                ve.reciprocal(est[:, 7:8], est[:, 6:7]).then_inc(s2.h, 1)
                ve.wait_ge(s2.h, b2 + 8)
                ve.tensor_scalar(z2[:, :], z1[:, :], est[:, 2:3], est[:, 7:8],
                                 OP.subtract, OP.mult).then_inc(s2.h, 1)
                ve.wait_ge(s2.h, b2 + 9)
                ve.tensor_tensor(z4[:, :], z2[:, :], selg_b[:, :], OP.mult).then_inc(s2.h, 1)
                ve.wait_ge(s2.h, b2 + 10)
                ve.tensor_tensor(z3[:, :], z4[:, :], selb_b[:, :], OP.add).then_inc(s2.h, 1)
                for k in range(4):
                    ve.wait_ge(s1.h, b1 + 18 + k + 1)
                    ve.tensor_copy(zTt[:, k * 128:(k + 1) * 128],
                                   pbh[:, (k % 2) * 128:(k % 2) * 128 + 128]).then_inc(s2.h, 1)
                for oh in range(2):
                    ve.wait_ge(s1.h, b1 + 22 + 5 * (oh + 1))
                    ve.tensor_scalar(dbuf[:, oh * 512:(oh + 1) * 512], pb[2 + oh][:, :],
                                     ftile_t[:, i:i + 1], None, OP.mult).then_inc(s2.h, 1)
                s2.v += 17

            @blk.sync
            def _(sy):
                sy.wait_ge(s2.h, b2 + 17)
                sy.dma_start(out=delta_contrib.ap()[i * 128:(i + 1) * 128, :],
                             in_=dbuf[:, :]).then_inc(sd.h, 16)
                sd.v += 16
                sy.wait_ge(sd.h, sd.v)
    Dd.pop()

    with nc.Block() as blk:
        @blk.gpsimd
        def _(gp):
            gp.collective_compute("AllReduce", OP.add, replica_groups=groups8,
                                  ins=[delta_contrib.ap()], outs=[delta_all.ap()[0:MOE_M, :]]).then_inc(sco.h, 1)
            sco.v += 1
            gp.wait_ge(sco.h, sco.v)

    if upto == "P7h":
        tap_copy_dram("delta", delta_all, [MOE_M, D], F32)
        emit_store_y()
        return nc, tap_tensors

    with nc.Block() as blk:
        @blk.vector
        def _(ve):
            ve.memset(dbuf[:, :], 0.0)

        @blk.sync
        def _(sy):
            for k in range(16):
                sy.dma_start(out=own_scale.ap()[k * 128:(k + 1) * 128, :],
                             in_=oc_f[:, 0:1]).then_inc(sd.h, 16)
            sd.v += 16 * 16
            sy.wait_ge(sd.h, sd.v)
    with nc.Block() as blk:
        @blk.sync
        def _(sy):
            sy.dma_start(out=jcols_s[:], in_=jcols0.ap()).then_inc(sd.h, 16)
            sy.dma_start(out=bigc_s[:], in_=bigcol0.ap()).then_inc(sd.h, 16)
            sd.v += 2 * 16
            sy.wait_ge(sd.h, sd.v)
            for k in range(17):
                sy.dma_start(out=own_inv.ap()[k * 128:(k + 1) * 128, :],
                             in_=bigc_s[:, :]).then_inc(sd.h, 16)
            sd.v += 17 * 16
            sy.wait_ge(sd.h, sd.v)

    with nc.Block() as blk:
        @blk.gpsimd
        def _(gp):
            for i in range(16):
                gp.indirect_dma_start(
                    own_scale.ap(), IndirectOffsetOnAxis(lidxc[:, i:i + 1], 0),
                    swv_t[:, i:i + 1], None,
                ).then_inc(sg.h, 16)
                sg.v += 16
                gp.wait_ge(sg.h, sg.v)

    jvals_e = fscr[:, 0:16]
    with nc.Block() as blk:
        @blk.vector
        def _(ve):
            ve.tensor_copy(jvals_e, jcols_s[:, :])

    with nc.Block() as blk:
        @blk.gpsimd
        def _(gp):
            for i in range(16):
                gp.indirect_dma_start(
                    own_inv.ap(), IndirectOffsetOnAxis(lidxc[:, i:i + 1], 0),
                    jvals_e[:, i:i + 1], None,
                ).then_inc(sg.h, 16)
                sg.v += 16
                gp.wait_ge(sg.h, sg.v)

    with nc.Block() as blk:
        @blk.sync
        def _(sy):
            for i in range(16):
                sy.dma_start(out=invf[:, i:i + 1],
                             in_=own_inv.ap()[i * 128:(i + 1) * 128, :]).then_inc(sd.h, 16)
            sd.v += 16 * 16
            sy.wait_ge(sd.h, sd.v)



    dmask = fscr[:, 32:48]
    dbuf2v = vf[:, 0:1024]
    with nc.Block() as blk:
        @blk.vector
        def _(ve):
            ve.tensor_scalar(invx[:, :], invf[:, :], float(MOE_M), None, OP.min).then_inc(sv.h, 1)
            sv.v += 1
            ve.wait_ge(sv.h, sv.v)
            ve.tensor_scalar(dmask[:, :], invf[:, :], float(MOE_M) - 0.5, None, OP.is_lt)

    with nc.Block() as blk:
        b2g, bgg, bdg = s2.v, sg.v, sd.v

        @blk.gpsimd
        def _(gp):
            for i in range(16):
                if i > 0:
                    gp.wait_ge(s2.h, b2g + i)
                gp.indirect_dma_start(
                    dbuf[:, :], None,
                    delta_all.ap(), IndirectOffsetOnAxis(invx[:, i:i + 1], 0),
                ).then_inc(sg.h, 16)
                sg.v += 16
                gp.wait_ge(sg.h, sg.v)

        @blk.vector
        def _(ve):
            for i in range(16):
                ve.wait_ge(sg.h, bgg + 16 * (i + 1))
                if i > 0:
                    ve.wait_ge(sd.h, bdg + 16 * i)
                ve.tensor_scalar(dbuf2v[:, :], dbuf[:, :], dmask[:, i:i + 1],
                                 None, OP.mult).then_inc(s2.h, 1)
            s2.v += 16

        @blk.sync
        def _(sy):
            for i in range(16):
                sy.wait_ge(s2.h, b2g + i + 1)
                sy.dma_start(out=own_delta.ap()[i * 128:(i + 1) * 128, :],
                             in_=dbuf2v[:, :]).then_inc(sd.h, 16)
            sd.v += 16 * 16
            sy.wait_ge(sd.h, sd.v)

    with nc.Block() as blk:
        @blk.sync
        def _(sy):
            sy.dma_start(out=sigf[0:1, 0:NTOK],
                         in_=bass.AP(own_scale, 0, [[NTOK, 1], [1, NTOK]])).then_inc(sd.h, 16)
            sd.v += 16
            sy.wait_ge(sd.h, sd.v)

    if upto == "P7i2":
        tap_fm("gating", swv_t, nch=1, width=16, dtype=F32)
        tap_fm("gatingf", ftile_t, nch=1, width=16, dtype=F32)
        tap_fm("sigf", sigf, nch=1, width=MOE_M, dtype=F32)
        tap_fm("gwb", gw_b, nch=1, width=NCH * 16, dtype=BF16)
        tap_fm("vf", vf, nch=1, width=MOE_M, dtype=F32)
        tap_fm("gmisc", gmisc, nch=1, width=128, dtype=F32)
        tap_copy_dram("oscale", own_scale, [NTOK + 128, 1], F32)
        tap_copy_dram("odelta", own_delta, [NTOK, D], F32)
        tap_copy_dram("dall", delta_all, [MOE_M + 128, D], F32)
        tap_copy_dram("oinv", own_inv, [NTOK + 128, 1], F32)
        tap_copy_dram("dcon", delta_contrib, [MOE_M, D], F32)
        emit_store_y()
        return nc, tap_tensors

    with nc.Block() as blk:
        b1, b2, b3 = s1.v, s2.v, s3.v

        @blk.tensor
        def _(te):
            n = 0
            for qb in range(NQB):
                if qb > 0:
                    te.wait_ge(s2.h, b2 + 9 * qb)
                te.matmul(pb[4][:, :], or_f[0:1, :], sigf[0:1, qb * QB:(qb + 1) * QB],
                          start=True, stop=True).then_inc(s1.h, 1)
                s1.v += 1
                for oc in range(NCH):
                    if oc >= 2:
                        te.wait_ge(s2.h, b2 + 9 * qb + oc)
                    for hh in range(4):
                        te.matmul(pb[oc % 2][:, :],
                                  sw3T_b[:, hh * D + oc * 128: hh * D + (oc + 1) * 128],
                                  sh[:, hh * NTOK + qb * QB: hh * NTOK + qb * QB + QB],
                                  start=(hh == 0), stop=(hh == 3)).then_inc(s1.h, 1)
                    s1.v += 4

        @blk.vector
        def _(ve):
            n1 = b1
            for qb in range(NQB):
                ve.wait_ge(s1.h, n1 + 1)
                ve.tensor_copy(scb[:, :], pb[4][:, :]).then_inc(s2.h, 1)
                n1 += 1
                for oc in range(NCH):
                    ve.wait_ge(s1.h, n1 + 4)
                    ve.scalar_tensor_tensor(dbuf[:, 0:512], pb[oc % 2][:, :],
                                            sb3_s[:, oc:oc + 1], scb[:, :],
                                            OP.add, OP.mult).then_inc(sv.h, 1)
                    sv.v += 1
                    ve.wait_ge(sv.h, sv.v)
                    ve.tensor_tensor(xres[:, oc * NTOK + qb * QB: oc * NTOK + qb * QB + QB],
                                     dbuf[:, 0:512],
                                     xres[:, oc * NTOK + qb * QB: oc * NTOK + qb * QB + QB],
                                     OP.add).then_inc(s2.h, 1)
                    n1 += 4
            s2.v += NQB * 9

    if upto == "P7i3":
        tap_fm("xcomb", xres[:, :], dtype=F32)
        tap_copy_dram("oscale", own_scale, [NTOK, 1], F32)
        emit_store_y()
        return nc, tap_tensors

    with nc.Block() as blk:
        b1, b2, bd = s1.v, s2.v, sd.v

        @blk.sync
        def _(sy):
            n = 0
            for qb in range(NQB):
                for oc in range(NCH):
                    for k in range(4):
                        if n >= 2:
                            sy.wait_ge(s1.h, b1 + n - 1)
                        sy.dma_start(out=dbuf[:, (n % 2) * 512:(n % 2) * 512 + 128],
                                     in_=own_delta.ap()[qb * QB + k * 128: qb * QB + (k + 1) * 128,
                                                        oc * 128:(oc + 1) * 128]).then_inc(sd.h, 16)
                        n += 1
            sd.v += 128 * 16

        @blk.tensor
        def _(te):
            n = 0
            for qb in range(NQB):
                for oc in range(NCH):
                    for k in range(4):
                        te.wait_ge(sd.h, bd + 16 * (n + 1))
                        if n >= 1:
                            te.wait_ge(s2.h, b2 + n)
                        te.transpose(pb[5][:, 0:128],
                                     dbuf[:, (n % 2) * 512:(n % 2) * 512 + 128], idf_s[:]).then_inc(s1.h, 1)
                        n += 1
            s1.v += 128

        @blk.vector
        def _(ve):
            n = 0
            for qb in range(NQB):
                for oc in range(NCH):
                    for k in range(4):
                        ve.wait_ge(s1.h, b1 + n + 1)
                        cols = slice(oc * NTOK + qb * QB + k * 128, oc * NTOK + qb * QB + (k + 1) * 128)
                        ve.tensor_tensor(xres[:, cols], pb[5][:, 0:128],
                                         xres[:, cols], OP.add).then_inc(s2.h, 1)
                        n += 1
            s2.v += 128
    E.pop()
    Dd.pop() if False else None
    A.pop()
    Bb.pop()
    C.pop() if False else None
    C.off = 0
    C.marks.clear()

    if upto == "P7":
        tap_fm("xffn", xres[:, :], dtype=F32)
        tap_copy_dram("odelta", own_delta, [NTOK, D], F32)
        tap_copy_dram("oscale", own_scale, [NTOK + 128, 1], F32)
        emit_store_y()
        return nc, tap_tensors

    E.off = 0
    E.marks.clear()

    A.push()
    xn3 = A.take(NCH * NTOK)
    emit_ln(xres[:, :], xn3)

    C.off = 0
    C.marks.clear()
    wqkv_b = C.take(NCH * D)
    emit_wload(w_qkvT, wqkv_b, NCH, D)
    Bb.push()
    qkv_own = Bb.take(NCH * NTOK)
    with nc.Block() as blk:
        b1, b2 = s1.v, s2.v

        @blk.tensor
        def _(te):
            n = 0
            for oc in range(NCH):
                for qb in range(NQB):
                    if n >= 2:
                        te.wait_ge(s2.h, b2 + n - 1)
                    for c in range(NCH):
                        te.matmul(pb[n % 2][:, :], wqkv_b[:, c * D + oc * 128: c * D + (oc + 1) * 128],
                                  xn3[:, c * NTOK + qb * QB: c * NTOK + qb * QB + QB],
                                  start=(c == 0), stop=(c == NCH - 1)).then_inc(s1.h, 1)
                    n += 1
            s1.v += 32 * NCH

        @blk.vector
        def _(ve):
            n = 0
            for oc in range(NCH):
                for qb in range(NQB):
                    ve.wait_ge(s1.h, b1 + NCH * (n + 1))
                    ve.tensor_copy(qkv_own[:, oc * NTOK + qb * QB: oc * NTOK + qb * QB + QB],
                                   pb[n % 2][:, :]).then_inc(s2.h, 1)
                    n += 1
            s2.v += 32

    Dd.push()
    stg4 = Dd.take(2 * 128)
    with nc.Block() as blk:
        b1, b2, bd = s1.v, s2.v, sd.v

        @blk.tensor
        def _(te):
            n = 0
            for i in range(16):
                for c in range(NCH):
                    if n >= 1:
                        te.wait_ge(s2.h, b2 + n)
                    te.transpose(pbh[:, (n % 2) * 128:(n % 2) * 128 + 128],
                                 qkv_own[:, c * NTOK + i * 128: c * NTOK + (i + 1) * 128],
                                 idb_s[:]).then_inc(s1.h, 1)
                    n += 1
            s1.v += 128

        @blk.vector
        def _(ve):
            n = 0
            for i in range(16):
                for c in range(NCH):
                    ve.wait_ge(s1.h, b1 + n + 1)
                    if n >= 2:
                        ve.wait_ge(sd.h, bd + 16 * (n - 1))
                    ve.tensor_copy(stg4[:, (n % 2) * 128:(n % 2) * 128 + 128],
                                   pbh[:, (n % 2) * 128:(n % 2) * 128 + 128]).then_inc(s2.h, 1)
                    n += 1
            s2.v += 128

        @blk.sync
        def _(sy):
            n = 0
            for i in range(16):
                for c in range(NCH):
                    sy.wait_ge(s2.h, b2 + n + 1)
                    sy.dma_start(out=qkv_contrib.ap()[NTOK + i * 128: NTOK + (i + 1) * 128,
                                                      c * 128:(c + 1) * 128],
                                 in_=stg4[:, (n % 2) * 128:(n % 2) * 128 + 128]).then_inc(sd.h, 16)
                    n += 1
            sd.v += 128 * 16
    Dd.pop()
    with nc.Block() as blk:
        @blk.sync
        def _(sy):
            for c in range(NCH):
                fmv = bass.AP(qkv_contrib, c * 128 * NTOK, [[NTOK, 128], [1, NTOK]])
                sy.dma_start(out=fmv, in_=qkv_own[:, c * NTOK:(c + 1) * NTOK]).then_inc(sd.h, 16)
            sd.v += NCH * 16
            sy.wait_ge(sd.h, sd.v)

    with nc.Block() as blk:
        @blk.gpsimd
        def _(gp):
            gp.collective_compute("AllGather", OP.bypass, replica_groups=groups4,
                                  ins=[qkv_contrib.ap()], outs=[qkv_ag.ap()]).then_inc(sco.h, 1)
            sco.v += 1
            gp.wait_ge(sco.h, sco.v)

    if upto == "P8a":
        tap_copy_dram("qkvag", qkv_ag, [4 * 2 * NTOK, D], BF16)
        emit_store_y()
        return nc, tap_tensors

    A.pop()
    A.push()
    ctxbuf = A.take(NCH * NTOK)
    C.off = 0
    C.marks.clear()
    kb = C.take(8192)
    vb = C.take(8192)
    Dd.push()
    ptl = [Dd.take(QB), Dd.take(QB)]
    E.push()
    mrun2 = E.take(8 * QB)
    permax2 = E.take(8)
    negm82 = E.take(8)
    biast2 = E.take(8)
    recipf2 = E.take(1024)
    rects2 = E.take(512)
    scrb2 = Dd.take(128)
    negmrow2 = Dd.take(16)

    for hp in range(NH // 2):
        with nc.Block() as blk:
            @blk.sync
            def _(sy):
                for r in range(4):
                    fmv = bass.AP(qkv_ag, (r * 2 * NTOK) * D + hp * 128 * NTOK,
                                  [[NTOK, 128], [1, NTOK]])
                    sy.dma_start(out=kb[:, r * NTOK:(r + 1) * NTOK], in_=fmv).then_inc(sd.h, 16)
                    tmv = bass.AP(qkv_ag, (r * 2 * NTOK + NTOK) * D + hp * 128,
                                  [[D, 128], [128 * D, 16], [1, 128]])
                    dstv = bass.AP(vb.tensor, vb.offset + r * 16 * 128,
                                   [[vb.tensor.shape[1], 128], [128, 16], [1, 128]]) if False else None
                    sy.dma_start(out=bass.AP(kb.tensor, kb.offset, [[1, 1], [1, 1]]), in_=bass.AP(qkv_ag, 0, [[1, 1], [1, 1]])) if False else None
                    n0 = r * 16
                    for kt in range(16):
                        sy.dma_start(out=vb[:, (n0 + kt) * 128:(n0 + kt + 1) * 128],
                                     in_=qkv_ag.ap()[r * 2 * NTOK + NTOK + kt * 128:
                                                     r * 2 * NTOK + NTOK + (kt + 1) * 128,
                                                     hp * 128:(hp + 1) * 128]).then_inc(sd.h, 16)
                sd.v += 4 * 17 * 16
                sy.wait_ge(sd.h, sd.v)

        with nc.Block() as blk:
            b1, b2, bv = s1.v, s2.v, sv.v
            ntile = 2 * 4 * 64

            @blk.tensor
            def _(te):
                a = 0
                for qb in range(NQB):
                    for kt in range(64):
                        for h01 in range(2):
                            if a >= 4:
                                te.wait_ge(s2.h, b2 + a - 3)
                            te.matmul(pb[a % 4][:, :],
                                      kb[h01 * 64:(h01 + 1) * 64, kt * 128:(kt + 1) * 128],
                                      qkv_own[h01 * 64:(h01 + 1) * 64,
                                              hp * NTOK + qb * QB: hp * NTOK + (qb + 1) * QB],
                                      start=True, stop=True).then_inc(s1.h, 1)
                            a += 1
                s1.v += ntile
                te.wait_ge(s2.h, b2 + ntile + 1)
                te.transpose(pbh[:, 0:128], scrb2[:, 0:128], idb_s[:]).then_inc(s1.h, 1)
                te.wait_ge(s2.h, b2 + ntile + 2)
                te.transpose(pbh[:, 128:256], scrb2[:, 0:128], idb_s[:]).then_inc(s1.h, 1)
                te.wait_ge(s2.h, b2 + ntile + 3)
                te.matmul(pb[4][:, 0:16], or_b[0:1, :], negmrow2[0:1, 0:16],
                          start=True, stop=True).then_inc(s1.h, 1)
                s1.v += 3

            @blk.vector
            def _(ve):
                a = 0
                for qb in range(NQB):
                    for kt in range(64):
                        for h01 in range(2):
                            slot = h01 * 4 + qb
                            ve.wait_ge(s1.h, b1 + a + 1)
                            if kt == 0:
                                ve.tensor_copy(mrun2[:, slot * QB:(slot + 1) * QB],
                                               pb[a % 4][:, :]).then_inc(s2.h, 1)
                            else:
                                ve.tensor_tensor(mrun2[:, slot * QB:(slot + 1) * QB],
                                                 pb[a % 4][:, :],
                                                 mrun2[:, slot * QB:(slot + 1) * QB],
                                                 OP.max).then_inc(s2.h, 1)
                            a += 1
                s2.v += ntile
                for k in range(8):
                    bi = ve.reduce_max(permax2[:, k:k + 1], mrun2[:, k * QB:(k + 1) * QB], axis=AX.X)
                bi.then_inc(sv.h, 1)
                ve.wait_ge(sv.h, bv + 1)
                ve.tensor_scalar(scrb2[:, 0:8], permax2[:, 0:8], 0.125, None, OP.mult).then_inc(s2.h, 1)
                ve.wait_ge(s1.h, b1 + ntile + 1)
                ve.reduce_max(negm82[0:8, 0:1], pbh[0:8, 0:128], axis=AX.X, negate=True).then_inc(sv.h, 1)
                ve.wait_ge(sv.h, bv + 2)
                ve.tensor_copy(scrb2[0:8, 0:1], negm82[0:8, 0:1]).then_inc(s2.h, 1)
                ve.wait_ge(s1.h, b1 + ntile + 2)
                ve.tensor_copy(negmrow2[0:1, 0:16], pbh[0:1, 128:144]).then_inc(s2.h, 1)
                ve.wait_ge(s1.h, b1 + ntile + 3)
                ve.tensor_copy(biast2[:, 0:8], pb[4][:, 0:8])
                ve.engine_nop().then_inc(s2.h, 1)
                s2.v += 4
                sv.v += 2

        with nc.Block() as blk:
            b1, b2, b3 = s1.v, s2.v, s3.v
            tiles = []
            for qb in range(NQB):
                for kt in range(64):
                    for h01 in range(2):
                        tiles.append((qb, kt, h01))
            ntl = len(tiles)
            pe_after_B = [0] * ntl
            pe_after_sum = [0] * ntl
            act_after = [0] * ntl
            epi_sum_free = {}
            epi_ctx_free = {}
            pe_c = act_c = dve_c = 0
            for t, (qb, kt, h01) in enumerate(tiles):
                pe_c += 1
                pe_after_B[t] = pe_c
                act_c += 1
                act_after[t] = act_c
                pe_c += 2
                pe_after_sum[t] = pe_c
                if t + 1 == ntl or tiles[t + 1][0] != qb:
                    epi_sum_free[qb] = dve_c + 1
                    epi_ctx_free[qb] = dve_c + 3
                    dve_c += 3
                    pe_c += 2

            @blk.tensor
            def _(te):
                for t, (qb, kt, h01) in enumerate(tiles):
                    if t >= 2:
                        te.wait_ge(s3.h, b3 + act_after[t - 2])
                    te.matmul(pb[4 + t % 2][:, :],
                              kb[h01 * 64:(h01 + 1) * 64, kt * 128:(kt + 1) * 128],
                              qkv_own[h01 * 64:(h01 + 1) * 64,
                                      hp * NTOK + qb * QB: hp * NTOK + (qb + 1) * QB],
                              start=True, stop=True).then_inc(s1.h, 1)
                    te.wait_ge(s3.h, b3 + act_after[t])
                    if kt == 0 and qb > 0:
                        te.wait_ge(s2.h, b2 + epi_ctx_free[qb - 1])
                    te.matmul(pb[0][h01 * 64:(h01 + 1) * 64, :],
                              vb[:, kt * 128 + h01 * 64: kt * 128 + h01 * 64 + 64],
                              ptl[t % 2],
                              start=(kt == 0), stop=(kt == 63),
                              tile_position=(0, 64) if h01 == 1 else None).then_inc(s1.h, 1)
                    if kt == 0 and qb > 0 and h01 == 0:
                        te.wait_ge(s2.h, b2 + epi_sum_free[qb - 1])
                    te.matmul(pb[2][0:1, :] if h01 == 0 else pb[2][32:33, :],
                              oc_b[:, 0:1], ptl[t % 2],
                              start=(kt == 0), stop=(kt == 63),
                              tile_position=(0, 32) if h01 == 1 else None).then_inc(s1.h, 1)
                    if t + 1 == ntl or tiles[t + 1][0] != qb:
                        te.wait_ge(s2.h, b2 + epi_sum_free[qb])
                        te.matmul(pb[3][0:64, :], or_f[0:1, 0:64], recipf2[0:1, 0:QB],
                                  start=True, stop=True).then_inc(s1.h, 1)
                        te.matmul(pb[3][64:128, :], or_f[0:1, 0:64], recipf2[0:1, QB:2 * QB],
                                  start=True, stop=True,
                                  tile_position=(0, 64)).then_inc(s1.h, 1)
                s1.v += pe_c

            @blk.scalar
            def _(sc):
                for t, (qb, kt, h01) in enumerate(tiles):
                    slot = h01 * 4 + qb
                    sc.wait_ge(s1.h, b1 + pe_after_B[t])
                    sc.activation(ptl[t % 2], pb[4 + t % 2][:, :], AF.Exp,
                                  bias=biast2[:, slot:slot + 1], scale=0.125).then_inc(s3.h, 1)
                s3.v += ntl

            @blk.vector
            def _(ve):
                for t, (qb, kt, h01) in enumerate(tiles):
                    if t + 1 == ntl or tiles[t + 1][0] != qb:
                        ve.wait_ge(s1.h, b1 + pe_after_sum[t])
                        ve.reciprocal(recipf2[0:1, 0:QB], pb[2][0:1, :])
                        ve.reciprocal(recipf2[0:1, QB:2 * QB], pb[2][32:33, :]).then_inc(sv.h, 1)
                        ve.wait_ge(sv.h, sv.v + 1)
                        sv.v += 1
                        ve.engine_nop().then_inc(s2.h, 1)
                        mm_base = b1 + pe_after_sum[t]
                        ve.wait_ge(s1.h, mm_base + 2)
                        ve.tensor_copy(rects2[0:128, 0:QB], pb[3][:, :]).then_inc(s2.h, 1)
                        ve.tensor_tensor(ctxbuf[:, hp * NTOK + qb * QB: hp * NTOK + (qb + 1) * QB],
                                         pb[0][:, :], rects2[0:128, 0:QB],
                                         OP.mult).then_inc(s2.h, 1)
                s2.v += dve_c
    Dd.pop()
    E.pop()

    if upto == "P8b":
        tap_fm("ctxb", ctxbuf, dtype=BF16)
        emit_store_y()
        return nc, tap_tensors

    C.off = 0
    C.marks.clear()
    wo_b = C.take(NCH * D)
    emit_wload(w_l1oT, wo_b, NCH, D)
    with nc.Block() as blk:
        b1, b2 = s1.v, s2.v

        @blk.tensor
        def _(te):
            n = 0
            for oc in range(NCH):
                for qb in range(NQB):
                    if n >= 2:
                        te.wait_ge(s2.h, b2 + n - 1)
                    for c in range(NCH):
                        te.matmul(pb[n % 2][:, :], wo_b[:, c * D + oc * 128: c * D + (oc + 1) * 128],
                                  ctxbuf[:, c * NTOK + qb * QB: c * NTOK + qb * QB + QB],
                                  start=(c == 0), stop=(c == NCH - 1)).then_inc(s1.h, 1)
                    n += 1
            s1.v += 32 * NCH

        @blk.vector
        def _(ve):
            n = 0
            for oc in range(NCH):
                for qb in range(NQB):
                    ve.wait_ge(s1.h, b1 + NCH * (n + 1))
                    sl = slice(oc * NTOK + qb * QB, oc * NTOK + qb * QB + QB)
                    ve.tensor_tensor(xres[:, sl], pb[n % 2][:, :], xres[:, sl],
                                     OP.add).then_inc(s2.h, 1)
                    n += 1
            s2.v += 32

    if upto == "P8":
        tap_fm("xl1a", xres[:, :], dtype=F32)
        emit_store_y()
        return nc, tap_tensors

    E.push()
    sstg = E.take(D)
    fstg = E.take(D)
    f2stg = E.take(D)
    Dd.push()
    wstg = Dd.take(D)
    with nc.Block() as blk:
        bd, bg2, b4 = sd.v, sg.v, s4.v

        @blk.sync
        def _(sy):
            n = 0
            for i in range(6):
                for c in range(NCH):
                    if n >= 1:
                        sy.wait_ge(s4.h, b4 + 2 * n)
                    sy.dma_start(out=sstg[:, :], in_=scT.ap()[i, c * 128:(c + 1) * 128, :]).then_inc(sd.h, 16)
                    n += 1
            sd.v += 48 * 16

        @blk.gpsimd
        def _(gp):
            n = 0
            for i in range(6):
                for c in range(NCH):
                    if n >= 1:
                        gp.wait_ge(s4.h, b4 + 2 * n - 1)
                    gp.dma_start(out=fstg[:, :], in_=qwT.ap()[i, c * 128:(c + 1) * 128, :]).then_inc(sg.h, 16)
                    gp.wait_ge(sg.h, bg2 + 16 * (2 * n + 1))
                    gp.wait_ge(sd.h, bd + 16 * (n + 1))
                    gp.tensor_scalar(f2stg[:, :], fstg[:, :], 1.0 / 7.5, -1.0,
                                     OP.mult, OP.add).then_inc(s4.h, 1)
                    gp.tensor_tensor(wstg[:, :], f2stg[:, :], sstg[:, :], OP.mult).then_inc(s4.h, 1)
                    gp.dma_start(out=wq_dram.ap()[i, c * 128:(c + 1) * 128, :],
                                 in_=wstg[:, :]).then_inc(sg.h, 16)
                    gp.wait_ge(sg.h, bg2 + 16 * (2 * n + 2))
                    n += 1
            s4.v += 96
            sg.v += 96 * 16
    Dd.pop()
    E.pop()

    if upto == "P9a":
        tap_copy_dram("wq", wq_dram, [6, D, D], BF16)
        emit_store_y()
        return nc, tap_tensors

    l1b_s = nc.alloc_sbuf_tensor("l1b_s", [128, 48], F32)
    lng_s = nc.alloc_sbuf_tensor("lng_s", [128, 24], F32)
    lnb_s = nc.alloc_sbuf_tensor("lnb_s", [128, 24], F32)
    with nc.Block() as blk:
        @blk.sync
        def _(sy):
            for i in range(6):
                for oc in range(NCH):
                    sy.dma_start(out=l1b_s[:, i * 8 + oc: i * 8 + oc + 1],
                                 in_=bass.AP(l1_biases, i * D + oc * 128, [[1, 128], [1, 1]])).then_inc(sd.h, 16)
            for blki in range(3):
                for oc in range(NCH):
                    sy.dma_start(out=lng_s[:, blki * 8 + oc: blki * 8 + oc + 1],
                                 in_=bass.AP(ln_g, blki * D + oc * 128, [[1, 128], [1, 1]])).then_inc(sd.h, 16)
                    sy.dma_start(out=lnb_s[:, blki * 8 + oc: blki * 8 + oc + 1],
                                 in_=bass.AP(ln_b, blki * D + oc * 128, [[1, 128], [1, 1]])).then_inc(sd.h, 16)
            sd.v += (48 + 48) * 16
            sy.wait_ge(sd.h, sd.v)

    A.pop()
    A.push()
    hb_t = A.take(4096)
    ybf_t = A.take(4096)
    g_t = A.take(4096)
    sqs_t = A.take(512)
    abb3 = A.take(1024)
    Dd.push()
    t1b_t = Dd.take(512)
    E.push()
    h_t = E.take(4096)
    lr0 = E.take(512)
    lr1 = E.take(512)
    lr2 = E.take(512)
    tb_t = A.take(512)

    C.off = 0
    C.marks.clear()
    wt_b = C.take(NCH * D)
    lab_b = C.take(NCH * 32)
    lbb_b = C.take(D)

    def l1_linear(i, rhs_t, out_cb):
        emit_wload(wq_dram.ap()[i], wt_b, NCH, D)
        emit_wload(laT.ap()[i], lab_b, NCH, 32)
        emit_wload(lbT.ap()[i], lbb_b, 1, D, rows=32)
        with nc.Block() as blk:
            b1, b2 = s1.v, s2.v

            @blk.tensor
            def _(te):
                for c in range(NCH):
                    te.matmul(pb[5][0:32, :], lab_b[:, c * 32:(c + 1) * 32],
                              rhs_t[:, c * 512:(c + 1) * 512],
                              start=(c == 0), stop=(c == NCH - 1)).then_inc(s1.h, 1)
                s1.v += NCH
                for oc in range(NCH):
                    if oc >= 2:
                        te.wait_ge(s2.h, b2 + oc)
                    for c in range(NCH):
                        te.matmul(pb[oc % 2][:, :], wt_b[:, c * D + oc * 128: c * D + (oc + 1) * 128],
                                  rhs_t[:, c * 512:(c + 1) * 512],
                                  start=(c == 0), stop=False).then_inc(s1.h, 1)
                    te.wait_ge(s2.h, b2 + 1)
                    te.matmul(pb[oc % 2][:, :], lbb_b[0:32, oc * 128:(oc + 1) * 128],
                              t1b_t[0:32, :], start=False, stop=True).then_inc(s1.h, 1)
                s1.v += NCH * (NCH + 1)

            @blk.vector
            def _(ve):
                ve.wait_ge(s1.h, b1 + NCH)
                ve.tensor_copy(t1b_t[0:32, :], pb[5][0:32, :]).then_inc(s2.h, 1)
                for oc in range(NCH):
                    ve.wait_ge(s1.h, b1 + NCH + (NCH + 1) * (oc + 1))
                    out_cb(ve, oc)
                s2.v += 1 + NCH

    for qb in range(NQB):
        emit_ln(xres[:, :], hb_t, w=QB, dst_f=h_t, nch=NCH, cs=NTOK, t0=qb * QB)
        for blki in range(3):
            i0, i1 = 2 * blki, 2 * blki + 1

            def cb0(ve, oc, i0=i0):
                ve.tensor_scalar(ybf_t[:, oc * 512:(oc + 1) * 512], pb[oc % 2][:, :],
                                 l1b_s[:, i0 * 8 + oc: i0 * 8 + oc + 1], None,
                                 OP.add).then_inc(s2.h, 1)
            l1_linear(i0, hb_t, cb0)

            with nc.Block() as blk:
                b1, b2, b3 = s1.v, s2.v, s3.v

                @blk.scalar
                def _(sc):
                    for c in range(NCH):
                        if c > 0:
                            sc.wait_ge(s1.h, b1 + 2 * c)
                        sc.activation(sqs_t[:, :], ybf_t[:, c * 512:(c + 1) * 512],
                                      AF.Square).then_inc(s3.h, 1)
                    s3.v += NCH

                @blk.tensor
                def _(te):
                    for c in range(NCH):
                        te.matmul(pb[2][0:1, :], oc_b[:, 0:1],
                                  ybf_t[:, c * 512:(c + 1) * 512],
                                  start=(c == 0), stop=(c == NCH - 1)).then_inc(s1.h, 1)
                        te.wait_ge(s3.h, b3 + c + 1)
                        te.matmul(pb[3][0:1, :], oc_b[:, 0:1], sqs_t[:, :],
                                  start=(c == 0), stop=(c == NCH - 1)).then_inc(s1.h, 1)
                    s1.v += 2 * NCH

                @blk.vector
                def _(ve):
                    ve.wait_ge(s1.h, b1 + 2 * NCH)
                    ve.tensor_copy(lr0[0:1, :], pb[2][0:1, :]).then_inc(sv.h, 1)
                    sv.v += 1
                    ve.wait_ge(sv.h, sv.v)
                    ve.tensor_copy(lr1[0:1, :], pb[3][0:1, :]).then_inc(sv.h, 1)
                    sv.v += 1
                    ve.wait_ge(sv.h, sv.v)
                    ve.tensor_scalar(lr2[0:1, :], lr1[0:1, :], 1.0 / D, float(EPS), OP.mult, OP.add).then_inc(sv.h, 1)
                    sv.v += 1
                    ve.wait_ge(sv.h, sv.v)
                    ve.tensor_scalar(lr1[0:1, :], lr0[0:1, :], 1.0 / D, None, OP.mult).then_inc(sv.h, 1)
                    sv.v += 1
                    ve.wait_ge(sv.h, sv.v)
                    ve.tensor_tensor(lr0[0:1, :], lr1[0:1, :], lr1[0:1, :], OP.mult).then_inc(sv.h, 1)
                    sv.v += 1
                    ve.wait_ge(sv.h, sv.v)
                    ve.tensor_tensor(lr0[0:1, :], lr2[0:1, :], lr0[0:1, :], OP.subtract).then_inc(s2.h, 1)
                    s2.v += 1

                @blk.scalar
                def _(sc):
                    sc.wait_ge(s2.h, b2 + 1)
                    sc.activation(lr2[0:1, :], lr0[0:1, :], AF.Sqrt).then_inc(s3.h, 1)
                    s3.v += 1

            with nc.Block() as blk:
                @blk.vector
                def _(ve):
                    ve.reciprocal(lr0[0:1, :], lr2[0:1, :]).then_inc(sv.h, 1)
                    sv.v += 1
                    ve.wait_ge(sv.h, sv.v)
                    ve.tensor_tensor(lr2[0:1, :], lr1[0:1, :], lr0[0:1, :], OP.mult).then_inc(sv.h, 1)
                    sv.v += 1
                    ve.wait_ge(sv.h, sv.v)
                    ve.tensor_copy(abb3[0:1, 0:512], lr0[0:1, :])
                    ve.tensor_copy(abb3[0:1, 512:1024], lr2[0:1, :])

            with nc.Block() as blk:
                b1, b2, b3 = s1.v, s2.v, s3.v

                @blk.tensor
                def _(te):
                    te.matmul(pb[2][:, :], or_b[0:1, :], abb3[0:1, 0:512],
                              start=True, stop=True).then_inc(s1.h, 1)
                    te.matmul(pb[3][:, :], or_b[0:1, :], abb3[0:1, 512:1024],
                              start=True, stop=True).then_inc(s1.h, 1)
                    s1.v += 2

                @blk.vector
                def _(ve):
                    ve.wait_ge(s1.h, b1 + 2)
                    for c in range(NCH):
                        sl = slice(c * 512, (c + 1) * 512)
                        ve.tensor_tensor(sqs_t[:, :], ybf_t[:, sl], pb[2][:, :], OP.mult)
                        ve.tensor_tensor(tb_t[:, :], sqs_t[:, :], pb[3][:, :], OP.subtract)
                        ve.tensor_scalar(ybf_t[:, sl], tb_t[:, :],
                                         lng_s[:, blki * 8 + c: blki * 8 + c + 1],
                                         lnb_s[:, blki * 8 + c: blki * 8 + c + 1],
                                         OP.mult, OP.add).then_inc(s2.h, 1)
                    s2.v += NCH

                @blk.scalar
                def _(sc):
                    for c in range(NCH):
                        sc.wait_ge(s2.h, b2 + c + 1)
                        sc.activation(g_t[:, c * 512:(c + 1) * 512],
                                      ybf_t[:, c * 512:(c + 1) * 512], AF.Gelu).then_inc(s3.h, 1)
                    s3.v += NCH

            def cb1(ve, oc, i1=i1):
                sl = slice(oc * 512, (oc + 1) * 512)
                ve.scalar_tensor_tensor(h_t[:, sl], pb[oc % 2][:, :],
                                        l1b_s[:, i1 * 8 + oc: i1 * 8 + oc + 1],
                                        h_t[:, sl], OP.add, OP.add).then_inc(sv.h, 1)
                sv.v += 1
                ve.wait_ge(sv.h, sv.v)
                ve.tensor_copy(hb_t[:, sl], h_t[:, sl]).then_inc(s2.h, 1)
            l1_linear(i1, g_t, cb1)

        with nc.Block() as blk:
            @blk.vector
            def _(ve):
                for c in range(NCH):
                    sl = slice(c * NTOK + qb * QB, c * NTOK + qb * QB + QB)
                    ve.tensor_tensor(xres[:, sl], h_t[:, c * 512:(c + 1) * 512],
                                     xres[:, sl], OP.add)
    E.pop()
    Dd.pop()
    A.pop()

    emit_store_y()
    return nc, tap_tensors


def make_in_maps(inputs):
    xnp = np.asarray(inputs["L0_attn_x"])
    TT = lambda a: np.ascontiguousarray(np.asarray(a).T)
    Cc = lambda a: np.ascontiguousarray(np.asarray(a))

    gw = np.zeros((16, 1024), np.float32)
    gw[0:8] = np.asarray(inputs["L0_ffn_eg_w"])
    gw[8] = np.asarray(inputs["L0_ffn_sg_w"])[0]
    gbv = np.zeros((16, 1), np.float32)
    gbv[0:8, 0] = np.asarray(inputs["L0_ffn_eg_b"])
    gbv[8, 0] = np.asarray(inputs["L0_ffn_sg_b"])[0]

    shared = {
        "w_dqT": TT(inputs["L0_attn_W_dq"]),
        "w_dq": Cc(inputs["L0_attn_W_dq"]),
        "w_uqT": TT(inputs["L0_attn_W_uq"]),
        "w_uq": Cc(inputs["L0_attn_W_uq"]),
        "w_dkvT": TT(inputs["L0_attn_W_dkv"]),
        "w_uk": Cc(inputs["L0_attn_W_uk"]),
        "w_uv": Cc(inputs["L0_attn_W_uv"]),
        "w_oT": TT(inputs["L0_attn_W_o"]),
        "sw1T": TT(inputs["L0_ffn_sw1"]),
        "sw2T": TT(inputs["L0_ffn_sw2"]),
        "sw3T": TT(inputs["L0_ffn_sw3"]),
        "sb1": Cc(inputs["L0_ffn_sb1"]).reshape(-1, 1),
        "sb2": Cc(inputs["L0_ffn_sb2"]).reshape(-1, 1),
        "sb3": Cc(inputs["L0_ffn_sb3"]).reshape(-1, 1),
        "s_ln_g": Cc(inputs["L0_ffn_s_ln_g"]).reshape(-1, 1),
        "s_ln_b": Cc(inputs["L0_ffn_s_ln_b"]).reshape(-1, 1),
        "gwT": TT(gw),
        "gb": gbv,
        "exp_bias": Cc(inputs["L0_ffn_exp_bias"]).reshape(-1, 1),
        "sel_ln_g": Cc(inputs["L0_ffn_sel_ln_g"]).reshape(1, -1),
        "sel_ln_b": Cc(inputs["L0_ffn_sel_ln_b"]).reshape(1, -1),
        "idx_b": np.asarray(inputs["L0_ffn_index_b"], dtype=np.int32),
        "idx_t": np.asarray(inputs["L0_ffn_index_t"], dtype=np.int32),
        "w_qkvT": TT(inputs["L1_attn_W_qkv"]),
        "w_l1oT": TT(inputs["L1_attn_W_o"]),
        "qwT": Cc(np.asarray(inputs["L1_ffn_qweights"], dtype=np.int32).transpose(0, 2, 1)),
        "scT": Cc(np.repeat(np.asarray(inputs["L1_ffn_scales"]), 16, axis=2).transpose(0, 2, 1)),
        "l1_biases": Cc(inputs["L1_ffn_biases"]),
        "laT": Cc(np.asarray(inputs["L1_ffn_lora_a"]).transpose(0, 2, 1)),
        "lbT": Cc(np.asarray(inputs["L1_ffn_lora_b"]).transpose(0, 2, 1)),
        "ln_g": Cc(inputs["L1_ffn_ln_gamma"]),
        "ln_b": Cc(inputs["L1_ffn_ln_beta"]),
    }
    ew1 = np.asarray(inputs["L0_ffn_ew1"])
    ew2 = np.asarray(inputs["L0_ffn_ew2"])
    ew3 = np.asarray(inputs["L0_ffn_ew3"])
    eb1 = np.asarray(inputs["L0_ffn_eb1"])
    eb2 = np.asarray(inputs["L0_ffn_eb2"])
    eb3 = np.asarray(inputs["L0_ffn_eb3"])

    in_maps = []
    for c in range(8):
        gs = _core_blocks(c)
        bb = c // 4
        xo = np.concatenate([xnp[bb, (g % 16) * QB:(g % 16) * QB + QB] for g in gs], axis=0)
        m = dict(shared)
        m["xT"] = np.ascontiguousarray(xo.T)
        m["ew1T"] = TT(ew1[c])
        m["ew2T"] = TT(ew2[c])
        m["ew3T"] = TT(ew3[c])
        m["eb1"] = Cc(eb1[c]).reshape(1, -1)
        m["eb2"] = Cc(eb2[c]).reshape(1, -1)
        m["eb3"] = Cc(eb3[c]).reshape(1, -1)
        meta = np.zeros((128, 4), np.int32)
        for j, g in enumerate(gs):
            meta[:, j] = g
        m["core_meta"] = meta
        oh = np.zeros((128, 8), np.float32)
        oh[:, c] = 1.0
        m["my_onehot"] = oh
        q0v = np.zeros((128, 4), np.float32)
        ktp = np.zeros((128, 256), np.float32)
        for j, g in enumerate(gs):
            q0v[:, j] = (g % 16) * QB
            v = 4 * ((g % 16) + 1)
            ktp[:, j * 64 + v: j * 64 + 64] = -30000.0
        m["q0s"] = q0v
        m["ktpen"] = ktp
        m["meta_f"] = meta.astype(np.float32)
        in_maps.append(m)
    return in_maps


def assemble_output(results):
    y = np.zeros((B, T, D), dtype=np.float32)
    for c in range(8):
        gs = _core_blocks(c)
        bb = c // 4
        yt = results[c]["y"].T
        for j, g in enumerate(gs):
            y[bb, (g % 16) * QB:(g % 16) * QB + QB] = yt[j * QB:(j + 1) * QB]
    return y


def kernel(_taps=(), _upto=None, **inputs):
    nc, tap_tensors = build_program(taps=_taps, upto=_upto)
    in_maps = make_in_maps(inputs)
    res = run_bass_kernel_spmd(nc, in_maps, core_ids=list(range(8)))
    if _taps or _upto:
        return res
    return assemble_output(res.results)


# version 02
# speedup vs baseline: 1.2906x; 1.2906x over previous
import numpy as np
import ml_dtypes as md
import concourse.bass as bass
import concourse.mybir as mybir
from concourse.bass_core_rust import IndirectOffsetOnAxis
from concourse.bass_utils import run_bass_kernel_spmd

F32 = mybir.dt.float32
BF16 = mybir.dt.bfloat16
I32 = mybir.dt.int32
AF = mybir.ActivationFunctionType
OP = mybir.AluOpType
AX = mybir.AxisListType

B, T, D = 2, 8192, 1024
NTOK = 2048
NQB = 4
QB = 512
NCH = 8
NH, HS, KVL = 16, 64, 128
EPS = 1e-5
MOE_E, MOE_H, MOE_M = 8, 512, 2048
LORA_R = 32

bf = md.bfloat16


def _core_blocks(c):
    b, i = c // 4, c % 4
    return [16 * b + i, 16 * b + 7 - i, 16 * b + 8 + i, 16 * b + 15 - i]


def _block_owner(g):
    b, gg = g // 16, g % 16
    if gg <= 3:
        i, j = gg, 0
    elif gg <= 7:
        i, j = 7 - gg, 1
    elif gg <= 11:
        i, j = gg - 8, 2
    else:
        i, j = 15 - gg, 3
    return 4 * b + i, j


class Ctr:
    def __init__(self, nc, name):
        self.h = nc.alloc_semaphore(name)
        self.v = 0


class Arena:
    def __init__(self, nc, name, n, dtype):
        self.t = nc.alloc_sbuf_tensor(name, [128, n], dtype)
        self.n = n
        self.off = 0
        self.marks = []

    def take(self, n):
        assert self.off + n <= self.n, f"arena {self.t.name} overflow {self.off}+{n}>{self.n}"
        off = self.off
        self.off += n
        return self.t[:, off:off + n]

    def push(self):
        self.marks.append(self.off)

    def pop(self):
        self.off = self.marks.pop()


def build_program(taps=(), upto=None):
    nc = bass.Bass("TRN2", target_bir_lowering=False)

    def inp(name, shape, dtype=F32):
        return nc.dram_tensor(name, shape, dtype, kind="ExternalInput")

    xT_in = inp("xT", [D, NTOK])
    w_dqT = inp("w_dqT", [D, KVL])
    w_dq = inp("w_dq", [KVL, D])
    w_uq = inp("w_uq", [D, KVL])
    w_uqT = inp("w_uqT", [KVL, D])
    w_dkvT = inp("w_dkvT", [D, KVL])
    w_uk = inp("w_uk", [D, KVL])
    w_uv = inp("w_uv", [D, KVL])
    w_oT = inp("w_oT", [D, D])
    sw1T = inp("sw1T", [D, MOE_H])
    sw2T = inp("sw2T", [D, MOE_H])
    sw3T = inp("sw3T", [MOE_H, D])
    sb1 = inp("sb1", [MOE_H, 1])
    sb2 = inp("sb2", [MOE_H, 1])
    sb3 = inp("sb3", [D, 1])
    s_ln_g = inp("s_ln_g", [MOE_H, 1])
    s_ln_b = inp("s_ln_b", [MOE_H, 1])
    gwT = inp("gwT", [D, 16])
    gb = inp("gb", [16, 1])
    exp_bias = inp("exp_bias", [MOE_E, 1])
    ew1T = inp("ew1T", [D, MOE_H])
    ew2T = inp("ew2T", [D, MOE_H])
    ew3T = inp("ew3T", [MOE_H, D])
    eb1 = inp("eb1", [1, MOE_H])
    eb2 = inp("eb2", [1, MOE_H])
    eb3 = inp("eb3", [1, D])
    sel_ln_g = inp("sel_ln_g", [1, MOE_H])
    sel_ln_b = inp("sel_ln_b", [1, MOE_H])
    idx_b = inp("idx_b", [MOE_M], I32)
    idx_t = inp("idx_t", [MOE_M], I32)
    core_meta = inp("core_meta", [128, 4], I32)
    my_onehot = inp("my_onehot", [128, 8])
    q0s = inp("q0s", [128, 4])
    ktpen = inp("ktpen", [128, 256])
    meta_f = inp("meta_f", [128, 4])
    w_qkvT = inp("w_qkvT", [D, D])
    w_l1oT = inp("w_l1oT", [D, D])
    qwT = inp("qwT", [6, D, D], I32)
    scT = inp("scT", [6, D, D])
    l1_biases = inp("l1_biases", [6, D])
    laT = inp("laT", [6, D, LORA_R])
    lbT = inp("lbT", [6, LORA_R, D])
    ln_g = inp("ln_g", [3, D])
    ln_b = inp("ln_b", [3, D])

    y_out = nc.dram_tensor("y", [D, NTOK], F32, kind="ExternalOutput")

    identb = nc.inline_tensor(np.eye(128, dtype=bf), "identb")
    identf = nc.inline_tensor(np.eye(128, dtype=np.float32), "identf")
    ones_col_b = nc.inline_tensor(np.ones((128, 2), dtype=bf), "ones_col_b")
    ones_row_b = nc.inline_tensor(np.ones((1, 128), dtype=bf), "ones_row_b")
    ones_col_f = nc.inline_tensor(np.ones((128, 1), dtype=np.float32), "ones_col_f")
    ones_row_f = nc.inline_tensor(np.ones((1, 128), dtype=np.float32), "ones_row_f")
    jcols0 = nc.inline_tensor((np.arange(128, dtype=np.float32)[:, None] +
                               128.0 * np.arange(16, dtype=np.float32)[None, :]).copy(), "jcols0")
    bigcol0 = nc.inline_tensor(np.full((128, 1), 65536.0, np.float32), "bigcol0")
    colmap0 = nc.inline_tensor(
        np.broadcast_to(np.arange(QB, dtype=np.float32)[None, :], (128, QB)).copy(), "colmap0")
    rowmap0 = nc.inline_tensor(
        (np.arange(128, dtype=np.float32)[:, None] +
         128.0 * np.arange(64, dtype=np.float32)[None, :]).copy(), "rowmap0")

    ckv_contrib = nc.dram_tensor("ckv_contrib", [2 * NTOK, KVL], BF16)
    ckv_ag = nc.dram_tensor("ckv_ag", [4 * 2 * NTOK, KVL], BF16)
    qkv_contrib = nc.dram_tensor("qkv_contrib", [2 * NTOK, D], BF16)
    qkv_ag = nc.dram_tensor("qkv_ag", [4 * 2 * NTOK, D], BF16)
    xm_tm_dram = nc.dram_tensor("xm_tm_dram", [NTOK + 128, D], BF16)
    xmoe_contrib = nc.dram_tensor("xmoe_contrib", [MOE_M, D], BF16)
    xmoe_all = nc.dram_tensor("xmoe_all", [MOE_M, D], BF16)
    delta_contrib = nc.dram_tensor("delta_contrib", [MOE_M, D], F32)
    delta_all = nc.dram_tensor("delta_all", [MOE_M + 128, D], F32)
    own_scale = nc.dram_tensor("own_scale", [NTOK + 128, 1], F32)
    own_delta = nc.dram_tensor("own_delta", [NTOK, D], F32)
    own_inv = nc.dram_tensor("own_inv", [NTOK + 128, 1], F32)
    wq_dram = nc.dram_tensor("wq_dram", [6, D, D], BF16)

    xres = nc.alloc_sbuf_tensor("xres", [128, NCH * NTOK], F32)
    A = Arena(nc, "argA", 16384, BF16)
    Bb = Arena(nc, "argB", 16384, BF16)
    C = Arena(nc, "argC", 16384, BF16)
    Dd = Arena(nc, "argD", 4096, BF16)
    E = Arena(nc, "argE", 8192, F32)
    Si = Arena(nc, "argSi", 256, I32)

    idb_s = nc.alloc_sbuf_tensor("idb_s", [128, 128], BF16)
    idf_s = nc.alloc_sbuf_tensor("idf_s", [128, 128], F32)
    oc_b = nc.alloc_sbuf_tensor("oc_b", [128, 2], BF16)
    or_b = nc.alloc_sbuf_tensor("or_b", [1, 128], BF16)
    oc_f = nc.alloc_sbuf_tensor("oc_f", [128, 1], F32)
    or_f = nc.alloc_sbuf_tensor("or_f", [1, 128], F32)
    colmap_s = nc.alloc_sbuf_tensor("colmap_s", [128, QB], F32)
    rowmap_s = nc.alloc_sbuf_tensor("rowmap_s", [128, 64], F32)
    q0s_s = nc.alloc_sbuf_tensor("q0s_s", [128, 4], F32)
    ktpen_s = nc.alloc_sbuf_tensor("ktpen_s", [128, 256], F32)
    meta_s = nc.alloc_sbuf_tensor("meta_s", [128, 4], I32)
    metaf_s = nc.alloc_sbuf_tensor("metaf_s", [128, 4], F32)
    onehot_s = nc.alloc_sbuf_tensor("onehot_s", [128, 8], F32)

    pb = [nc.alloc_psum_tensor(f"pb{i}", [128, 512], F32) for i in range(7)]
    pbh = nc.alloc_psum_tensor("pbh", [128, 1024], BF16)

    sd = Ctr(nc, "sd")
    sg = Ctr(nc, "sg")
    s1 = Ctr(nc, "s1")
    s2 = Ctr(nc, "s2")
    s3 = Ctr(nc, "s3")
    s4 = Ctr(nc, "s4")
    sv = Ctr(nc, "sv")
    sco = Ctr(nc, "sco")

    tap_tensors = {}

    def tap_fm(name, src, nch=NCH, width=NTOK, dtype=F32):
        if name not in taps:
            return
        t = nc.dram_tensor("tap_" + name, [nch * 128, width], dtype, kind="ExternalOutput")
        tap_tensors[name] = t
        with nc.Block() as blk:
            @blk.sync
            def _(sy):
                for c in range(nch):
                    sy.dma_start(out=t.ap()[c * 128:(c + 1) * 128, :],
                                 in_=src[:, c * width:(c + 1) * width]).then_inc(sd.h, 16)
                sd.v += nch * 16
                sy.wait_ge(sd.h, sd.v)

    def tap_copy_dram(name, src_dram, shape, dtype):
        if name not in taps:
            return
        t = nc.dram_tensor("tap_" + name, shape, dtype, kind="ExternalOutput")
        tap_tensors[name] = t
        with nc.Block() as blk:
            @blk.sync
            def _(sy):
                sy.dma_start(out=t.ap(), in_=src_dram.ap()).then_inc(sd.h, 16)
                sd.v += 16
                sy.wait_ge(sd.h, sd.v)

    with nc.Block() as blk:
        @blk.sync
        def _(sy):
            for c in range(NCH):
                sy.dma_start(out=xres[:, c * NTOK:(c + 1) * NTOK],
                             in_=xT_in.ap()[c * 128:(c + 1) * 128, :]).then_inc(sd.h, 16)
            sy.dma_start(out=idb_s[:], in_=identb.ap()).then_inc(sd.h, 16)
            sy.dma_start(out=idf_s[:], in_=identf.ap()).then_inc(sd.h, 16)
            sy.dma_start(out=oc_b[:], in_=ones_col_b.ap()).then_inc(sd.h, 16)
            sy.dma_start(out=or_b[:], in_=ones_row_b.ap()).then_inc(sd.h, 16)
            sy.dma_start(out=oc_f[:], in_=ones_col_f.ap()).then_inc(sd.h, 16)
            sy.dma_start(out=meta_s[:], in_=core_meta.ap()).then_inc(sd.h, 16)
            sy.dma_start(out=metaf_s[:], in_=meta_f.ap()).then_inc(sd.h, 16)
            sy.dma_start(out=onehot_s[:], in_=my_onehot.ap()).then_inc(sd.h, 16)
            sy.dma_start(out=or_f[:], in_=ones_row_f.ap()).then_inc(sd.h, 16)
            sy.dma_start(out=colmap_s[:], in_=colmap0.ap()).then_inc(sd.h, 16)
            sy.dma_start(out=rowmap_s[:], in_=rowmap0.ap()).then_inc(sd.h, 16)
            sy.dma_start(out=q0s_s[:], in_=q0s.ap()).then_inc(sd.h, 16)
            sy.dma_start(out=ktpen_s[:], in_=ktpen.ap()).then_inc(sd.h, 16)
            sd.v += (NCH + 13) * 16
            sy.wait_ge(sd.h, sd.v)

    def emit_ln(src, dst_b, w=NTOK, dst_f=None, nch=NCH, cs=None, t0=0):
        E.push(); Dd.push()
        r0t = E.take(w)
        r1t = E.take(w)
        r2t = E.take(w)
        sq = E.take(512)
        tsc = E.take(512)
        abb = Dd.take(2 * w)
        nqb = w // QB
        if cs is None:
            cs = w
        srow, qrow = r0t[0:1, :], r1t[0:1, :]
        arow = r2t[0:1, :]

        with nc.Block() as blk:
            b1, b2, b3 = s1.v, s2.v, s3.v

            @blk.scalar
            def _(sc):
                for qb in range(nqb):
                    for c in range(nch):
                        i = qb * nch + c
                        if i > 0:
                            sc.wait_ge(s1.h, b1 + 2 * i)
                        sc.activation(sq[:], src[:, c * cs + t0 + qb * QB: c * cs + t0 + qb * QB + QB],
                                      AF.Square).then_inc(s3.h, 1)
                s3.v += nqb * nch

            @blk.tensor
            def _(te):
                for qb in range(nqb):
                    for c in range(nch):
                        i = qb * nch + c
                        if c == 0 and qb > 0:
                            te.wait_ge(s2.h, b2 + 2 * qb)
                        te.matmul(pb[0][0:1, :], oc_f[:, 0:1],
                                  src[:, c * cs + t0 + qb * QB: c * cs + t0 + qb * QB + QB],
                                  start=(c == 0), stop=(c == nch - 1)).then_inc(s1.h, 1)
                        te.wait_ge(s3.h, b3 + i + 1)
                        te.matmul(pb[1][0:1, :], oc_f[:, 0:1], sq[:],
                                  start=(c == 0), stop=(c == nch - 1)).then_inc(s1.h, 1)
                s1.v += 2 * nqb * nch

            @blk.vector
            def _(ve):
                for qb in range(nqb):
                    ve.wait_ge(s1.h, b1 + 2 * nch * (qb + 1))
                    ve.tensor_copy(srow[0:1, qb * QB:(qb + 1) * QB], pb[0][0:1, :]).then_inc(s2.h, 1)
                    ve.tensor_copy(qrow[0:1, qb * QB:(qb + 1) * QB], pb[1][0:1, :]).then_inc(s2.h, 1)
                s2.v += 2 * nqb

        dd = float(nch * 128)
        with nc.Block() as blk:
            @blk.vector
            def _(ve):
                ve.tensor_scalar(arow, qrow, 1.0 / dd, float(EPS), OP.mult, OP.add)
        with nc.Block() as blk:
            @blk.vector
            def _(ve):
                ve.tensor_scalar(qrow, srow, 1.0 / dd, None, OP.mult)
        with nc.Block() as blk:
            @blk.vector
            def _(ve):
                ve.tensor_tensor(srow, qrow, qrow, OP.mult)
        with nc.Block() as blk:
            @blk.vector
            def _(ve):
                ve.tensor_tensor(srow, arow, srow, OP.subtract)
        with nc.Block() as blk:
            @blk.scalar
            def _(sc):
                sc.activation(arow, srow, AF.Sqrt)
        with nc.Block() as blk:
            @blk.vector
            def _(ve):
                ve.reciprocal(srow, arow)
        with nc.Block() as blk:
            @blk.vector
            def _(ve):
                ve.tensor_tensor(arow, qrow, srow, OP.mult)
        with nc.Block() as blk:
            @blk.vector
            def _(ve):
                ve.tensor_copy(abb[0:1, 0:w], srow)
                ve.tensor_copy(abb[0:1, w:2 * w], arow)

        with nc.Block() as blk:
            b1, b2 = s1.v, s2.v

            @blk.tensor
            def _(te):
                for qb in range(nqb):
                    if qb > 0:
                        te.wait_ge(s2.h, b2 + qb)
                    te.matmul(pb[2][:, :], or_b[0:1, :], abb[0:1, qb * QB:(qb + 1) * QB],
                              start=True, stop=True).then_inc(s1.h, 1)
                    te.matmul(pb[3][:, :], or_b[0:1, :], abb[0:1, w + qb * QB: w + (qb + 1) * QB],
                              start=True, stop=True).then_inc(s1.h, 1)
                s1.v += 2 * nqb

            @blk.vector
            def _(ve):
                for qb in range(nqb):
                    ve.wait_ge(s1.h, b1 + 2 * qb + 2)
                    for c in range(nch):
                        sl = slice(c * w + qb * QB, c * w + qb * QB + QB)
                        ssl = slice(c * cs + t0 + qb * QB, c * cs + t0 + qb * QB + QB)
                        ve.tensor_tensor(tsc[:], src[:, ssl], pb[2][:, :], OP.mult)
                        if dst_f is not None:
                            ve.tensor_tensor(dst_f[:, sl], tsc[:], pb[3][:, :], OP.subtract)
                            bi = ve.tensor_copy(dst_b[:, sl], dst_f[:, sl])
                        else:
                            bi = ve.tensor_tensor(dst_b[:, sl], tsc[:], pb[3][:, :], OP.subtract)
                        if c == nch - 1:
                            bi.then_inc(s2.h, 1)
                s2.v += nqb
        E.pop(); Dd.pop()

    def emit_wload(wdram, dst, nblk, cols, rows=128):
        wap = wdram if isinstance(wdram, bass.AP) else wdram.ap()
        if wap.tensor.dtype == BF16:
            with nc.Block() as blk:
                @blk.sync
                def _(sy):
                    for i in range(nblk):
                        sy.dma_start(out=dst[0:rows, i * cols:(i + 1) * cols],
                                     in_=wap[i * rows:(i + 1) * rows, :]).then_inc(sd.h, 16)
                    sd.v += nblk * 16
                    sy.wait_ge(sd.h, sd.v)
            return
        E.push()
        stg = [E.take(cols) for _ in range(2)]
        with nc.Block() as blk:
            b4 = s4.v
            bd = sd.v

            @blk.sync
            def _(sy):
                for i in range(nblk):
                    if i >= 2:
                        sy.wait_ge(s4.h, b4 + i - 1)
                    sy.dma_start(out=stg[i % 2][0:rows, :],
                                 in_=wap[i * rows:(i + 1) * rows, :]).then_inc(sd.h, 16)
                sd.v += nblk * 16

            @blk.gpsimd
            def _(gp):
                for i in range(nblk):
                    gp.wait_ge(sd.h, bd + 16 * (i + 1))
                    gp.tensor_copy(dst[0:rows, i * cols:(i + 1) * cols],
                                   stg[i % 2][0:rows, :]).then_inc(s4.h, 1)
                s4.v += nblk
        E.pop()

    A.push()
    xn = A.take(NCH * NTOK)
    emit_ln(xres[:, :], xn)
    tap_fm("xn", xn, dtype=BF16)

    def emit_store_y():
        with nc.Block() as blk:
            @blk.sync
            def _(sy):
                for c in range(NCH):
                    sy.dma_start(out=y_out.ap()[c * 128:(c + 1) * 128, :],
                                 in_=xres[:, c * NTOK:(c + 1) * NTOK]).then_inc(sd.h, 16)
                sd.v += NCH * 16
                sy.wait_ge(sd.h, sd.v)

    if upto == "P1":
        emit_store_y()
        return nc, tap_tensors

    keff_b = C.take(NCH * KVL)
    v1_b = C.take(NCH * KVL)
    C.push()
    wdkvT_b = C.take(NCH * KVL)
    wdqT_b = C.take(NCH * KVL)
    wuqT_b = C.take(D)
    wAux_b = C.take(NCH * KVL)
    wuk_b = C.take(NCH * KVL)
    wdq_b = C.take(D)
    c1_b = C.take(KVL)

    emit_wload(w_dkvT, wdkvT_b, NCH, KVL)
    emit_wload(w_dqT, wdqT_b, NCH, KVL)
    emit_wload(w_uqT, wuqT_b, 1, D)
    emit_wload(w_uq, wAux_b, NCH, KVL)
    emit_wload(w_uk, wuk_b, NCH, KVL)
    emit_wload(w_dq, wdq_b, 1, D)

    with nc.Block() as blk:
        b1 = s1.v

        @blk.tensor
        def _(te):
            for i in range(NCH):
                te.matmul(pb[0][:, 0:KVL], wAux_b[:, i * KVL:(i + 1) * KVL],
                          wuk_b[:, i * KVL:(i + 1) * KVL],
                          start=(i == 0), stop=(i == NCH - 1)).then_inc(s1.h, 1)
            s1.v += NCH

        @blk.vector
        def _(ve):
            ve.wait_ge(s1.h, b1 + NCH)
            ve.tensor_copy(c1_b[:, :], pb[0][:, 0:KVL])

    with nc.Block() as blk:
        b1, b2 = s1.v, s2.v

        @blk.tensor
        def _(te):
            for c in range(NCH):
                if c >= 2:
                    te.wait_ge(s2.h, b2 + c - 1)
                te.matmul(pb[c % 2][:, 0:KVL], wdq_b[:, c * KVL:(c + 1) * KVL],
                          c1_b[:, :], start=True, stop=True).then_inc(s1.h, 1)
            s1.v += NCH

        @blk.vector
        def _(ve):
            for c in range(NCH):
                ve.wait_ge(s1.h, b1 + c + 1)
                ve.tensor_copy(keff_b[:, c * KVL:(c + 1) * KVL],
                               pb[c % 2][:, 0:KVL]).then_inc(s2.h, 1)
            s2.v += NCH

    emit_wload(w_uv, wAux_b, NCH, KVL)
    Bb.push()
    woT_b = Bb.take(NCH * D)
    emit_wload(w_oT, woT_b, NCH, D)
    with nc.Block() as blk:
        b1, b2 = s1.v, s2.v

        @blk.tensor
        def _(te):
            for j in range(NCH):
                if j >= 2:
                    te.wait_ge(s2.h, b2 + j - 1)
                for c in range(NCH):
                    te.matmul(pb[j % 2][:, 0:KVL], wAux_b[:, c * KVL:(c + 1) * KVL],
                              woT_b[:, c * D + j * 128: c * D + j * 128 + 128],
                              start=(c == 0), stop=(c == NCH - 1)).then_inc(s1.h, 1)
            s1.v += NCH * NCH

        @blk.vector
        def _(ve):
            for j in range(NCH):
                ve.wait_ge(s1.h, b1 + NCH * (j + 1))
                ve.tensor_copy(v1_b[:, j * KVL:(j + 1) * KVL],
                               pb[j % 2][:, 0:KVL]).then_inc(s2.h, 1)
            s2.v += NCH
    Bb.pop()

    if upto == "P2":
        tap_fm("keff", keff_b, nch=1, width=NCH * KVL, dtype=BF16)
        tap_fm("v1", v1_b, nch=1, width=NCH * KVL, dtype=BF16)
        emit_store_y()
        return nc, tap_tensors

    Dd.push()
    ckv_own = Dd.take(NTOK)
    ckv_own_tm = Dd.take(NTOK)
    with nc.Block() as blk:
        b1, b2 = s1.v, s2.v

        @blk.tensor
        def _(te):
            for qb in range(NQB):
                if qb >= 2:
                    te.wait_ge(s2.h, b2 + qb - 1)
                for c in range(NCH):
                    te.matmul(pb[qb % 2][:, :], wdkvT_b[:, c * KVL:(c + 1) * KVL],
                              xn[:, c * NTOK + qb * QB: c * NTOK + qb * QB + QB],
                              start=(c == 0), stop=(c == NCH - 1)).then_inc(s1.h, 1)
            s1.v += NQB * NCH

        @blk.vector
        def _(ve):
            for qb in range(NQB):
                ve.wait_ge(s1.h, b1 + NCH * (qb + 1))
                ve.tensor_copy(ckv_own[:, qb * QB:(qb + 1) * QB],
                               pb[qb % 2][:, :]).then_inc(s2.h, 1)
            s2.v += NQB

    with nc.Block() as blk:
        b1, b2 = s1.v, s2.v

        @blk.tensor
        def _(te):
            for i in range(16):
                if i >= 1:
                    te.wait_ge(s2.h, b2 + i)
                te.transpose(pbh[:, (i % 2) * 128:(i % 2) * 128 + 128],
                             ckv_own[:, i * 128:(i + 1) * 128], idb_s[:]).then_inc(s1.h, 1)
            s1.v += 16

        @blk.vector
        def _(ve):
            for i in range(16):
                ve.wait_ge(s1.h, b1 + i + 1)
                ve.tensor_copy(ckv_own_tm[:, i * 128:(i + 1) * 128],
                               pbh[:, (i % 2) * 128:(i % 2) * 128 + 128]).then_inc(s2.h, 1)
            s2.v += 16

    with nc.Block() as blk:
        @blk.sync
        def _(sy):
            fmv = bass.AP(ckv_contrib, 0, [[NTOK, 128], [1, NTOK]])
            sy.dma_start(out=fmv, in_=ckv_own[:, :]).then_inc(sd.h, 16)
            for i in range(16):
                sy.dma_start(out=ckv_contrib.ap()[NTOK + i * 128: NTOK + (i + 1) * 128, :],
                             in_=ckv_own_tm[:, i * 128:(i + 1) * 128]).then_inc(sd.h, 16)
            sd.v += 17 * 16
            sy.wait_ge(sd.h, sd.v)
    Dd.pop()

    if upto == "P3a":
        emit_store_y()
        return nc, tap_tensors

    groups4 = [[0, 1, 2, 3], [4, 5, 6, 7]]
    with nc.Block() as blk:
        @blk.gpsimd
        def _(gp):
            gp.collective_compute("AllGather", OP.bypass, replica_groups=groups4,
                                  ins=[ckv_contrib.ap()], outs=[ckv_ag.ap()]).then_inc(sco.h, 1)
            sco.v += 1
            gp.wait_ge(sco.h, sco.v)

    if upto == "P3b":
        emit_store_y()
        return nc, tap_tensors

    Bb.push()
    qT = Bb.take(NCH * NTOK)
    Dd.push()
    t1_b = Dd.take(NTOK)
    with nc.Block() as blk:
        b1, b2 = s1.v, s2.v

        @blk.tensor
        def _(te):
            for qb in range(NQB):
                if qb >= 2:
                    te.wait_ge(s2.h, b2 + qb - 1)
                for c in range(NCH):
                    te.matmul(pb[qb % 2][:, :], wdqT_b[:, c * KVL:(c + 1) * KVL],
                              xn[:, c * NTOK + qb * QB: c * NTOK + qb * QB + QB],
                              start=(c == 0), stop=(c == NCH - 1)).then_inc(s1.h, 1)
            s1.v += NQB * NCH

        @blk.vector
        def _(ve):
            for qb in range(NQB):
                ve.wait_ge(s1.h, b1 + NCH * (qb + 1))
                ve.tensor_copy(t1_b[:, qb * QB:(qb + 1) * QB],
                               pb[qb % 2][:, :]).then_inc(s2.h, 1)
            s2.v += NQB

    with nc.Block() as blk:
        b1, b2 = s1.v, s2.v

        @blk.tensor
        def _(te):
            n = 0
            for c in range(NCH):
                for qb in range(NQB):
                    if n >= 2:
                        te.wait_ge(s2.h, b2 + n - 1)
                    te.matmul(pb[n % 2][:, :], wuqT_b[:, c * 128:(c + 1) * 128],
                              t1_b[:, qb * QB:(qb + 1) * QB],
                              start=True, stop=True).then_inc(s1.h, 1)
                    n += 1
            s1.v += NCH * NQB

        @blk.vector
        def _(ve):
            n = 0
            for c in range(NCH):
                for qb in range(NQB):
                    ve.wait_ge(s1.h, b1 + n + 1)
                    ve.tensor_copy(qT[:, c * NTOK + qb * QB: c * NTOK + qb * QB + QB],
                                   pb[n % 2][:, :]).then_inc(s2.h, 1)
                    n += 1
            s2.v += NCH * NQB
    Dd.pop()
    C.pop()

    A.pop()
    A.push()
    ckv_fm = A.take(8 * 1024)
    ckv_tm = A.take(8 * 1024)
    with nc.Block() as blk:
        @blk.sync
        def _(sy):
            for gg in range(16):
                cc, slot = _block_owner(gg)
                cc = cc % 4
                base = cc * 2 * NTOK
                fmv = bass.AP(ckv_ag, base * KVL + slot * QB, [[NTOK, 128], [1, QB]])
                sy.dma_start(out=ckv_fm[:, gg * QB:(gg + 1) * QB], in_=fmv).then_inc(sd.h, 16)
                for j in range(4):
                    kt = gg * 4 + j
                    r0 = base + NTOK + slot * QB + j * 128
                    sy.dma_start(out=ckv_tm[:, kt * 128:(kt + 1) * 128],
                                 in_=ckv_ag.ap()[r0:r0 + 128, :]).then_inc(sd.h, 16)
            sd.v += 16 * 5 * 16
            sy.wait_ge(sd.h, sd.v)

    if upto == "P3":
        tap_fm("ckv_fm", ckv_fm, nch=1, width=8192, dtype=BF16)
        tap_fm("ckv_tm", ckv_tm, nch=1, width=8192, dtype=BF16)
        tap_fm("qT", qT, nch=1, width=NCH * NTOK, dtype=BF16)
        emit_store_y()
        return nc, tap_tensors

    Dd.push()
    qlh = Dd.take(2 * NTOK)
    E.push()
    mrun = E.take(8 * QB)
    permax = E.take(8)
    negm8 = E.take(8)
    biast = E.take(8)
    biastx = E.take(512)
    recipf = E.take(1024)
    rects = E.take(1024)
    C.push()
    pt = [C.take(QB), C.take(QB)]
    ptm = [C.take(QB), C.take(QB)]
    ctxe = [C.take(QB), C.take(QB)]
    scrb = C.take(128)
    negmrow = C.take(16)

    UB = [16, 32, 48, 64]
    MB = [(0, 16), (16, 32), (32, 48), (48, 64)]

    for hp in range(NH // 2):
        g0, g1 = 2 * hp, 2 * hp + 1
        with nc.Block() as blk:
            b1, b2, bv = s1.v, s2.v, sv.v
            akts = {qb: list(range(0, UB[qb], 2)) for qb in range(NQB)}
            ntile = 2 * sum(len(v) for v in akts.values())

            @blk.tensor
            def _(te):
                n = 0
                for h01 in range(2):
                    for qb in range(NQB):
                        if n >= 2:
                            te.wait_ge(s2.h, b2 + n - 1)
                        te.matmul(pb[n % 2][:, :],
                                  keff_b[h01 * 64:(h01 + 1) * 64, hp * 128:(hp + 1) * 128],
                                  qT[h01 * 64:(h01 + 1) * 64,
                                     hp * NTOK + qb * QB: hp * NTOK + (qb + 1) * QB],
                                  start=True, stop=True).then_inc(s1.h, 1)
                        n += 1
                a = 0
                for qb in range(NQB):
                    for kt in akts[qb]:
                        for h01 in range(2):
                            if a < 4:
                                te.wait_ge(s2.h, b2 + 8)
                            else:
                                te.wait_ge(s2.h, b2 + 8 + a - 3)
                            te.matmul(pb[a % 4][:, :],
                                      ckv_fm[:, kt * 128:(kt + 1) * 128],
                                      qlh[:, h01 * NTOK + qb * QB: h01 * NTOK + (qb + 1) * QB],
                                      start=True, stop=True).then_inc(s1.h, 1)
                            a += 1
                s1.v += 8 + ntile
                te.wait_ge(s2.h, b2 + 8 + ntile + 1)
                te.transpose(pbh[:, 0:128], scrb[:, 0:128], idb_s[:]).then_inc(s1.h, 1)
                te.wait_ge(s2.h, b2 + 8 + ntile + 2)
                te.transpose(pbh[:, 128:256], scrb[:, 0:128], idb_s[:]).then_inc(s1.h, 1)
                te.wait_ge(s2.h, b2 + 8 + ntile + 3)
                te.matmul(pb[4][:, 0:16], or_b[0:1, :], negmrow[0:1, 0:16],
                          start=True, stop=True).then_inc(s1.h, 1)
                s1.v += 3

            @blk.vector
            def _(ve):
                for n in range(8):
                    h01, qb = divmod(n, NQB)
                    ve.wait_ge(s1.h, b1 + n + 1)
                    ve.tensor_copy(qlh[:, h01 * NTOK + qb * QB: h01 * NTOK + (qb + 1) * QB],
                                   pb[n % 2][:, :]).then_inc(s2.h, 1)
                a = 0
                for qb in range(NQB):
                    for kt in akts[qb]:
                        for h01 in range(2):
                            slot = h01 * 4 + qb
                            ve.wait_ge(s1.h, b1 + 8 + a + 1)
                            if kt == 0:
                                ve.tensor_copy(mrun[:, slot * QB:(slot + 1) * QB],
                                               pb[a % 4][:, :]).then_inc(s2.h, 1)
                            else:
                                ve.tensor_tensor(mrun[:, slot * QB:(slot + 1) * QB],
                                                 pb[a % 4][:, :],
                                                 mrun[:, slot * QB:(slot + 1) * QB],
                                                 OP.max).then_inc(s2.h, 1)
                            a += 1
                s2.v += 8 + ntile
                for k in range(8):
                    bi = ve.reduce_max(permax[:, k:k + 1], mrun[:, k * QB:(k + 1) * QB], axis=AX.X)
                bi.then_inc(sv.h, 1)
                ve.wait_ge(sv.h, bv + 1)
                ve.tensor_scalar(scrb[:, 0:8], permax[:, 0:8], 0.125, None, OP.mult).then_inc(s2.h, 1)
                ve.wait_ge(s1.h, b1 + 8 + ntile + 1)
                ve.reduce_max(negm8[0:8, 0:1], pbh[0:8, 0:128], axis=AX.X, negate=True).then_inc(sv.h, 1)
                ve.wait_ge(sv.h, bv + 2)
                ve.tensor_copy(scrb[0:8, 0:1], negm8[0:8, 0:1]).then_inc(s2.h, 1)
                ve.wait_ge(s1.h, b1 + 8 + ntile + 2)
                ve.tensor_copy(negmrow[0:1, 0:16], pbh[0:1, 128:144]).then_inc(s2.h, 1)
                ve.wait_ge(s1.h, b1 + 8 + ntile + 3)
                ve.tensor_copy(biast[:, 0:8], pb[4][:, 0:8]).then_inc(sv.h, 1)
                ve.wait_ge(sv.h, bv + 3)
                for h01 in range(2):
                    for qb in range(NQB):
                        slot = h01 * 4 + qb
                        ve.tensor_scalar(biastx[:, slot * 64: slot * 64 + UB[qb]],
                                         ktpen_s[:, qb * 64: qb * 64 + UB[qb]],
                                         biast[:, slot:slot + 1], None, OP.add)
                for qb in range(NQB):
                    ve.tensor_scalar(mrun[:, qb * QB:(qb + 1) * QB], colmap_s[:, :],
                                     q0s_s[:, qb:qb + 1], None, OP.add)
                ve.engine_nop().then_inc(s2.h, 1)
                s2.v += 4
                sv.v += 3

        with nc.Block() as blk:
            b1, b2, b3, b4 = s1.v, s2.v, s3.v, s4.v
            tiles = []
            for qb in range(NQB):
                for kt in range(UB[qb]):
                    for h01 in range(2):
                        tiles.append((qb, kt, h01, MB[qb][0] <= kt < MB[qb][1]))
            ntl = len(tiles)
            pe_after_B = [0] * ntl
            pe_after_sum = [0] * ntl
            act_after = [0] * ntl
            pool_after = [0] * ntl
            epi_sum_free = {}
            epi_ctx_free = {}
            epi_out_free = {}
            pe_c = act_c = pool_c = dve_c = 0
            for t, (qb, kt, h01, mskd) in enumerate(tiles):
                act_c += 1
                act_after[t] = act_c
                if mskd:
                    pool_c += 1
                    pool_after[t] = pool_c
                if t + 1 == ntl or tiles[t + 1][0] != qb:
                    epi_sum_free[qb] = dve_c + 1
                    epi_ctx_free[qb] = (dve_c + 4, dve_c + 5)
                    epi_out_free[qb] = dve_c + 6
                    dve_c += 6
            for step in range(ntl + 1):
                if step < ntl:
                    pe_c += 1
                    pe_after_B[step] = pe_c
                u = step - 1 if step < ntl else ntl - 1
                if u >= 0:
                    pe_c += 2
                    pe_after_sum[u] = pe_c
                    if u + 1 == ntl or tiles[u + 1][0] != tiles[u][0]:
                        pe_c += 4

            @blk.tensor
            def _(te):
                pe_n = [0]

                def emit_pv(u):
                    qb, kt, h01, mskd = tiles[u]
                    if mskd:
                        te.wait_ge(s4.h, b4 + pool_after[u])
                    else:
                        te.wait_ge(s3.h, b3 + act_after[u])
                    if kt == 0 and qb > 0:
                        te.wait_ge(s2.h, b2 + epi_ctx_free[qb - 1][h01])
                    rhs = ptm[u % 2] if mskd else pt[u % 2]
                    te.matmul(pb[h01][:, :], ckv_tm[:, kt * 128:(kt + 1) * 128], rhs,
                              start=(kt == 0), stop=(kt == UB[qb] - 1)).then_inc(s1.h, 1)
                    if kt == 0 and qb > 0 and h01 == 0:
                        te.wait_ge(s2.h, b2 + epi_sum_free[qb - 1])
                    te.matmul(pb[2][0:1, :] if h01 == 0 else pb[2][32:33, :],
                              oc_b[:, 0:1], rhs,
                              start=(kt == 0), stop=(kt == UB[qb] - 1),
                              tile_position=(0, 32) if h01 == 1 else None).then_inc(s1.h, 1)
                    pe_n[0] += 2
                    if u + 1 == ntl or tiles[u + 1][0] != qb:
                        te.wait_ge(s2.h, b2 + epi_sum_free[qb] + 0)
                        te.matmul(pb[3][:, :], or_f[0:1, :], recipf[0:1, 0:QB],
                                  start=True, stop=True).then_inc(s1.h, 1)
                        te.wait_ge(s2.h, b2 + epi_sum_free[qb] + 1)
                        te.matmul(pb[3][:, :], or_f[0:1, :], recipf[0:1, QB:2 * QB],
                                  start=True, stop=True).then_inc(s1.h, 1)
                        te.wait_ge(s2.h, b2 + epi_ctx_free[qb][0])
                        te.matmul(pb[6][0:64, :], v1_b[:, g0 * 64:(g0 + 1) * 64], ctxe[0],
                                  start=True, stop=True).then_inc(s1.h, 1)
                        te.wait_ge(s2.h, b2 + epi_ctx_free[qb][1])
                        te.matmul(pb[6][64:128, :], v1_b[:, g1 * 64:(g1 + 1) * 64], ctxe[1],
                                  start=True, stop=True,
                                  tile_position=(0, 64)).then_inc(s1.h, 1)
                        pe_n[0] += 4

                for t, (qb, kt, h01, mskd) in enumerate(tiles):
                    if t >= 2:
                        te.wait_ge(s3.h, b3 + act_after[t - 2])
                    te.matmul(pb[4 + t % 2][:, :],
                              ckv_fm[:, kt * 128:(kt + 1) * 128],
                              qlh[:, h01 * NTOK + qb * QB: h01 * NTOK + (qb + 1) * QB],
                              start=True, stop=True).then_inc(s1.h, 1)
                    pe_n[0] += 1
                    if t >= 1:
                        emit_pv(t - 1)
                emit_pv(ntl - 1)
                s1.v += pe_n[0]

            @blk.scalar
            def _(sc):
                for t, (qb, kt, h01, mskd) in enumerate(tiles):
                    slot = h01 * 4 + qb
                    sc.wait_ge(s1.h, b1 + pe_after_B[t])
                    sc.activation(pt[t % 2], pb[4 + t % 2][:, :], AF.Exp,
                                  bias=biastx[:, slot * 64 + kt: slot * 64 + kt + 1],
                                  scale=0.125).then_inc(s3.h, 1)
                s3.v += ntl

            @blk.vector
            def _(ve):
                for t, (qb, kt, h01, mskd) in enumerate(tiles):
                    if mskd:
                        ve.wait_ge(s3.h, b3 + act_after[t])
                        ve.scalar_tensor_tensor(ptm[t % 2], mrun[:, qb * QB:(qb + 1) * QB],
                                                rowmap_s[:, kt:kt + 1], pt[t % 2],
                                                OP.is_ge, OP.mult).then_inc(s4.h, 1)
                    if t + 1 == ntl or tiles[t + 1][0] != qb:
                        ve.wait_ge(s1.h, b1 + pe_after_sum[t])
                        ve.reciprocal(recipf[0:1, 0:QB], pb[2][0:1, :])
                        ve.reciprocal(recipf[0:1, QB:2 * QB], pb[2][32:33, :]).then_inc(sv.h, 1)
                        ve.wait_ge(sv.h, sv.v + 1)
                        sv.v += 1
                        ve.engine_nop().then_inc(s2.h, 1)
                        mm_base = b1 + pe_after_sum[t]
                        ve.wait_ge(s1.h, mm_base + 1)
                        ve.tensor_copy(rects.tensor if False else rects[0:128, 0:QB], pb[3][:, :]).then_inc(s2.h, 1)
                        ve.wait_ge(s1.h, mm_base + 2)
                        ve.tensor_copy(rects[0:128, QB:2 * QB], pb[3][:, :]).then_inc(s2.h, 1)
                        ve.tensor_tensor(ctxe[0], pb[0][:, :], rects[0:128, 0:QB], OP.mult).then_inc(s2.h, 1)
                        ve.tensor_tensor(ctxe[1], pb[1][:, :], rects[0:128, QB:2 * QB], OP.mult).then_inc(s2.h, 1)
                        ve.wait_ge(s1.h, mm_base + 4)
                        ve.tensor_tensor(xres[:, hp * NTOK + qb * QB: hp * NTOK + (qb + 1) * QB],
                                         pb[6][:, :],
                                         xres[:, hp * NTOK + qb * QB: hp * NTOK + (qb + 1) * QB],
                                         OP.add).then_inc(s2.h, 1)
                s2.v += dve_c
                s4.v += pool_c

    C.pop()
    Dd.pop()
    E.pop()

    if upto == "P5":
        tap_fm("xattn", xres[:, :], dtype=F32)
        emit_store_y()
        return nc, tap_tensors

    A.pop()
    A.push()
    xn2 = A.take(NCH * NTOK)
    emit_ln(xres[:, :], xn2)

    if upto == "P6":
        tap_fm("xn2", xn2, dtype=BF16)
        emit_store_y()
        return nc, tap_tensors

    Dd.push()
    stg2 = Dd.take(2 * 128)
    with nc.Block() as blk:
        b1, b2, bd = s1.v, s2.v, sd.v

        @blk.tensor
        def _(te):
            n = 0
            for i in range(16):
                for c in range(NCH):
                    if n >= 1:
                        te.wait_ge(s2.h, b2 + n)
                    te.transpose(pbh[:, (n % 2) * 128:(n % 2) * 128 + 128],
                                 xn2[:, c * NTOK + i * 128: c * NTOK + (i + 1) * 128],
                                 idb_s[:]).then_inc(s1.h, 1)
                    n += 1
            s1.v += 128

        @blk.vector
        def _(ve):
            n = 0
            for i in range(16):
                for c in range(NCH):
                    ve.wait_ge(s1.h, b1 + n + 1)
                    if n >= 2:
                        ve.wait_ge(sd.h, bd + 16 * (n - 1))
                    ve.tensor_copy(stg2[:, (n % 2) * 128:(n % 2) * 128 + 128],
                                   pbh[:, (n % 2) * 128:(n % 2) * 128 + 128]).then_inc(s2.h, 1)
                    n += 1
            s2.v += 128

        @blk.sync
        def _(sy):
            n = 0
            for i in range(16):
                for c in range(NCH):
                    sy.wait_ge(s2.h, b2 + n + 1)
                    sy.dma_start(out=xm_tm_dram.ap()[i * 128:(i + 1) * 128, c * 128:(c + 1) * 128],
                                 in_=stg2[:, (n % 2) * 128:(n % 2) * 128 + 128]).then_inc(sd.h, 16)
                    n += 1
            sd.v += 128 * 16
    Dd.pop()

    if upto == "P7a":
        emit_store_y()
        return nc, tap_tensors

    Bb.pop()
    Bb.push()
    sh_pre = Bb.take(4 * NTOK)
    sh = Bb.take(4 * NTOK)
    C.off = 0
    C.marks.clear()
    C.push()
    sw1T_b = C.take(NCH * MOE_H)
    sw2T_b = C.take(NCH * MOE_H)
    abb2 = C.take(2 * NTOK)
    sqb = C.take(512)
    t1s = C.take(512)
    emit_wload(sw1T, sw1T_b, NCH, MOE_H)
    emit_wload(sw2T, sw2T_b, NCH, MOE_H)
    sb12_s = nc.alloc_sbuf_tensor("sb12_s", [128, 8], F32)
    slng_s = nc.alloc_sbuf_tensor("slng_s", [128, 8], F32)
    with nc.Block() as blk:
        @blk.sync
        def _(sy):
            for hh in range(4):
                sy.dma_start(out=sb12_s[:, hh:hh + 1], in_=sb1.ap()[hh * 128:(hh + 1) * 128, :]).then_inc(sd.h, 16)
                sy.dma_start(out=sb12_s[:, 4 + hh:5 + hh], in_=sb2.ap()[hh * 128:(hh + 1) * 128, :]).then_inc(sd.h, 16)
                sy.dma_start(out=slng_s[:, hh:hh + 1], in_=s_ln_g.ap()[hh * 128:(hh + 1) * 128, :]).then_inc(sd.h, 16)
                sy.dma_start(out=slng_s[:, 4 + hh:5 + hh], in_=s_ln_b.ap()[hh * 128:(hh + 1) * 128, :]).then_inc(sd.h, 16)
            sd.v += 16 * 16
            sy.wait_ge(sd.h, sd.v)

    with nc.Block() as blk:
        b1, b2, b3 = s1.v, s2.v, s3.v

        @blk.tensor
        def _(te):
            n = 0
            for hh in range(4):
                for qb in range(NQB):
                    if n > 0:
                        te.wait_ge(s2.h, b2 + n)
                    for c in range(NCH):
                        te.matmul(pb[0][:, :], sw1T_b[:, c * MOE_H + hh * 128: c * MOE_H + (hh + 1) * 128],
                                  xn2[:, c * NTOK + qb * QB: c * NTOK + qb * QB + QB],
                                  start=(c == 0), stop=(c == NCH - 1)).then_inc(s1.h, 1)
                    for c in range(NCH):
                        te.matmul(pb[1][:, :], sw2T_b[:, c * MOE_H + hh * 128: c * MOE_H + (hh + 1) * 128],
                                  xn2[:, c * NTOK + qb * QB: c * NTOK + qb * QB + QB],
                                  start=(c == 0), stop=(c == NCH - 1)).then_inc(s1.h, 1)
                    n += 1
            s1.v += 16 * 16

        @blk.scalar
        def _(sc):
            n = 0
            for hh in range(4):
                for qb in range(NQB):
                    sc.wait_ge(s1.h, b1 + 16 * n + 8)
                    if n > 0:
                        sc.wait_ge(s2.h, b2 + n)
                    sc.activation(t1s[:, :], pb[0][:, :], AF.Silu,
                                  bias=sb12_s[:, hh:hh + 1]).then_inc(s3.h, 1)
                    n += 1
            s3.v += 16

        @blk.vector
        def _(ve):
            n = 0
            for hh in range(4):
                for qb in range(NQB):
                    ve.wait_ge(s1.h, b1 + 16 * (n + 1))
                    ve.wait_ge(s3.h, b3 + n + 1)
                    ve.scalar_tensor_tensor(sh_pre[:, hh * NTOK + qb * QB: hh * NTOK + qb * QB + QB],
                                            pb[1][:, :], sb12_s[:, 4 + hh:5 + hh], t1s[:, :],
                                            OP.add, OP.mult).then_inc(s2.h, 1)
                    n += 1
            s2.v += 16

    E.push()
    r0t = E.take(NTOK)
    r1t = E.take(NTOK)
    r2t = E.take(NTOK)
    sqq = E.take(512)
    sqq2 = E.take(512)
    sr0, sr1, sr2 = r0t[0:1, :], r1t[0:1, :], r2t[0:1, :]
    with nc.Block() as blk:
        b1, b2, b3 = s1.v, s2.v, s3.v

        @blk.scalar
        def _(sc):
            for qb in range(NQB):
                for hh in range(4):
                    i = qb * 4 + hh
                    if i > 0:
                        sc.wait_ge(s1.h, b1 + 2 * i)
                    sc.activation(sqb[:, :], sh_pre[:, hh * NTOK + qb * QB: hh * NTOK + qb * QB + QB],
                                  AF.Square).then_inc(s3.h, 1)
            s3.v += 16

        @blk.tensor
        def _(te):
            for qb in range(NQB):
                for hh in range(4):
                    i = qb * 4 + hh
                    if hh == 0 and qb > 0:
                        te.wait_ge(s2.h, b2 + 2 * qb)
                    te.matmul(pb[0][0:1, :], oc_b[:, 0:1],
                              sh_pre[:, hh * NTOK + qb * QB: hh * NTOK + qb * QB + QB],
                              start=(hh == 0), stop=(hh == 3)).then_inc(s1.h, 1)
                    te.wait_ge(s3.h, b3 + i + 1)
                    te.matmul(pb[1][0:1, :], oc_b[:, 0:1], sqb[:, :],
                              start=(hh == 0), stop=(hh == 3)).then_inc(s1.h, 1)
            s1.v += 32

        @blk.vector
        def _(ve):
            for qb in range(NQB):
                ve.wait_ge(s1.h, b1 + 8 * (qb + 1))
                ve.tensor_copy(sr0[0:1, qb * QB:(qb + 1) * QB], pb[0][0:1, :]).then_inc(s2.h, 1)
                ve.tensor_copy(sr1[0:1, qb * QB:(qb + 1) * QB], pb[1][0:1, :]).then_inc(s2.h, 1)
            s2.v += 8

    ddh = float(MOE_H)
    with nc.Block() as blk:
        @blk.vector
        def _(ve):
            ve.tensor_scalar(sr2, sr1, 1.0 / ddh, float(EPS), OP.mult, OP.add)
    with nc.Block() as blk:
        @blk.vector
        def _(ve):
            ve.tensor_scalar(sr1, sr0, 1.0 / ddh, None, OP.mult)
    with nc.Block() as blk:
        @blk.vector
        def _(ve):
            ve.tensor_tensor(sr0, sr1, sr1, OP.mult)
    with nc.Block() as blk:
        @blk.vector
        def _(ve):
            ve.tensor_tensor(sr0, sr2, sr0, OP.subtract)
    with nc.Block() as blk:
        @blk.scalar
        def _(sc):
            sc.activation(sr2, sr0, AF.Sqrt)
    with nc.Block() as blk:
        @blk.vector
        def _(ve):
            ve.reciprocal(sr0, sr2)
    with nc.Block() as blk:
        @blk.vector
        def _(ve):
            ve.tensor_tensor(sr2, sr1, sr0, OP.mult)
    with nc.Block() as blk:
        @blk.vector
        def _(ve):
            ve.tensor_copy(abb2[0:1, 0:NTOK], sr0)
            ve.tensor_copy(abb2[0:1, NTOK:2 * NTOK], sr2)
    with nc.Block() as blk:
        b1, b2 = s1.v, s2.v

        @blk.tensor
        def _(te):
            for qb in range(NQB):
                if qb > 0:
                    te.wait_ge(s2.h, b2 + qb)
                te.matmul(pb[2][:, :], or_b[0:1, :], abb2[0:1, qb * QB:(qb + 1) * QB],
                          start=True, stop=True).then_inc(s1.h, 1)
                te.matmul(pb[3][:, :], or_b[0:1, :], abb2[0:1, NTOK + qb * QB: NTOK + (qb + 1) * QB],
                          start=True, stop=True).then_inc(s1.h, 1)
            s1.v += 8

        @blk.vector
        def _(ve):
            for qb in range(NQB):
                ve.wait_ge(s1.h, b1 + 2 * qb + 2)
                for hh in range(4):
                    sl = slice(hh * NTOK + qb * QB, hh * NTOK + qb * QB + QB)
                    ve.tensor_tensor(sqq[:, :], sh_pre[:, sl], pb[2][:, :], OP.mult)
                    ve.tensor_tensor(sqq2[:, :], sqq[:, :], pb[3][:, :], OP.subtract)
                    bi = ve.tensor_scalar(sh[:, sl], sqq2[:, :], slng_s[:, hh:hh + 1],
                                          slng_s[:, 4 + hh:5 + hh], OP.mult, OP.add)
                    if hh == 3:
                        bi.then_inc(s2.h, 1)
            s2.v += NQB
    E.pop()
    C.pop()

    if upto == "P7g":
        tap_fm("sh", sh, nch=4, width=NTOK, dtype=BF16)
        emit_store_y()
        return nc, tap_tensors

    lidx = Si.take(16)
    itmp = Si.take(16 * 5)
    ib_t = itmp[:, 0:16]
    it_t = itmp[:, 16:32]
    gs_t = itmp[:, 32:48]
    w_t = itmp[:, 48:64]
    e_t = itmp[:, 64:80]
    with nc.Block() as blk:
        @blk.sync
        def _(sy):
            for i in range(16):
                sy.dma_start(out=ib_t[:, i:i + 1], in_=bass.AP(idx_b, i * 128, [[1, 128], [1, 1]])).then_inc(sd.h, 16)
                sy.dma_start(out=it_t[:, i:i + 1], in_=bass.AP(idx_t, i * 128, [[1, 128], [1, 1]])).then_inc(sd.h, 16)
            sd.v += 32 * 16
            sy.wait_ge(sd.h, sd.v)

    def vstep(fn):
        with nc.Block() as blk:
            @blk.vector
            def _(ve):
                fn(ve)

    vstep(lambda ve: ve.tensor_scalar(gs_t, ib_t, 16, None, OP.mult))
    vstep(lambda ve: ve.tensor_scalar(w_t, it_t, 9, None, OP.arith_shift_right))
    vstep(lambda ve: ve.tensor_tensor(e_t, gs_t, w_t, OP.add))
    vstep(lambda ve: ve.tensor_scalar(w_t, w_t, 9, None, OP.arith_shift_left))
    vstep(lambda ve: ve.tensor_tensor(w_t, it_t, w_t, OP.subtract))
    fscr = E.take(16 * 4)
    gsf = fscr[:, 0:16]
    accf = fscr[:, 16:32]
    ef = fscr[:, 32:48]
    sumf = fscr[:, 48:64]
    vstep(lambda ve: ve.tensor_copy(gsf, e_t))
    vstep(lambda ve: ve.tensor_copy(accf, w_t))
    vstep(lambda ve: ve.tensor_scalar(sumf, accf, 0.0, None, OP.mult))
    for j in range(4):
        def fj(ve, j=j):
            ve.tensor_scalar(ef, gsf, metaf_s[:, j:j + 1], None, OP.is_equal)
        vstep(fj)
        def fj2(ve, j=j):
            ve.tensor_tensor(sumf, ef, sumf, OP.add)
        vstep(fj2)
        if j > 0:
            def fj3(ve, j=j):
                ve.tensor_scalar(ef, ef, float(j * 512), None, OP.mult)
            vstep(fj3)
            def fj4(ve, j=j):
                ve.tensor_tensor(accf, ef, accf, OP.add)
            vstep(fj4)
    vstep(lambda ve: ve.tensor_scalar(ef, sumf, -float(1 << 20), float(1 << 20), OP.mult, OP.add))
    vstep(lambda ve: ve.tensor_tensor(accf, ef, accf, OP.add))
    vstep(lambda ve: ve.tensor_copy(lidx, accf))
    lidxc = Si.take(16)
    vstep(lambda ve: ve.tensor_scalar(lidxc, lidx, NTOK, None, OP.min))

    Dd.push()
    gbuf = Dd.take(1024)
    gbuf2 = Dd.take(1024)
    with nc.Block() as blk:
        b2, bg, bd = s2.v, sg.v, sd.v

        @blk.gpsimd
        def _(gp):
            for i in range(16):
                if i > 0:
                    gp.wait_ge(s2.h, b2 + i)
                gp.indirect_dma_start(
                    gbuf[:, :], None,
                    xm_tm_dram.ap(), IndirectOffsetOnAxis(lidxc[:, i:i + 1], 0),
                ).then_inc(sg.h, 16)
                sg.v += 16
                gp.wait_ge(sg.h, sg.v)

        @blk.vector
        def _(ve):
            for i in range(16):
                ve.wait_ge(sg.h, bg + 16 * (i + 1))
                if i > 0:
                    ve.wait_ge(sd.h, bd + 16 * i)
                ve.tensor_scalar(gbuf2[:, :], gbuf[:, :], sumf[:, i:i + 1],
                                 None, OP.mult).then_inc(s2.h, 1)
            s2.v += 16

        @blk.sync
        def _(sy):
            for i in range(16):
                sy.wait_ge(s2.h, b2 + i + 1)
                sy.dma_start(out=xmoe_contrib.ap()[i * 128:(i + 1) * 128, :],
                             in_=gbuf2[:, :]).then_inc(sd.h, 16)
            sd.v += 16 * 16
            sy.wait_ge(sd.h, sd.v)
    Dd.pop()

    groups8 = [[0, 1, 2, 3, 4, 5, 6, 7]]
    with nc.Block() as blk:
        @blk.gpsimd
        def _(gp):
            gp.collective_compute("AllReduce", OP.add, replica_groups=groups8,
                                  ins=[xmoe_contrib.ap()], outs=[xmoe_all.ap()]).then_inc(sco.h, 1)
            sco.v += 1
            gp.wait_ge(sco.h, sco.v)

    if upto == "P7c":
        tap_copy_dram("xmoe", xmoe_all, [MOE_M, D], BF16)
        emit_store_y()
        return nc, tap_tensors

    xm_fm = C.take(NCH * MOE_M)
    Dd.push()
    stg3 = Dd.take(2 * 1024)
    with nc.Block() as blk:
        b1, b2, bd = s1.v, s2.v, sd.v

        @blk.sync
        def _(sy):
            for i in range(16):
                if i >= 2:
                    sy.wait_ge(s1.h, b1 + 8 * (i - 1))
                sy.dma_start(out=stg3[:, (i % 2) * 1024:(i % 2) * 1024 + 1024],
                             in_=xmoe_all.ap()[i * 128:(i + 1) * 128, :]).then_inc(sd.h, 16)
            sd.v += 16 * 16

        @blk.tensor
        def _(te):
            n = 0
            for i in range(16):
                te.wait_ge(sd.h, bd + 16 * (i + 1))
                for c in range(NCH):
                    if n >= 1:
                        te.wait_ge(s2.h, b2 + n)
                    te.transpose(pbh[:, (n % 2) * 128:(n % 2) * 128 + 128],
                                 stg3[:, (i % 2) * 1024 + c * 128: (i % 2) * 1024 + (c + 1) * 128],
                                 idb_s[:]).then_inc(s1.h, 1)
                    n += 1
            s1.v += 128

        @blk.vector
        def _(ve):
            n = 0
            for i in range(16):
                for c in range(NCH):
                    ve.wait_ge(s1.h, b1 + n + 1)
                    ve.tensor_copy(xm_fm[:, c * MOE_M + i * 128: c * MOE_M + (i + 1) * 128],
                                   pbh[:, (n % 2) * 128:(n % 2) * 128 + 128]).then_inc(s2.h, 1)
                    n += 1
            s2.v += 128
    Dd.pop()

    A.pop()
    A.push()
    vb8t = A.take(MOE_M)
    gw_b = A.take(NCH * 16)
    ebrow_b = A.take(2 * MOE_H + D)
    sw3T_b = A.take(4 * D)
    ew3T_b = A.take(4 * D)
    E.push()
    sigf = E.take(MOE_M)
    gmisc = E.take(128)
    swv_t = E.take(16)
    ftile_t = E.take(16)
    dbuf = E.take(1024)
    selg_b = E.take(512)
    selb_b = E.take(512)
    erow_stg = E.take(1024)
    scb = E.take(512)
    sb3_s = nc.alloc_sbuf_tensor("sb3_s", [128, 8], F32)
    gb_s = nc.alloc_sbuf_tensor("gb_s", [16, 1], F32)
    jcols_s = nc.alloc_sbuf_tensor("jcols_s", [128, 16], F32)
    bigc_s = nc.alloc_sbuf_tensor("bigc_s", [128, 1], F32)
    invx = nc.alloc_sbuf_tensor("invx", [128, 16], I32)
    invf = nc.alloc_sbuf_tensor("invf", [128, 16], F32)
    expb_s = nc.alloc_sbuf_tensor("expb_s", [8, 1], F32)

    emit_wload(gwT, gw_b, NCH, 16)
    emit_wload(sw3T, sw3T_b, 4, D)
    emit_wload(ew3T, ew3T_b, 4, D)
    ew1T_b = Bb.t[:, 0:NCH * MOE_H]
    ew2T_b = Bb.t[:, NCH * MOE_H:2 * NCH * MOE_H]
    emit_wload(ew1T, ew1T_b, NCH, MOE_H)
    emit_wload(ew2T, ew2T_b, NCH, MOE_H)

    with nc.Block() as blk:
        bd = sd.v

        @blk.sync
        def _(sy):
            sy.dma_start(out=gb_s[:], in_=gb.ap()).then_inc(sd.h, 16)
            sy.dma_start(out=expb_s[:], in_=exp_bias.ap()).then_inc(sd.h, 16)
            sy.dma_start(out=erow_stg[0:1, 0:512], in_=eb1.ap()).then_inc(sd.h, 16)
            sy.dma_start(out=erow_stg[0:1, 512:1024], in_=eb2.ap()).then_inc(sd.h, 16)
            sy.dma_start(out=dbuf[0:1, 0:1024], in_=eb3.ap()).then_inc(sd.h, 16)
            sy.dma_start(out=selg_b[0:1, 0:512], in_=sel_ln_g.ap()).then_inc(sd.h, 16)
            sy.dma_start(out=selb_b[0:1, 0:512], in_=sel_ln_b.ap()).then_inc(sd.h, 16)
            for oc in range(8):
                sy.dma_start(out=sb3_s[:, oc:oc + 1],
                             in_=sb3.ap()[oc * 128:(oc + 1) * 128, :]).then_inc(sd.h, 16)
            sd.v += 15 * 16
            sy.wait_ge(sd.h, sd.v)

        @blk.vector
        def _(ve):
            ve.wait_ge(sd.h, bd + 15 * 16)
            ve.tensor_copy(ebrow_b[0:1, 0:1024], erow_stg[0:1, 0:1024])
            ve.tensor_copy(ebrow_b[0:1, 1024:2048], dbuf[0:1, 0:1024])

    with nc.Block() as blk:
        b1 = s1.v

        @blk.tensor
        def _(te):
            te.matmul(pb[2][:, :], or_f[0:1, :], selg_b[0:1, 0:512], start=True, stop=True).then_inc(s1.h, 1)
            te.matmul(pb[3][:, :], or_f[0:1, :], selb_b[0:1, 0:512], start=True, stop=True).then_inc(s1.h, 1)
            s1.v += 2

        @blk.vector
        def _(ve):
            ve.wait_ge(s1.h, b1 + 2)
            ve.tensor_copy(selg_b[:, :], pb[2][:, :])
            ve.tensor_copy(selb_b[:, :], pb[3][:, :])

    with nc.Block() as blk:
        b1, b3 = s1.v, s3.v

        @blk.tensor
        def _(te):
            for qb in range(NQB):
                if qb >= 2:
                    te.wait_ge(s3.h, b3 + qb - 1)
                for c in range(NCH):
                    te.matmul(pb[qb % 2][0:16, :], gw_b[:, c * 16:(c + 1) * 16],
                              xm_fm[:, c * MOE_M + qb * QB: c * MOE_M + qb * QB + QB],
                              start=(c == 0), stop=(c == NCH - 1)).then_inc(s1.h, 1)
            s1.v += 32

        @blk.scalar
        def _(sc):
            for qb in range(NQB):
                sc.wait_ge(s1.h, b1 + 8 * (qb + 1))
                sc.activation(sigf[0:16, qb * QB:(qb + 1) * QB], pb[qb % 2][0:16, :],
                              AF.Sigmoid, bias=gb_s[0:16, 0:1]).then_inc(s3.h, 1)
            s3.v += 4

    vf = E.take(MOE_M)
    with nc.Block() as blk:
        @blk.vector
        def _(ve):
            ve.tensor_copy(vf[0:16, :], sigf[0:16, :]).then_inc(sv.h, 1)
            sv.v += 1
            ve.wait_ge(sv.h, sv.v)
            ve.tensor_copy(vf[32:40, :], sigf[0:8, :]).then_inc(sv.h, 1)
            sv.v += 1
            ve.wait_ge(sv.h, sv.v)
            ve.tensor_scalar(vf[0:8, :], sigf[0:8, :], expb_s[0:8, 0:1], None, OP.add)

    gt = gmisc
    with nc.Block() as blk:
        b1, b2, b3 = s1.v, s2.v, s3.v

        @blk.tensor
        def _(te):
            for i in range(16):
                if i >= 1:
                    te.wait_ge(s2.h, b2 + 3 * (i - 1) + 1)
                te.transpose(pb[5][:, 0:128],
                             vf[:, i * 128:(i + 1) * 128], idf_s[:]).then_inc(s1.h, 1)
            s1.v += 16

        @blk.vector
        def _(ve):
            for i in range(16):
                ve.wait_ge(s1.h, b1 + i + 1)
                ve.tensor_copy(gt[:, 0:48], pb[5][:, 0:48]).then_inc(s2.h, 1)
                ve.engine_nop().then_inc(sv.h, 1)
                ve.wait_ge(sv.h, sv.v + 1); sv.v += 1
                ve.reduce_max(gt[:, 48:49], gt[:, 0:8], axis=AX.X).then_inc(sv.h, 1)
                ve.wait_ge(sv.h, sv.v + 1); sv.v += 1
                ve.tensor_scalar(gt[:, 49:57], gt[:, 0:8], gt[:, 48:49], None, OP.is_equal).then_inc(sv.h, 1)
                ve.wait_ge(sv.h, sv.v + 1); sv.v += 1
                ve.tensor_tensor(gt[:, 57:65], gt[:, 49:57], gt[:, 32:40], OP.mult).then_inc(sv.h, 1)
                ve.wait_ge(sv.h, sv.v + 1); sv.v += 1
                ve.reduce_sum(gt[:, 65:66], gt[:, 57:65], axis=AX.X).then_inc(sv.h, 1)
                ve.wait_ge(sv.h, sv.v + 1); sv.v += 1
                ve.tensor_tensor(gt[:, 66:74], gt[:, 49:57], onehot_s[:, :], OP.mult).then_inc(sv.h, 1)
                ve.wait_ge(sv.h, sv.v + 1); sv.v += 1
                ve.reduce_sum(gt[:, 74:75], gt[:, 66:74], axis=AX.X).then_inc(sv.h, 1)
                ve.wait_ge(sv.h, sv.v + 1); sv.v += 1
                ve.tensor_tensor(gt[:, 75:76], gt[:, 8:9], gt[:, 65:66], OP.subtract).then_inc(s2.h, 1)
                ve.wait_ge(s3.h, b3 + 2 * (i + 1))
                ve.tensor_tensor(ftile_t[:, i:i + 1], gt[:, 76:77], gt[:, 74:75], OP.mult).then_inc(s2.h, 1)
            s2.v += 48

        @blk.scalar
        def _(sc):
            for i in range(16):
                sc.wait_ge(s2.h, b2 + 3 * i + 2)
                sc.activation(swv_t[:, i:i + 1], gt[:, 75:76], AF.Sigmoid).then_inc(s3.h, 1)
                sc.activation(gt[:, 76:77], gt[:, 75:76], AF.Sigmoid, scale=-1.0).then_inc(s3.h, 1)
            s3.v += 32

    if upto == "P7f":
        tap_fm("gating", swv_t, nch=1, width=16, dtype=F32)
        tap_fm("gatingf", ftile_t, nch=1, width=16, dtype=F32)
        emit_store_y()
        return nc, tap_tensors

    Dd.push()
    t4 = Dd.take(512)
    z1 = Dd.take(512)
    sqb4 = Dd.take(512)
    z2 = Dd.take(512)
    z3 = Dd.take(512)
    z4 = Dd.take(512)
    zTt = Dd.take(512)
    est = E.take(8)
    for i in range(16):
        with nc.Block() as blk:
            b1, b2, b3, bv = s1.v, s2.v, s3.v, sv.v

            @blk.tensor
            def _(te):
                for c in range(NCH):
                    te.matmul(pb[0][:, :], xm_fm[:, c * MOE_M + i * 128: c * MOE_M + (i + 1) * 128],
                              ew1T_b[:, c * MOE_H:(c + 1) * MOE_H],
                              start=(c == 0), stop=False).then_inc(s1.h, 1)
                te.matmul(pb[0][:, :], or_b[0:1, :], ebrow_b[0:1, 0:512],
                          start=False, stop=True).then_inc(s1.h, 1)
                for c in range(NCH):
                    te.matmul(pb[1][:, :], xm_fm[:, c * MOE_M + i * 128: c * MOE_M + (i + 1) * 128],
                              ew2T_b[:, c * MOE_H:(c + 1) * MOE_H],
                              start=(c == 0), stop=False).then_inc(s1.h, 1)
                te.matmul(pb[1][:, :], or_b[0:1, :], ebrow_b[0:1, 512:1024],
                          start=False, stop=True).then_inc(s1.h, 1)
                s1.v += 18
                for k in range(4):
                    te.wait_ge(s2.h, b2 + 11 + k)
                    te.transpose(pbh[:, (k % 2) * 128:(k % 2) * 128 + 128],
                                 z3[:, k * 128:(k + 1) * 128], idb_s[:]).then_inc(s1.h, 1)
                s1.v += 4
                te.wait_ge(s2.h, b2 + 15)
                for oh in range(2):
                    for k in range(4):
                        te.matmul(pb[2 + oh][:, :], zTt[:, k * 128:(k + 1) * 128],
                                  ew3T_b[:, k * D + oh * 512: k * D + oh * 512 + 512],
                                  start=(k == 0), stop=False).then_inc(s1.h, 1)
                    te.matmul(pb[2 + oh][:, :], or_b[0:1, :],
                              ebrow_b[0:1, 1024 + oh * 512: 1024 + (oh + 1) * 512],
                              start=False, stop=True).then_inc(s1.h, 1)
                s1.v += 10

            @blk.scalar
            def _(sc):
                sc.wait_ge(s1.h, b1 + 9)
                sc.activation(t4[:, :], pb[0][:, :], AF.Silu).then_inc(s3.h, 1)
                sc.wait_ge(s2.h, b2 + 1)
                sc.activation(sqb4[:, :], z1[:, :], AF.Square).then_inc(s3.h, 1)
                sc.wait_ge(s2.h, b2 + 7)
                sc.activation(est[:, 6:7], est[:, 5:6], AF.Sqrt).then_inc(s3.h, 1)
                s3.v += 3

            @blk.vector
            def _(ve):
                ve.wait_ge(s1.h, b1 + 18)
                ve.wait_ge(s3.h, b3 + 1)
                ve.tensor_tensor(z1[:, :], t4[:, :], pb[1][:, :], OP.mult).then_inc(s2.h, 1)
                ve.reduce_sum(est[:, 1:2], z1[:, :], axis=AX.X).then_inc(s2.h, 1)
                ve.wait_ge(s2.h, b2 + 2)
                ve.tensor_scalar(est[:, 2:3], est[:, 1:2], 1.0 / 512.0, None, OP.mult).then_inc(s2.h, 1)
                ve.wait_ge(s3.h, b3 + 2)
                ve.reduce_sum(est[:, 0:1], sqb4[:, :], axis=AX.X).then_inc(s2.h, 1)
                ve.wait_ge(s2.h, b2 + 4)
                ve.tensor_scalar(est[:, 3:4], est[:, 0:1], 1.0 / 512.0, float(EPS), OP.mult, OP.add).then_inc(s2.h, 1)
                ve.wait_ge(s2.h, b2 + 5)
                ve.tensor_tensor(est[:, 4:5], est[:, 2:3], est[:, 2:3], OP.mult).then_inc(s2.h, 1)
                ve.wait_ge(s2.h, b2 + 6)
                ve.tensor_tensor(est[:, 5:6], est[:, 3:4], est[:, 4:5], OP.subtract).then_inc(s2.h, 1)
                ve.wait_ge(s3.h, b3 + 3)
                ve.reciprocal(est[:, 7:8], est[:, 6:7]).then_inc(s2.h, 1)
                ve.wait_ge(s2.h, b2 + 8)
                ve.tensor_scalar(z2[:, :], z1[:, :], est[:, 2:3], est[:, 7:8],
                                 OP.subtract, OP.mult).then_inc(s2.h, 1)
                ve.wait_ge(s2.h, b2 + 9)
                ve.tensor_tensor(z4[:, :], z2[:, :], selg_b[:, :], OP.mult).then_inc(s2.h, 1)
                ve.wait_ge(s2.h, b2 + 10)
                ve.tensor_tensor(z3[:, :], z4[:, :], selb_b[:, :], OP.add).then_inc(s2.h, 1)
                for k in range(4):
                    ve.wait_ge(s1.h, b1 + 18 + k + 1)
                    ve.tensor_copy(zTt[:, k * 128:(k + 1) * 128],
                                   pbh[:, (k % 2) * 128:(k % 2) * 128 + 128]).then_inc(s2.h, 1)
                for oh in range(2):
                    ve.wait_ge(s1.h, b1 + 22 + 5 * (oh + 1))
                    ve.tensor_scalar(dbuf[:, oh * 512:(oh + 1) * 512], pb[2 + oh][:, :],
                                     ftile_t[:, i:i + 1], None, OP.mult).then_inc(s2.h, 1)
                s2.v += 17

            @blk.sync
            def _(sy):
                sy.wait_ge(s2.h, b2 + 17)
                sy.dma_start(out=delta_contrib.ap()[i * 128:(i + 1) * 128, :],
                             in_=dbuf[:, :]).then_inc(sd.h, 16)
                sd.v += 16
                sy.wait_ge(sd.h, sd.v)
    Dd.pop()

    with nc.Block() as blk:
        @blk.gpsimd
        def _(gp):
            gp.collective_compute("AllReduce", OP.add, replica_groups=groups8,
                                  ins=[delta_contrib.ap()], outs=[delta_all.ap()[0:MOE_M, :]]).then_inc(sco.h, 1)
            sco.v += 1
            gp.wait_ge(sco.h, sco.v)

    if upto == "P7h":
        tap_copy_dram("delta", delta_all, [MOE_M, D], F32)
        emit_store_y()
        return nc, tap_tensors

    with nc.Block() as blk:
        @blk.vector
        def _(ve):
            ve.memset(dbuf[:, :], 0.0)

        @blk.sync
        def _(sy):
            for k in range(16):
                sy.dma_start(out=own_scale.ap()[k * 128:(k + 1) * 128, :],
                             in_=oc_f[:, 0:1]).then_inc(sd.h, 16)
            sd.v += 16 * 16
            sy.wait_ge(sd.h, sd.v)
    with nc.Block() as blk:
        @blk.sync
        def _(sy):
            sy.dma_start(out=jcols_s[:], in_=jcols0.ap()).then_inc(sd.h, 16)
            sy.dma_start(out=bigc_s[:], in_=bigcol0.ap()).then_inc(sd.h, 16)
            sd.v += 2 * 16
            sy.wait_ge(sd.h, sd.v)
            for k in range(17):
                sy.dma_start(out=own_inv.ap()[k * 128:(k + 1) * 128, :],
                             in_=bigc_s[:, :]).then_inc(sd.h, 16)
            sd.v += 17 * 16
            sy.wait_ge(sd.h, sd.v)

    with nc.Block() as blk:
        @blk.gpsimd
        def _(gp):
            for i in range(16):
                gp.indirect_dma_start(
                    own_scale.ap(), IndirectOffsetOnAxis(lidxc[:, i:i + 1], 0),
                    swv_t[:, i:i + 1], None,
                ).then_inc(sg.h, 16)
                sg.v += 16
                gp.wait_ge(sg.h, sg.v)

    jvals_e = fscr[:, 0:16]
    with nc.Block() as blk:
        @blk.vector
        def _(ve):
            ve.tensor_copy(jvals_e, jcols_s[:, :])

    with nc.Block() as blk:
        @blk.gpsimd
        def _(gp):
            for i in range(16):
                gp.indirect_dma_start(
                    own_inv.ap(), IndirectOffsetOnAxis(lidxc[:, i:i + 1], 0),
                    jvals_e[:, i:i + 1], None,
                ).then_inc(sg.h, 16)
                sg.v += 16
                gp.wait_ge(sg.h, sg.v)

    with nc.Block() as blk:
        @blk.sync
        def _(sy):
            for i in range(16):
                sy.dma_start(out=invf[:, i:i + 1],
                             in_=own_inv.ap()[i * 128:(i + 1) * 128, :]).then_inc(sd.h, 16)
            sd.v += 16 * 16
            sy.wait_ge(sd.h, sd.v)



    dmask = fscr[:, 32:48]
    dbuf2v = vf[:, 0:1024]
    with nc.Block() as blk:
        @blk.vector
        def _(ve):
            ve.tensor_scalar(invx[:, :], invf[:, :], float(MOE_M), None, OP.min).then_inc(sv.h, 1)
            sv.v += 1
            ve.wait_ge(sv.h, sv.v)
            ve.tensor_scalar(dmask[:, :], invf[:, :], float(MOE_M) - 0.5, None, OP.is_lt)

    with nc.Block() as blk:
        b2g, bgg, bdg = s2.v, sg.v, sd.v

        @blk.gpsimd
        def _(gp):
            for i in range(16):
                if i > 0:
                    gp.wait_ge(s2.h, b2g + i)
                gp.indirect_dma_start(
                    dbuf[:, :], None,
                    delta_all.ap(), IndirectOffsetOnAxis(invx[:, i:i + 1], 0),
                ).then_inc(sg.h, 16)
                sg.v += 16
                gp.wait_ge(sg.h, sg.v)

        @blk.vector
        def _(ve):
            for i in range(16):
                ve.wait_ge(sg.h, bgg + 16 * (i + 1))
                if i > 0:
                    ve.wait_ge(sd.h, bdg + 16 * i)
                ve.tensor_scalar(dbuf2v[:, :], dbuf[:, :], dmask[:, i:i + 1],
                                 None, OP.mult).then_inc(s2.h, 1)
            s2.v += 16

        @blk.sync
        def _(sy):
            for i in range(16):
                sy.wait_ge(s2.h, b2g + i + 1)
                sy.dma_start(out=own_delta.ap()[i * 128:(i + 1) * 128, :],
                             in_=dbuf2v[:, :]).then_inc(sd.h, 16)
            sd.v += 16 * 16
            sy.wait_ge(sd.h, sd.v)

    with nc.Block() as blk:
        @blk.sync
        def _(sy):
            sy.dma_start(out=sigf[0:1, 0:NTOK],
                         in_=bass.AP(own_scale, 0, [[NTOK, 1], [1, NTOK]])).then_inc(sd.h, 16)
            sd.v += 16
            sy.wait_ge(sd.h, sd.v)

    if upto == "P7i2":
        tap_fm("gating", swv_t, nch=1, width=16, dtype=F32)
        tap_fm("gatingf", ftile_t, nch=1, width=16, dtype=F32)
        tap_fm("sigf", sigf, nch=1, width=MOE_M, dtype=F32)
        tap_fm("gwb", gw_b, nch=1, width=NCH * 16, dtype=BF16)
        tap_fm("vf", vf, nch=1, width=MOE_M, dtype=F32)
        tap_fm("gmisc", gmisc, nch=1, width=128, dtype=F32)
        tap_copy_dram("oscale", own_scale, [NTOK + 128, 1], F32)
        tap_copy_dram("odelta", own_delta, [NTOK, D], F32)
        tap_copy_dram("dall", delta_all, [MOE_M + 128, D], F32)
        tap_copy_dram("oinv", own_inv, [NTOK + 128, 1], F32)
        tap_copy_dram("dcon", delta_contrib, [MOE_M, D], F32)
        emit_store_y()
        return nc, tap_tensors

    with nc.Block() as blk:
        b1, b2, b3 = s1.v, s2.v, s3.v

        @blk.tensor
        def _(te):
            n = 0
            for qb in range(NQB):
                if qb > 0:
                    te.wait_ge(s2.h, b2 + 9 * qb)
                te.matmul(pb[4][:, :], or_f[0:1, :], sigf[0:1, qb * QB:(qb + 1) * QB],
                          start=True, stop=True).then_inc(s1.h, 1)
                s1.v += 1
                for oc in range(NCH):
                    if oc >= 2:
                        te.wait_ge(s2.h, b2 + 9 * qb + oc)
                    for hh in range(4):
                        te.matmul(pb[oc % 2][:, :],
                                  sw3T_b[:, hh * D + oc * 128: hh * D + (oc + 1) * 128],
                                  sh[:, hh * NTOK + qb * QB: hh * NTOK + qb * QB + QB],
                                  start=(hh == 0), stop=(hh == 3)).then_inc(s1.h, 1)
                    s1.v += 4

        @blk.vector
        def _(ve):
            n1 = b1
            for qb in range(NQB):
                ve.wait_ge(s1.h, n1 + 1)
                ve.tensor_copy(scb[:, :], pb[4][:, :]).then_inc(s2.h, 1)
                n1 += 1
                for oc in range(NCH):
                    ve.wait_ge(s1.h, n1 + 4)
                    ve.scalar_tensor_tensor(dbuf[:, 0:512], pb[oc % 2][:, :],
                                            sb3_s[:, oc:oc + 1], scb[:, :],
                                            OP.add, OP.mult).then_inc(sv.h, 1)
                    sv.v += 1
                    ve.wait_ge(sv.h, sv.v)
                    ve.tensor_tensor(xres[:, oc * NTOK + qb * QB: oc * NTOK + qb * QB + QB],
                                     dbuf[:, 0:512],
                                     xres[:, oc * NTOK + qb * QB: oc * NTOK + qb * QB + QB],
                                     OP.add).then_inc(s2.h, 1)
                    n1 += 4
            s2.v += NQB * 9

    if upto == "P7i3":
        tap_fm("xcomb", xres[:, :], dtype=F32)
        tap_copy_dram("oscale", own_scale, [NTOK, 1], F32)
        emit_store_y()
        return nc, tap_tensors

    with nc.Block() as blk:
        b1, b2, bd = s1.v, s2.v, sd.v

        @blk.sync
        def _(sy):
            n = 0
            for qb in range(NQB):
                for oc in range(NCH):
                    for k in range(4):
                        if n >= 2:
                            sy.wait_ge(s1.h, b1 + n - 1)
                        sy.dma_start(out=dbuf[:, (n % 2) * 512:(n % 2) * 512 + 128],
                                     in_=own_delta.ap()[qb * QB + k * 128: qb * QB + (k + 1) * 128,
                                                        oc * 128:(oc + 1) * 128]).then_inc(sd.h, 16)
                        n += 1
            sd.v += 128 * 16

        @blk.tensor
        def _(te):
            n = 0
            for qb in range(NQB):
                for oc in range(NCH):
                    for k in range(4):
                        te.wait_ge(sd.h, bd + 16 * (n + 1))
                        if n >= 1:
                            te.wait_ge(s2.h, b2 + n)
                        te.transpose(pb[5][:, 0:128],
                                     dbuf[:, (n % 2) * 512:(n % 2) * 512 + 128], idf_s[:]).then_inc(s1.h, 1)
                        n += 1
            s1.v += 128

        @blk.vector
        def _(ve):
            n = 0
            for qb in range(NQB):
                for oc in range(NCH):
                    for k in range(4):
                        ve.wait_ge(s1.h, b1 + n + 1)
                        cols = slice(oc * NTOK + qb * QB + k * 128, oc * NTOK + qb * QB + (k + 1) * 128)
                        ve.tensor_tensor(xres[:, cols], pb[5][:, 0:128],
                                         xres[:, cols], OP.add).then_inc(s2.h, 1)
                        n += 1
            s2.v += 128
    E.pop()
    Dd.pop() if False else None
    A.pop()
    Bb.pop()
    C.pop() if False else None
    C.off = 0
    C.marks.clear()

    if upto == "P7":
        tap_fm("xffn", xres[:, :], dtype=F32)
        tap_copy_dram("odelta", own_delta, [NTOK, D], F32)
        tap_copy_dram("oscale", own_scale, [NTOK + 128, 1], F32)
        emit_store_y()
        return nc, tap_tensors

    E.off = 0
    E.marks.clear()

    A.push()
    xn3 = A.take(NCH * NTOK)
    emit_ln(xres[:, :], xn3)

    C.off = 0
    C.marks.clear()
    wqkv_b = C.take(NCH * D)
    emit_wload(w_qkvT, wqkv_b, NCH, D)
    Bb.push()
    qkv_own = Bb.take(NCH * NTOK)
    with nc.Block() as blk:
        b1, b2 = s1.v, s2.v

        @blk.tensor
        def _(te):
            n = 0
            for oc in range(NCH):
                for qb in range(NQB):
                    if n >= 2:
                        te.wait_ge(s2.h, b2 + n - 1)
                    for c in range(NCH):
                        te.matmul(pb[n % 2][:, :], wqkv_b[:, c * D + oc * 128: c * D + (oc + 1) * 128],
                                  xn3[:, c * NTOK + qb * QB: c * NTOK + qb * QB + QB],
                                  start=(c == 0), stop=(c == NCH - 1)).then_inc(s1.h, 1)
                    n += 1
            s1.v += 32 * NCH

        @blk.vector
        def _(ve):
            n = 0
            for oc in range(NCH):
                for qb in range(NQB):
                    ve.wait_ge(s1.h, b1 + NCH * (n + 1))
                    ve.tensor_copy(qkv_own[:, oc * NTOK + qb * QB: oc * NTOK + qb * QB + QB],
                                   pb[n % 2][:, :]).then_inc(s2.h, 1)
                    n += 1
            s2.v += 32

    Dd.push()
    stg4 = Dd.take(2 * 128)
    with nc.Block() as blk:
        b1, b2, bd = s1.v, s2.v, sd.v

        @blk.tensor
        def _(te):
            n = 0
            for i in range(16):
                for c in range(NCH):
                    if n >= 1:
                        te.wait_ge(s2.h, b2 + n)
                    te.transpose(pbh[:, (n % 2) * 128:(n % 2) * 128 + 128],
                                 qkv_own[:, c * NTOK + i * 128: c * NTOK + (i + 1) * 128],
                                 idb_s[:]).then_inc(s1.h, 1)
                    n += 1
            s1.v += 128

        @blk.vector
        def _(ve):
            n = 0
            for i in range(16):
                for c in range(NCH):
                    ve.wait_ge(s1.h, b1 + n + 1)
                    if n >= 2:
                        ve.wait_ge(sd.h, bd + 16 * (n - 1))
                    ve.tensor_copy(stg4[:, (n % 2) * 128:(n % 2) * 128 + 128],
                                   pbh[:, (n % 2) * 128:(n % 2) * 128 + 128]).then_inc(s2.h, 1)
                    n += 1
            s2.v += 128

        @blk.sync
        def _(sy):
            n = 0
            for i in range(16):
                for c in range(NCH):
                    sy.wait_ge(s2.h, b2 + n + 1)
                    sy.dma_start(out=qkv_contrib.ap()[NTOK + i * 128: NTOK + (i + 1) * 128,
                                                      c * 128:(c + 1) * 128],
                                 in_=stg4[:, (n % 2) * 128:(n % 2) * 128 + 128]).then_inc(sd.h, 16)
                    n += 1
            sd.v += 128 * 16
    Dd.pop()
    with nc.Block() as blk:
        @blk.sync
        def _(sy):
            for c in range(NCH):
                fmv = bass.AP(qkv_contrib, c * 128 * NTOK, [[NTOK, 128], [1, NTOK]])
                sy.dma_start(out=fmv, in_=qkv_own[:, c * NTOK:(c + 1) * NTOK]).then_inc(sd.h, 16)
            sd.v += NCH * 16
            sy.wait_ge(sd.h, sd.v)

    with nc.Block() as blk:
        @blk.gpsimd
        def _(gp):
            gp.collective_compute("AllGather", OP.bypass, replica_groups=groups4,
                                  ins=[qkv_contrib.ap()], outs=[qkv_ag.ap()]).then_inc(sco.h, 1)
            sco.v += 1
            gp.wait_ge(sco.h, sco.v)

    if upto == "P8a":
        tap_copy_dram("qkvag", qkv_ag, [4 * 2 * NTOK, D], BF16)
        emit_store_y()
        return nc, tap_tensors

    A.pop()
    A.push()
    ctxbuf = A.take(NCH * NTOK)
    C.off = 0
    C.marks.clear()
    kb = C.take(8192)
    vb = C.take(8192)
    Dd.push()
    ptl = [Dd.take(QB), Dd.take(QB)]
    E.push()
    mrun2 = E.take(8 * QB)
    permax2 = E.take(8)
    negm82 = E.take(8)
    biast2 = E.take(8)
    recipf2 = E.take(1024)
    rects2 = E.take(512)
    scrb2 = Dd.take(128)
    negmrow2 = Dd.take(16)

    for hp in range(NH // 2):
        with nc.Block() as blk:
            @blk.sync
            def _(sy):
                for r in range(4):
                    fmv = bass.AP(qkv_ag, (r * 2 * NTOK) * D + hp * 128 * NTOK,
                                  [[NTOK, 128], [1, NTOK]])
                    sy.dma_start(out=kb[:, r * NTOK:(r + 1) * NTOK], in_=fmv).then_inc(sd.h, 16)
                    tmv = bass.AP(qkv_ag, (r * 2 * NTOK + NTOK) * D + hp * 128,
                                  [[D, 128], [128 * D, 16], [1, 128]])
                    dstv = bass.AP(vb.tensor, vb.offset + r * 16 * 128,
                                   [[vb.tensor.shape[1], 128], [128, 16], [1, 128]]) if False else None
                    sy.dma_start(out=bass.AP(kb.tensor, kb.offset, [[1, 1], [1, 1]]), in_=bass.AP(qkv_ag, 0, [[1, 1], [1, 1]])) if False else None
                    n0 = r * 16
                    for kt in range(16):
                        sy.dma_start(out=vb[:, (n0 + kt) * 128:(n0 + kt + 1) * 128],
                                     in_=qkv_ag.ap()[r * 2 * NTOK + NTOK + kt * 128:
                                                     r * 2 * NTOK + NTOK + (kt + 1) * 128,
                                                     hp * 128:(hp + 1) * 128]).then_inc(sd.h, 16)
                sd.v += 4 * 17 * 16
                sy.wait_ge(sd.h, sd.v)

        with nc.Block() as blk:
            b1, b2, bv = s1.v, s2.v, sv.v
            aktsL = list(range(0, 64, 2))
            ntile = 2 * 4 * len(aktsL)

            @blk.tensor
            def _(te):
                a = 0
                for qb in range(NQB):
                    for kt in aktsL:
                        for h01 in range(2):
                            if a >= 4:
                                te.wait_ge(s2.h, b2 + a - 3)
                            te.matmul(pb[a % 4][:, :],
                                      kb[h01 * 64:(h01 + 1) * 64, kt * 128:(kt + 1) * 128],
                                      qkv_own[h01 * 64:(h01 + 1) * 64,
                                              hp * NTOK + qb * QB: hp * NTOK + (qb + 1) * QB],
                                      start=True, stop=True).then_inc(s1.h, 1)
                            a += 1
                s1.v += ntile
                te.wait_ge(s2.h, b2 + ntile + 1)
                te.transpose(pbh[:, 0:128], scrb2[:, 0:128], idb_s[:]).then_inc(s1.h, 1)
                te.wait_ge(s2.h, b2 + ntile + 2)
                te.transpose(pbh[:, 128:256], scrb2[:, 0:128], idb_s[:]).then_inc(s1.h, 1)
                te.wait_ge(s2.h, b2 + ntile + 3)
                te.matmul(pb[4][:, 0:16], or_b[0:1, :], negmrow2[0:1, 0:16],
                          start=True, stop=True).then_inc(s1.h, 1)
                s1.v += 3

            @blk.vector
            def _(ve):
                a = 0
                for qb in range(NQB):
                    for kt in aktsL:
                        for h01 in range(2):
                            slot = h01 * 4 + qb
                            ve.wait_ge(s1.h, b1 + a + 1)
                            if kt == 0:
                                ve.tensor_copy(mrun2[:, slot * QB:(slot + 1) * QB],
                                               pb[a % 4][:, :]).then_inc(s2.h, 1)
                            else:
                                ve.tensor_tensor(mrun2[:, slot * QB:(slot + 1) * QB],
                                                 pb[a % 4][:, :],
                                                 mrun2[:, slot * QB:(slot + 1) * QB],
                                                 OP.max).then_inc(s2.h, 1)
                            a += 1
                s2.v += ntile
                for k in range(8):
                    bi = ve.reduce_max(permax2[:, k:k + 1], mrun2[:, k * QB:(k + 1) * QB], axis=AX.X)
                bi.then_inc(sv.h, 1)
                ve.wait_ge(sv.h, bv + 1)
                ve.tensor_scalar(scrb2[:, 0:8], permax2[:, 0:8], 0.125, None, OP.mult).then_inc(s2.h, 1)
                ve.wait_ge(s1.h, b1 + ntile + 1)
                ve.reduce_max(negm82[0:8, 0:1], pbh[0:8, 0:128], axis=AX.X, negate=True).then_inc(sv.h, 1)
                ve.wait_ge(sv.h, bv + 2)
                ve.tensor_copy(scrb2[0:8, 0:1], negm82[0:8, 0:1]).then_inc(s2.h, 1)
                ve.wait_ge(s1.h, b1 + ntile + 2)
                ve.tensor_copy(negmrow2[0:1, 0:16], pbh[0:1, 128:144]).then_inc(s2.h, 1)
                ve.wait_ge(s1.h, b1 + ntile + 3)
                ve.tensor_copy(biast2[:, 0:8], pb[4][:, 0:8])
                ve.engine_nop().then_inc(s2.h, 1)
                s2.v += 4
                sv.v += 2

        with nc.Block() as blk:
            b1, b2, b3 = s1.v, s2.v, s3.v
            tiles = []
            for qb in range(NQB):
                for kt in range(64):
                    for h01 in range(2):
                        tiles.append((qb, kt, h01))
            ntl = len(tiles)
            pe_after_B = [0] * ntl
            pe_after_sum = [0] * ntl
            act_after = [0] * ntl
            epi_sum_free = {}
            epi_ctx_free = {}
            pe_c = act_c = dve_c = 0
            for t, (qb, kt, h01) in enumerate(tiles):
                act_c += 1
                act_after[t] = act_c
                if t + 1 == ntl or tiles[t + 1][0] != qb:
                    epi_sum_free[qb] = dve_c + 1
                    epi_ctx_free[qb] = dve_c + 3
                    dve_c += 3
            for step in range(ntl + 1):
                if step < ntl:
                    pe_c += 1
                    pe_after_B[step] = pe_c
                u = step - 1 if step < ntl else ntl - 1
                if u >= 0:
                    pe_c += 2
                    pe_after_sum[u] = pe_c
                    if u + 1 == ntl or tiles[u + 1][0] != tiles[u][0]:
                        pe_c += 2

            @blk.tensor
            def _(te):
                def emit_pv(u):
                    qb, kt, h01 = tiles[u]
                    te.wait_ge(s3.h, b3 + act_after[u])
                    if kt == 0 and qb > 0:
                        te.wait_ge(s2.h, b2 + epi_ctx_free[qb - 1])
                    te.matmul(pb[0][h01 * 64:(h01 + 1) * 64, :],
                              vb[:, kt * 128 + h01 * 64: kt * 128 + h01 * 64 + 64],
                              ptl[u % 2],
                              start=(kt == 0), stop=(kt == 63),
                              tile_position=(0, 64) if h01 == 1 else None).then_inc(s1.h, 1)
                    if kt == 0 and qb > 0 and h01 == 0:
                        te.wait_ge(s2.h, b2 + epi_sum_free[qb - 1])
                    te.matmul(pb[2][0:1, :] if h01 == 0 else pb[2][32:33, :],
                              oc_b[:, 0:1], ptl[u % 2],
                              start=(kt == 0), stop=(kt == 63),
                              tile_position=(0, 32) if h01 == 1 else None).then_inc(s1.h, 1)
                    if u + 1 == ntl or tiles[u + 1][0] != qb:
                        te.wait_ge(s2.h, b2 + epi_sum_free[qb])
                        te.matmul(pb[3][0:64, :], or_f[0:1, 0:64], recipf2[0:1, 0:QB],
                                  start=True, stop=True).then_inc(s1.h, 1)
                        te.matmul(pb[3][64:128, :], or_f[0:1, 0:64], recipf2[0:1, QB:2 * QB],
                                  start=True, stop=True,
                                  tile_position=(0, 64)).then_inc(s1.h, 1)

                for t, (qb, kt, h01) in enumerate(tiles):
                    if t >= 2:
                        te.wait_ge(s3.h, b3 + act_after[t - 2])
                    te.matmul(pb[4 + t % 2][:, :],
                              kb[h01 * 64:(h01 + 1) * 64, kt * 128:(kt + 1) * 128],
                              qkv_own[h01 * 64:(h01 + 1) * 64,
                                      hp * NTOK + qb * QB: hp * NTOK + (qb + 1) * QB],
                              start=True, stop=True).then_inc(s1.h, 1)
                    if t >= 1:
                        emit_pv(t - 1)
                emit_pv(ntl - 1)
                s1.v += pe_c

            @blk.scalar
            def _(sc):
                for t, (qb, kt, h01) in enumerate(tiles):
                    slot = h01 * 4 + qb
                    sc.wait_ge(s1.h, b1 + pe_after_B[t])
                    sc.activation(ptl[t % 2], pb[4 + t % 2][:, :], AF.Exp,
                                  bias=biast2[:, slot:slot + 1], scale=0.125).then_inc(s3.h, 1)
                s3.v += ntl

            @blk.vector
            def _(ve):
                for t, (qb, kt, h01) in enumerate(tiles):
                    if t + 1 == ntl or tiles[t + 1][0] != qb:
                        ve.wait_ge(s1.h, b1 + pe_after_sum[t])
                        ve.reciprocal(recipf2[0:1, 0:QB], pb[2][0:1, :])
                        ve.reciprocal(recipf2[0:1, QB:2 * QB], pb[2][32:33, :]).then_inc(sv.h, 1)
                        ve.wait_ge(sv.h, sv.v + 1)
                        sv.v += 1
                        ve.engine_nop().then_inc(s2.h, 1)
                        mm_base = b1 + pe_after_sum[t]
                        ve.wait_ge(s1.h, mm_base + 2)
                        ve.tensor_copy(rects2[0:128, 0:QB], pb[3][:, :]).then_inc(s2.h, 1)
                        ve.tensor_tensor(ctxbuf[:, hp * NTOK + qb * QB: hp * NTOK + (qb + 1) * QB],
                                         pb[0][:, :], rects2[0:128, 0:QB],
                                         OP.mult).then_inc(s2.h, 1)
                s2.v += dve_c
    Dd.pop()
    E.pop()

    if upto == "P8b":
        tap_fm("ctxb", ctxbuf, dtype=BF16)
        emit_store_y()
        return nc, tap_tensors

    C.off = 0
    C.marks.clear()
    wo_b = C.take(NCH * D)
    emit_wload(w_l1oT, wo_b, NCH, D)
    with nc.Block() as blk:
        b1, b2 = s1.v, s2.v

        @blk.tensor
        def _(te):
            n = 0
            for oc in range(NCH):
                for qb in range(NQB):
                    if n >= 2:
                        te.wait_ge(s2.h, b2 + n - 1)
                    for c in range(NCH):
                        te.matmul(pb[n % 2][:, :], wo_b[:, c * D + oc * 128: c * D + (oc + 1) * 128],
                                  ctxbuf[:, c * NTOK + qb * QB: c * NTOK + qb * QB + QB],
                                  start=(c == 0), stop=(c == NCH - 1)).then_inc(s1.h, 1)
                    n += 1
            s1.v += 32 * NCH

        @blk.vector
        def _(ve):
            n = 0
            for oc in range(NCH):
                for qb in range(NQB):
                    ve.wait_ge(s1.h, b1 + NCH * (n + 1))
                    sl = slice(oc * NTOK + qb * QB, oc * NTOK + qb * QB + QB)
                    ve.tensor_tensor(xres[:, sl], pb[n % 2][:, :], xres[:, sl],
                                     OP.add).then_inc(s2.h, 1)
                    n += 1
            s2.v += 32

    if upto == "P8":
        tap_fm("xl1a", xres[:, :], dtype=F32)
        emit_store_y()
        return nc, tap_tensors

    E.push()
    sstg = E.take(D)
    fstg = E.take(D)
    f2stg = E.take(D)
    Dd.push()
    wstg = Dd.take(D)
    with nc.Block() as blk:
        bd, bg2, b4 = sd.v, sg.v, s4.v

        @blk.sync
        def _(sy):
            n = 0
            for i in range(6):
                for c in range(NCH):
                    if n >= 1:
                        sy.wait_ge(s4.h, b4 + 2 * n)
                    sy.dma_start(out=sstg[:, :], in_=scT.ap()[i, c * 128:(c + 1) * 128, :]).then_inc(sd.h, 16)
                    n += 1
            sd.v += 48 * 16

        @blk.gpsimd
        def _(gp):
            n = 0
            for i in range(6):
                for c in range(NCH):
                    if n >= 1:
                        gp.wait_ge(s4.h, b4 + 2 * n - 1)
                    gp.dma_start(out=fstg[:, :], in_=qwT.ap()[i, c * 128:(c + 1) * 128, :]).then_inc(sg.h, 16)
                    gp.wait_ge(sg.h, bg2 + 16 * (2 * n + 1))
                    gp.wait_ge(sd.h, bd + 16 * (n + 1))
                    gp.tensor_scalar(f2stg[:, :], fstg[:, :], 1.0 / 7.5, -1.0,
                                     OP.mult, OP.add).then_inc(s4.h, 1)
                    gp.tensor_tensor(wstg[:, :], f2stg[:, :], sstg[:, :], OP.mult).then_inc(s4.h, 1)
                    gp.dma_start(out=wq_dram.ap()[i, c * 128:(c + 1) * 128, :],
                                 in_=wstg[:, :]).then_inc(sg.h, 16)
                    gp.wait_ge(sg.h, bg2 + 16 * (2 * n + 2))
                    n += 1
            s4.v += 96
            sg.v += 96 * 16
    Dd.pop()
    E.pop()

    if upto == "P9a":
        tap_copy_dram("wq", wq_dram, [6, D, D], BF16)
        emit_store_y()
        return nc, tap_tensors

    l1b_s = nc.alloc_sbuf_tensor("l1b_s", [128, 48], F32)
    lng_s = nc.alloc_sbuf_tensor("lng_s", [128, 24], F32)
    lnb_s = nc.alloc_sbuf_tensor("lnb_s", [128, 24], F32)
    with nc.Block() as blk:
        @blk.sync
        def _(sy):
            for i in range(6):
                for oc in range(NCH):
                    sy.dma_start(out=l1b_s[:, i * 8 + oc: i * 8 + oc + 1],
                                 in_=bass.AP(l1_biases, i * D + oc * 128, [[1, 128], [1, 1]])).then_inc(sd.h, 16)
            for blki in range(3):
                for oc in range(NCH):
                    sy.dma_start(out=lng_s[:, blki * 8 + oc: blki * 8 + oc + 1],
                                 in_=bass.AP(ln_g, blki * D + oc * 128, [[1, 128], [1, 1]])).then_inc(sd.h, 16)
                    sy.dma_start(out=lnb_s[:, blki * 8 + oc: blki * 8 + oc + 1],
                                 in_=bass.AP(ln_b, blki * D + oc * 128, [[1, 128], [1, 1]])).then_inc(sd.h, 16)
            sd.v += (48 + 48) * 16
            sy.wait_ge(sd.h, sd.v)

    A.pop()
    A.push()
    hb_t = A.take(4096)
    ybf_t = A.take(4096)
    g_t = A.take(4096)
    sqs_t = A.take(512)
    abb3 = A.take(1024)
    Dd.push()
    t1b_t = Dd.take(512)
    E.push()
    h_t = E.take(4096)
    lr0 = E.take(512)
    lr1 = E.take(512)
    lr2 = E.take(512)
    tb_t = A.take(512)

    C.off = 0
    C.marks.clear()
    wt_b = C.take(NCH * D)
    lab_b = C.take(NCH * 32)
    lbb_b = C.take(D)

    def l1_linear(i, rhs_t, out_cb):
        emit_wload(wq_dram.ap()[i], wt_b, NCH, D)
        emit_wload(laT.ap()[i], lab_b, NCH, 32)
        emit_wload(lbT.ap()[i], lbb_b, 1, D, rows=32)
        with nc.Block() as blk:
            b1, b2 = s1.v, s2.v

            @blk.tensor
            def _(te):
                for c in range(NCH):
                    te.matmul(pb[5][0:32, :], lab_b[:, c * 32:(c + 1) * 32],
                              rhs_t[:, c * 512:(c + 1) * 512],
                              start=(c == 0), stop=(c == NCH - 1)).then_inc(s1.h, 1)
                s1.v += NCH
                for oc in range(NCH):
                    if oc >= 2:
                        te.wait_ge(s2.h, b2 + oc)
                    for c in range(NCH):
                        te.matmul(pb[oc % 2][:, :], wt_b[:, c * D + oc * 128: c * D + (oc + 1) * 128],
                                  rhs_t[:, c * 512:(c + 1) * 512],
                                  start=(c == 0), stop=False).then_inc(s1.h, 1)
                    te.wait_ge(s2.h, b2 + 1)
                    te.matmul(pb[oc % 2][:, :], lbb_b[0:32, oc * 128:(oc + 1) * 128],
                              t1b_t[0:32, :], start=False, stop=True).then_inc(s1.h, 1)
                s1.v += NCH * (NCH + 1)

            @blk.vector
            def _(ve):
                ve.wait_ge(s1.h, b1 + NCH)
                ve.tensor_copy(t1b_t[0:32, :], pb[5][0:32, :]).then_inc(s2.h, 1)
                for oc in range(NCH):
                    ve.wait_ge(s1.h, b1 + NCH + (NCH + 1) * (oc + 1))
                    out_cb(ve, oc)
                s2.v += 1 + NCH

    for qb in range(NQB):
        emit_ln(xres[:, :], hb_t, w=QB, dst_f=h_t, nch=NCH, cs=NTOK, t0=qb * QB)
        for blki in range(3):
            i0, i1 = 2 * blki, 2 * blki + 1

            def cb0(ve, oc, i0=i0):
                ve.tensor_scalar(ybf_t[:, oc * 512:(oc + 1) * 512], pb[oc % 2][:, :],
                                 l1b_s[:, i0 * 8 + oc: i0 * 8 + oc + 1], None,
                                 OP.add).then_inc(s2.h, 1)
            l1_linear(i0, hb_t, cb0)

            with nc.Block() as blk:
                b1, b2, b3 = s1.v, s2.v, s3.v

                @blk.scalar
                def _(sc):
                    for c in range(NCH):
                        if c > 0:
                            sc.wait_ge(s1.h, b1 + 2 * c)
                        sc.activation(sqs_t[:, :], ybf_t[:, c * 512:(c + 1) * 512],
                                      AF.Square).then_inc(s3.h, 1)
                    s3.v += NCH

                @blk.tensor
                def _(te):
                    for c in range(NCH):
                        te.matmul(pb[2][0:1, :], oc_b[:, 0:1],
                                  ybf_t[:, c * 512:(c + 1) * 512],
                                  start=(c == 0), stop=(c == NCH - 1)).then_inc(s1.h, 1)
                        te.wait_ge(s3.h, b3 + c + 1)
                        te.matmul(pb[3][0:1, :], oc_b[:, 0:1], sqs_t[:, :],
                                  start=(c == 0), stop=(c == NCH - 1)).then_inc(s1.h, 1)
                    s1.v += 2 * NCH

                @blk.vector
                def _(ve):
                    ve.wait_ge(s1.h, b1 + 2 * NCH)
                    ve.tensor_copy(lr0[0:1, :], pb[2][0:1, :]).then_inc(sv.h, 1)
                    sv.v += 1
                    ve.wait_ge(sv.h, sv.v)
                    ve.tensor_copy(lr1[0:1, :], pb[3][0:1, :]).then_inc(sv.h, 1)
                    sv.v += 1
                    ve.wait_ge(sv.h, sv.v)
                    ve.tensor_scalar(lr2[0:1, :], lr1[0:1, :], 1.0 / D, float(EPS), OP.mult, OP.add).then_inc(sv.h, 1)
                    sv.v += 1
                    ve.wait_ge(sv.h, sv.v)
                    ve.tensor_scalar(lr1[0:1, :], lr0[0:1, :], 1.0 / D, None, OP.mult).then_inc(sv.h, 1)
                    sv.v += 1
                    ve.wait_ge(sv.h, sv.v)
                    ve.tensor_tensor(lr0[0:1, :], lr1[0:1, :], lr1[0:1, :], OP.mult).then_inc(sv.h, 1)
                    sv.v += 1
                    ve.wait_ge(sv.h, sv.v)
                    ve.tensor_tensor(lr0[0:1, :], lr2[0:1, :], lr0[0:1, :], OP.subtract).then_inc(s2.h, 1)
                    s2.v += 1

                @blk.scalar
                def _(sc):
                    sc.wait_ge(s2.h, b2 + 1)
                    sc.activation(lr2[0:1, :], lr0[0:1, :], AF.Sqrt).then_inc(s3.h, 1)
                    s3.v += 1

            with nc.Block() as blk:
                @blk.vector
                def _(ve):
                    ve.reciprocal(lr0[0:1, :], lr2[0:1, :]).then_inc(sv.h, 1)
                    sv.v += 1
                    ve.wait_ge(sv.h, sv.v)
                    ve.tensor_tensor(lr2[0:1, :], lr1[0:1, :], lr0[0:1, :], OP.mult).then_inc(sv.h, 1)
                    sv.v += 1
                    ve.wait_ge(sv.h, sv.v)
                    ve.tensor_copy(abb3[0:1, 0:512], lr0[0:1, :])
                    ve.tensor_copy(abb3[0:1, 512:1024], lr2[0:1, :])

            with nc.Block() as blk:
                b1, b2, b3 = s1.v, s2.v, s3.v

                @blk.tensor
                def _(te):
                    te.matmul(pb[2][:, :], or_b[0:1, :], abb3[0:1, 0:512],
                              start=True, stop=True).then_inc(s1.h, 1)
                    te.matmul(pb[3][:, :], or_b[0:1, :], abb3[0:1, 512:1024],
                              start=True, stop=True).then_inc(s1.h, 1)
                    s1.v += 2

                @blk.vector
                def _(ve):
                    ve.wait_ge(s1.h, b1 + 2)
                    for c in range(NCH):
                        sl = slice(c * 512, (c + 1) * 512)
                        ve.tensor_tensor(sqs_t[:, :], ybf_t[:, sl], pb[2][:, :], OP.mult)
                        ve.tensor_tensor(tb_t[:, :], sqs_t[:, :], pb[3][:, :], OP.subtract)
                        ve.tensor_scalar(ybf_t[:, sl], tb_t[:, :],
                                         lng_s[:, blki * 8 + c: blki * 8 + c + 1],
                                         lnb_s[:, blki * 8 + c: blki * 8 + c + 1],
                                         OP.mult, OP.add).then_inc(s2.h, 1)
                    s2.v += NCH

                @blk.scalar
                def _(sc):
                    for c in range(NCH):
                        sc.wait_ge(s2.h, b2 + c + 1)
                        sc.activation(g_t[:, c * 512:(c + 1) * 512],
                                      ybf_t[:, c * 512:(c + 1) * 512], AF.Gelu).then_inc(s3.h, 1)
                    s3.v += NCH

            def cb1(ve, oc, i1=i1):
                sl = slice(oc * 512, (oc + 1) * 512)
                ve.scalar_tensor_tensor(h_t[:, sl], pb[oc % 2][:, :],
                                        l1b_s[:, i1 * 8 + oc: i1 * 8 + oc + 1],
                                        h_t[:, sl], OP.add, OP.add).then_inc(sv.h, 1)
                sv.v += 1
                ve.wait_ge(sv.h, sv.v)
                ve.tensor_copy(hb_t[:, sl], h_t[:, sl]).then_inc(s2.h, 1)
            l1_linear(i1, g_t, cb1)

        with nc.Block() as blk:
            @blk.vector
            def _(ve):
                for c in range(NCH):
                    sl = slice(c * NTOK + qb * QB, c * NTOK + qb * QB + QB)
                    ve.tensor_tensor(xres[:, sl], h_t[:, c * 512:(c + 1) * 512],
                                     xres[:, sl], OP.add)
    E.pop()
    Dd.pop()
    A.pop()

    emit_store_y()
    return nc, tap_tensors


def make_in_maps(inputs):
    xnp = np.asarray(inputs["L0_attn_x"])
    TT = lambda a: np.ascontiguousarray(np.asarray(a).T)
    Cc = lambda a: np.ascontiguousarray(np.asarray(a))

    gw = np.zeros((16, 1024), np.float32)
    gw[0:8] = np.asarray(inputs["L0_ffn_eg_w"])
    gw[8] = np.asarray(inputs["L0_ffn_sg_w"])[0]
    gbv = np.zeros((16, 1), np.float32)
    gbv[0:8, 0] = np.asarray(inputs["L0_ffn_eg_b"])
    gbv[8, 0] = np.asarray(inputs["L0_ffn_sg_b"])[0]

    shared = {
        "w_dqT": TT(inputs["L0_attn_W_dq"]),
        "w_dq": Cc(inputs["L0_attn_W_dq"]),
        "w_uqT": TT(inputs["L0_attn_W_uq"]),
        "w_uq": Cc(inputs["L0_attn_W_uq"]),
        "w_dkvT": TT(inputs["L0_attn_W_dkv"]),
        "w_uk": Cc(inputs["L0_attn_W_uk"]),
        "w_uv": Cc(inputs["L0_attn_W_uv"]),
        "w_oT": TT(inputs["L0_attn_W_o"]),
        "sw1T": TT(inputs["L0_ffn_sw1"]),
        "sw2T": TT(inputs["L0_ffn_sw2"]),
        "sw3T": TT(inputs["L0_ffn_sw3"]),
        "sb1": Cc(inputs["L0_ffn_sb1"]).reshape(-1, 1),
        "sb2": Cc(inputs["L0_ffn_sb2"]).reshape(-1, 1),
        "sb3": Cc(inputs["L0_ffn_sb3"]).reshape(-1, 1),
        "s_ln_g": Cc(inputs["L0_ffn_s_ln_g"]).reshape(-1, 1),
        "s_ln_b": Cc(inputs["L0_ffn_s_ln_b"]).reshape(-1, 1),
        "gwT": TT(gw),
        "gb": gbv,
        "exp_bias": Cc(inputs["L0_ffn_exp_bias"]).reshape(-1, 1),
        "sel_ln_g": Cc(inputs["L0_ffn_sel_ln_g"]).reshape(1, -1),
        "sel_ln_b": Cc(inputs["L0_ffn_sel_ln_b"]).reshape(1, -1),
        "idx_b": np.asarray(inputs["L0_ffn_index_b"], dtype=np.int32),
        "idx_t": np.asarray(inputs["L0_ffn_index_t"], dtype=np.int32),
        "w_qkvT": TT(inputs["L1_attn_W_qkv"]),
        "w_l1oT": TT(inputs["L1_attn_W_o"]),
        "qwT": Cc(np.asarray(inputs["L1_ffn_qweights"], dtype=np.int32).transpose(0, 2, 1)),
        "scT": Cc(np.repeat(np.asarray(inputs["L1_ffn_scales"]), 16, axis=2).transpose(0, 2, 1)),
        "l1_biases": Cc(inputs["L1_ffn_biases"]),
        "laT": Cc(np.asarray(inputs["L1_ffn_lora_a"]).transpose(0, 2, 1)),
        "lbT": Cc(np.asarray(inputs["L1_ffn_lora_b"]).transpose(0, 2, 1)),
        "ln_g": Cc(inputs["L1_ffn_ln_gamma"]),
        "ln_b": Cc(inputs["L1_ffn_ln_beta"]),
    }
    ew1 = np.asarray(inputs["L0_ffn_ew1"])
    ew2 = np.asarray(inputs["L0_ffn_ew2"])
    ew3 = np.asarray(inputs["L0_ffn_ew3"])
    eb1 = np.asarray(inputs["L0_ffn_eb1"])
    eb2 = np.asarray(inputs["L0_ffn_eb2"])
    eb3 = np.asarray(inputs["L0_ffn_eb3"])

    in_maps = []
    for c in range(8):
        gs = _core_blocks(c)
        bb = c // 4
        xo = np.concatenate([xnp[bb, (g % 16) * QB:(g % 16) * QB + QB] for g in gs], axis=0)
        m = dict(shared)
        m["xT"] = np.ascontiguousarray(xo.T)
        m["ew1T"] = TT(ew1[c])
        m["ew2T"] = TT(ew2[c])
        m["ew3T"] = TT(ew3[c])
        m["eb1"] = Cc(eb1[c]).reshape(1, -1)
        m["eb2"] = Cc(eb2[c]).reshape(1, -1)
        m["eb3"] = Cc(eb3[c]).reshape(1, -1)
        meta = np.zeros((128, 4), np.int32)
        for j, g in enumerate(gs):
            meta[:, j] = g
        m["core_meta"] = meta
        oh = np.zeros((128, 8), np.float32)
        oh[:, c] = 1.0
        m["my_onehot"] = oh
        q0v = np.zeros((128, 4), np.float32)
        ktp = np.zeros((128, 256), np.float32)
        for j, g in enumerate(gs):
            q0v[:, j] = (g % 16) * QB
            v = 4 * ((g % 16) + 1)
            ktp[:, j * 64 + v: j * 64 + 64] = -30000.0
        m["q0s"] = q0v
        m["ktpen"] = ktp
        m["meta_f"] = meta.astype(np.float32)
        in_maps.append(m)
    return in_maps


def assemble_output(results):
    y = np.zeros((B, T, D), dtype=np.float32)
    for c in range(8):
        gs = _core_blocks(c)
        bb = c // 4
        yt = results[c]["y"].T
        for j, g in enumerate(gs):
            y[bb, (g % 16) * QB:(g % 16) * QB + QB] = yt[j * QB:(j + 1) * QB]
    return y


def kernel(_taps=(), _upto=None, **inputs):
    nc, tap_tensors = build_program(taps=_taps, upto=_upto)
    in_maps = make_in_maps(inputs)
    res = run_bass_kernel_spmd(nc, in_maps, core_ids=list(range(8)))
    if _taps or _upto:
        return res
    return assemble_output(res.results)


# version 06
# speedup vs baseline: 1.7785x; 1.3780x over previous
import numpy as np
import ml_dtypes as md
import concourse.bass as bass
import concourse.mybir as mybir
from concourse.bass_core_rust import IndirectOffsetOnAxis
from concourse.bass_utils import run_bass_kernel_spmd

F32 = mybir.dt.float32
BF16 = mybir.dt.bfloat16
I32 = mybir.dt.int32
AF = mybir.ActivationFunctionType
OP = mybir.AluOpType
AX = mybir.AxisListType

B, T, D = 2, 8192, 1024
NTOK = 2048
NQB = 4
QB = 512
NCH = 8
NH, HS, KVL = 16, 64, 128
EPS = 1e-5
MOE_E, MOE_H, MOE_M = 8, 512, 2048
LORA_R = 32

bf = md.bfloat16


def _core_blocks(c):
    b, i = c // 4, c % 4
    return [16 * b + i, 16 * b + 7 - i, 16 * b + 8 + i, 16 * b + 15 - i]


def _block_owner(g):
    b, gg = g // 16, g % 16
    if gg <= 3:
        i, j = gg, 0
    elif gg <= 7:
        i, j = 7 - gg, 1
    elif gg <= 11:
        i, j = gg - 8, 2
    else:
        i, j = 15 - gg, 3
    return 4 * b + i, j


class Ctr:
    def __init__(self, nc, name):
        self.h = nc.alloc_semaphore(name)
        self.v = 0


class Arena:
    def __init__(self, nc, name, n, dtype):
        self.t = nc.alloc_sbuf_tensor(name, [128, n], dtype)
        self.n = n
        self.off = 0
        self.marks = []

    def take(self, n):
        assert self.off + n <= self.n, f"arena {self.t.name} overflow {self.off}+{n}>{self.n}"
        off = self.off
        self.off += n
        return self.t[:, off:off + n]

    def push(self):
        self.marks.append(self.off)

    def pop(self):
        self.off = self.marks.pop()


def build_program(taps=(), upto=None):
    nc = bass.Bass("TRN2", target_bir_lowering=False)

    def inp(name, shape, dtype=F32):
        return nc.dram_tensor(name, shape, dtype, kind="ExternalInput")

    xT_in = inp("xT", [D, NTOK])
    w_dqT = inp("w_dqT", [D, KVL])
    w_dq = inp("w_dq", [KVL, D])
    w_uq = inp("w_uq", [D, KVL])
    w_uqT = inp("w_uqT", [KVL, D])
    w_dkvT = inp("w_dkvT", [D, KVL])
    w_uk = inp("w_uk", [D, KVL])
    w_uv = inp("w_uv", [D, KVL])
    w_oT = inp("w_oT", [D, D])
    sw1T = inp("sw1T", [D, MOE_H])
    sw2T = inp("sw2T", [D, MOE_H])
    sw3T = inp("sw3T", [MOE_H, D])
    sb1 = inp("sb1", [MOE_H, 1])
    sb2 = inp("sb2", [MOE_H, 1])
    sb3 = inp("sb3", [D, 1])
    s_ln_g = inp("s_ln_g", [MOE_H, 1])
    s_ln_b = inp("s_ln_b", [MOE_H, 1])
    gwT = inp("gwT", [D, 16])
    gb = inp("gb", [16, 1])
    exp_bias = inp("exp_bias", [MOE_E, 1])
    ew1T = inp("ew1T", [D, MOE_H])
    ew2T = inp("ew2T", [D, MOE_H])
    ew3T = inp("ew3T", [MOE_H, D])
    eb1 = inp("eb1", [1, MOE_H])
    eb2 = inp("eb2", [1, MOE_H])
    eb3 = inp("eb3", [1, D])
    sel_ln_g = inp("sel_ln_g", [1, MOE_H])
    sel_ln_b = inp("sel_ln_b", [1, MOE_H])
    idx_b = inp("idx_b", [MOE_M], I32)
    idx_t = inp("idx_t", [MOE_M], I32)
    core_meta = inp("core_meta", [128, 4], I32)
    my_onehot = inp("my_onehot", [128, 8])
    q0s = inp("q0s", [128, 4])
    ktpen = inp("ktpen", [128, 256])
    meta_f = inp("meta_f", [128, 4])
    w_qkvT = inp("w_qkvT", [D, D])
    w_l1oT = inp("w_l1oT", [D, D])
    qwT = inp("qwT", [6, D, D], I32)
    scT = inp("scT", [6, D, D])
    l1_biases = inp("l1_biases", [6, D])
    laT = inp("laT", [6, D, LORA_R])
    lbT = inp("lbT", [6, LORA_R, D])
    ln_g = inp("ln_g", [3, D])
    ln_b = inp("ln_b", [3, D])

    y_out = nc.dram_tensor("y", [D, NTOK], F32, kind="ExternalOutput")

    identb = nc.inline_tensor(np.eye(128, dtype=bf), "identb")
    identf = nc.inline_tensor(np.eye(128, dtype=np.float32), "identf")
    ones_col_b = nc.inline_tensor(np.ones((128, 2), dtype=bf), "ones_col_b")
    ones_row_b = nc.inline_tensor(np.ones((1, 128), dtype=bf), "ones_row_b")
    ones_col_f = nc.inline_tensor(np.ones((128, 1), dtype=np.float32), "ones_col_f")
    ones_row_f = nc.inline_tensor(np.ones((1, 128), dtype=np.float32), "ones_row_f")
    jcols0 = nc.inline_tensor((np.arange(128, dtype=np.float32)[:, None] +
                               128.0 * np.arange(16, dtype=np.float32)[None, :]).copy(), "jcols0")
    bigcol0 = nc.inline_tensor(np.full((128, 1), 65536.0, np.float32), "bigcol0")
    colmap0 = nc.inline_tensor(
        np.broadcast_to(np.arange(QB, dtype=np.float32)[None, :], (128, QB)).copy(), "colmap0")
    rowmap0 = nc.inline_tensor(
        (np.arange(128, dtype=np.float32)[:, None] +
         128.0 * np.arange(64, dtype=np.float32)[None, :]).copy(), "rowmap0")

    ckv_contrib = nc.dram_tensor("ckv_contrib", [2 * NTOK, KVL], BF16)
    ckv_ag = nc.dram_tensor("ckv_ag", [4 * 2 * NTOK, KVL], BF16)
    qkv_contrib = nc.dram_tensor("qkv_contrib", [2 * NTOK, D], BF16)
    qkv_ag = nc.dram_tensor("qkv_ag", [4 * 2 * NTOK, D], BF16)
    xm_tm_dram = nc.dram_tensor("xm_tm_dram", [NTOK + 128, D], BF16)
    xmoe_contrib = nc.dram_tensor("xmoe_contrib", [MOE_M, D], BF16)
    xmoe_all = nc.dram_tensor("xmoe_all", [MOE_M, D], BF16)
    delta_contrib = nc.dram_tensor("delta_contrib", [MOE_M, D], F32)
    delta_all = nc.dram_tensor("delta_all", [MOE_M + 128, D], F32)
    own_scale = nc.dram_tensor("own_scale", [NTOK + 128, 1], F32)
    own_delta = nc.dram_tensor("own_delta", [NTOK, D], F32)
    own_inv = nc.dram_tensor("own_inv", [NTOK + 128, 1], F32)
    wq_dram = nc.dram_tensor("wq_dram", [6, D, D], BF16)

    xres = nc.alloc_sbuf_tensor("xres", [128, NCH * NTOK], F32)
    A = Arena(nc, "argA", 16384, BF16)
    Bb = Arena(nc, "argB", 16384, BF16)
    C = Arena(nc, "argC", 16384, BF16)
    Dd = Arena(nc, "argD", 4096, BF16)
    E = Arena(nc, "argE", 8192, F32)
    Si = Arena(nc, "argSi", 256, I32)

    idb_s = nc.alloc_sbuf_tensor("idb_s", [128, 128], BF16)
    idf_s = nc.alloc_sbuf_tensor("idf_s", [128, 128], F32)
    oc_b = nc.alloc_sbuf_tensor("oc_b", [128, 2], BF16)
    or_b = nc.alloc_sbuf_tensor("or_b", [1, 128], BF16)
    oc_f = nc.alloc_sbuf_tensor("oc_f", [128, 1], F32)
    or_f = nc.alloc_sbuf_tensor("or_f", [1, 128], F32)
    colmap_s = nc.alloc_sbuf_tensor("colmap_s", [128, QB], F32)
    rowmap_s = nc.alloc_sbuf_tensor("rowmap_s", [128, 64], F32)
    q0s_s = nc.alloc_sbuf_tensor("q0s_s", [128, 4], F32)
    ktpen_s = nc.alloc_sbuf_tensor("ktpen_s", [128, 256], F32)
    meta_s = nc.alloc_sbuf_tensor("meta_s", [128, 4], I32)
    metaf_s = nc.alloc_sbuf_tensor("metaf_s", [128, 4], F32)
    onehot_s = nc.alloc_sbuf_tensor("onehot_s", [128, 8], F32)

    pb = [nc.alloc_psum_tensor(f"pb{i}", [128, 512], F32) for i in range(7)]
    pbh = nc.alloc_psum_tensor("pbh", [128, 1024], BF16)

    sd = Ctr(nc, "sd")
    sg = Ctr(nc, "sg")
    s1 = Ctr(nc, "s1")
    s2 = Ctr(nc, "s2")
    s3 = Ctr(nc, "s3")
    s4 = Ctr(nc, "s4")
    sv = Ctr(nc, "sv")
    sco = Ctr(nc, "sco")

    tap_tensors = {}

    def tap_fm(name, src, nch=NCH, width=NTOK, dtype=F32):
        if name not in taps:
            return
        t = nc.dram_tensor("tap_" + name, [nch * 128, width], dtype, kind="ExternalOutput")
        tap_tensors[name] = t
        with nc.Block() as blk:
            @blk.sync
            def _(sy):
                for c in range(nch):
                    sy.dma_start(out=t.ap()[c * 128:(c + 1) * 128, :],
                                 in_=src[:, c * width:(c + 1) * width]).then_inc(sd.h, 16)
                sd.v += nch * 16
                sy.wait_ge(sd.h, sd.v)

    def tap_copy_dram(name, src_dram, shape, dtype):
        if name not in taps:
            return
        t = nc.dram_tensor("tap_" + name, shape, dtype, kind="ExternalOutput")
        tap_tensors[name] = t
        with nc.Block() as blk:
            @blk.sync
            def _(sy):
                sy.dma_start(out=t.ap(), in_=src_dram.ap()).then_inc(sd.h, 16)
                sd.v += 16
                sy.wait_ge(sd.h, sd.v)

    with nc.Block() as blk:
        @blk.sync
        def _(sy):
            for c in range(NCH):
                sy.dma_start(out=xres[:, c * NTOK:(c + 1) * NTOK],
                             in_=xT_in.ap()[c * 128:(c + 1) * 128, :]).then_inc(sd.h, 16)
            sy.dma_start(out=idb_s[:], in_=identb.ap()).then_inc(sd.h, 16)
            sy.dma_start(out=idf_s[:], in_=identf.ap()).then_inc(sd.h, 16)
            sy.dma_start(out=oc_b[:], in_=ones_col_b.ap()).then_inc(sd.h, 16)
            sy.dma_start(out=or_b[:], in_=ones_row_b.ap()).then_inc(sd.h, 16)
            sy.dma_start(out=oc_f[:], in_=ones_col_f.ap()).then_inc(sd.h, 16)
            sy.dma_start(out=meta_s[:], in_=core_meta.ap()).then_inc(sd.h, 16)
            sy.dma_start(out=metaf_s[:], in_=meta_f.ap()).then_inc(sd.h, 16)
            sy.dma_start(out=onehot_s[:], in_=my_onehot.ap()).then_inc(sd.h, 16)
            sy.dma_start(out=or_f[:], in_=ones_row_f.ap()).then_inc(sd.h, 16)
            sy.dma_start(out=colmap_s[:], in_=colmap0.ap()).then_inc(sd.h, 16)
            sy.dma_start(out=rowmap_s[:], in_=rowmap0.ap()).then_inc(sd.h, 16)
            sy.dma_start(out=q0s_s[:], in_=q0s.ap()).then_inc(sd.h, 16)
            sy.dma_start(out=ktpen_s[:], in_=ktpen.ap()).then_inc(sd.h, 16)
            sd.v += (NCH + 13) * 16
            sy.wait_ge(sd.h, sd.v)

    def emit_ln(src, dst_b, w=NTOK, dst_f=None, nch=NCH, cs=None, t0=0):
        E.push(); Dd.push()
        r0t = E.take(w)
        r1t = E.take(w)
        r2t = E.take(w)
        sq = E.take(512)
        tsc = E.take(512)
        abb = Dd.take(2 * w)
        nqb = w // QB
        if cs is None:
            cs = w
        srow, qrow = r0t[0:1, :], r1t[0:1, :]
        arow = r2t[0:1, :]

        with nc.Block() as blk:
            b1, b2, b3 = s1.v, s2.v, s3.v

            @blk.scalar
            def _(sc):
                for qb in range(nqb):
                    for c in range(nch):
                        i = qb * nch + c
                        if i > 0:
                            sc.wait_ge(s1.h, b1 + 2 * i)
                        sc.activation(sq[:], src[:, c * cs + t0 + qb * QB: c * cs + t0 + qb * QB + QB],
                                      AF.Square).then_inc(s3.h, 1)
                s3.v += nqb * nch

            @blk.tensor
            def _(te):
                for qb in range(nqb):
                    for c in range(nch):
                        i = qb * nch + c
                        if c == 0 and qb > 0:
                            te.wait_ge(s2.h, b2 + 2 * qb)
                        te.matmul(pb[0][0:1, :], oc_f[:, 0:1],
                                  src[:, c * cs + t0 + qb * QB: c * cs + t0 + qb * QB + QB],
                                  start=(c == 0), stop=(c == nch - 1)).then_inc(s1.h, 1)
                        te.wait_ge(s3.h, b3 + i + 1)
                        te.matmul(pb[1][0:1, :], oc_f[:, 0:1], sq[:],
                                  start=(c == 0), stop=(c == nch - 1)).then_inc(s1.h, 1)
                s1.v += 2 * nqb * nch

            @blk.vector
            def _(ve):
                for qb in range(nqb):
                    ve.wait_ge(s1.h, b1 + 2 * nch * (qb + 1))
                    ve.tensor_copy(srow[0:1, qb * QB:(qb + 1) * QB], pb[0][0:1, :]).then_inc(s2.h, 1)
                    ve.tensor_copy(qrow[0:1, qb * QB:(qb + 1) * QB], pb[1][0:1, :]).then_inc(s2.h, 1)
                s2.v += 2 * nqb

        dd = float(nch * 128)
        with nc.Block() as blk:
            @blk.vector
            def _(ve):
                ve.tensor_scalar(arow, qrow, 1.0 / dd, float(EPS), OP.mult, OP.add)
        with nc.Block() as blk:
            @blk.vector
            def _(ve):
                ve.tensor_scalar(qrow, srow, 1.0 / dd, None, OP.mult)
        with nc.Block() as blk:
            @blk.vector
            def _(ve):
                ve.tensor_tensor(srow, qrow, qrow, OP.mult)
        with nc.Block() as blk:
            @blk.vector
            def _(ve):
                ve.tensor_tensor(srow, arow, srow, OP.subtract)
        with nc.Block() as blk:
            @blk.scalar
            def _(sc):
                sc.activation(arow, srow, AF.Sqrt)
        with nc.Block() as blk:
            @blk.vector
            def _(ve):
                ve.reciprocal(srow, arow)
        with nc.Block() as blk:
            @blk.vector
            def _(ve):
                ve.tensor_tensor(arow, qrow, srow, OP.mult)
        with nc.Block() as blk:
            @blk.vector
            def _(ve):
                ve.tensor_copy(abb[0:1, 0:w], srow)
                ve.tensor_copy(abb[0:1, w:2 * w], arow)

        with nc.Block() as blk:
            b1, b2 = s1.v, s2.v

            @blk.tensor
            def _(te):
                for qb in range(nqb):
                    if qb > 0:
                        te.wait_ge(s2.h, b2 + qb)
                    te.matmul(pb[2][:, :], or_b[0:1, :], abb[0:1, qb * QB:(qb + 1) * QB],
                              start=True, stop=True).then_inc(s1.h, 1)
                    te.matmul(pb[3][:, :], or_b[0:1, :], abb[0:1, w + qb * QB: w + (qb + 1) * QB],
                              start=True, stop=True).then_inc(s1.h, 1)
                s1.v += 2 * nqb

            @blk.vector
            def _(ve):
                for qb in range(nqb):
                    ve.wait_ge(s1.h, b1 + 2 * qb + 2)
                    for c in range(nch):
                        sl = slice(c * w + qb * QB, c * w + qb * QB + QB)
                        ssl = slice(c * cs + t0 + qb * QB, c * cs + t0 + qb * QB + QB)
                        ve.tensor_tensor(tsc[:], src[:, ssl], pb[2][:, :], OP.mult)
                        if dst_f is not None:
                            ve.tensor_tensor(dst_f[:, sl], tsc[:], pb[3][:, :], OP.subtract)
                            bi = ve.tensor_copy(dst_b[:, sl], dst_f[:, sl])
                        else:
                            bi = ve.tensor_tensor(dst_b[:, sl], tsc[:], pb[3][:, :], OP.subtract)
                        if c == nch - 1:
                            bi.then_inc(s2.h, 1)
                s2.v += nqb
        E.pop(); Dd.pop()

    def emit_wload(wdram, dst, nblk, cols, rows=128):
        wap = wdram if isinstance(wdram, bass.AP) else wdram.ap()
        if wap.tensor.dtype == BF16:
            with nc.Block() as blk:
                @blk.sync
                def _(sy):
                    for i in range(nblk):
                        sy.dma_start(out=dst[0:rows, i * cols:(i + 1) * cols],
                                     in_=wap[i * rows:(i + 1) * rows, :]).then_inc(sd.h, 16)
                    sd.v += nblk * 16
                    sy.wait_ge(sd.h, sd.v)
            return
        E.push()
        stg = [E.take(cols) for _ in range(2)]
        with nc.Block() as blk:
            b4 = s4.v
            bd = sd.v

            @blk.sync
            def _(sy):
                for i in range(nblk):
                    if i >= 2:
                        sy.wait_ge(s4.h, b4 + i - 1)
                    sy.dma_start(out=stg[i % 2][0:rows, :],
                                 in_=wap[i * rows:(i + 1) * rows, :]).then_inc(sd.h, 16)
                sd.v += nblk * 16

            @blk.gpsimd
            def _(gp):
                for i in range(nblk):
                    gp.wait_ge(sd.h, bd + 16 * (i + 1))
                    gp.tensor_copy(dst[0:rows, i * cols:(i + 1) * cols],
                                   stg[i % 2][0:rows, :]).then_inc(s4.h, 1)
                s4.v += nblk
        E.pop()

    A.push()
    xn = A.take(NCH * NTOK)
    emit_ln(xres[:, :], xn)
    tap_fm("xn", xn, dtype=BF16)

    def emit_store_y():
        with nc.Block() as blk:
            @blk.sync
            def _(sy):
                for c in range(NCH):
                    sy.dma_start(out=y_out.ap()[c * 128:(c + 1) * 128, :],
                                 in_=xres[:, c * NTOK:(c + 1) * NTOK]).then_inc(sd.h, 16)
                sd.v += NCH * 16
                sy.wait_ge(sd.h, sd.v)

    if upto == "P1":
        emit_store_y()
        return nc, tap_tensors

    keff_b = C.take(NCH * KVL)
    v1_b = C.take(NCH * KVL)
    C.push()
    wdkvT_b = C.take(NCH * KVL)
    wdqT_b = C.take(NCH * KVL)
    wuqT_b = C.take(D)
    wAux_b = C.take(NCH * KVL)
    wuk_b = C.take(NCH * KVL)
    wdq_b = C.take(D)
    c1_b = C.take(KVL)

    emit_wload(w_dkvT, wdkvT_b, NCH, KVL)
    emit_wload(w_dqT, wdqT_b, NCH, KVL)
    emit_wload(w_uqT, wuqT_b, 1, D)
    emit_wload(w_uq, wAux_b, NCH, KVL)
    emit_wload(w_uk, wuk_b, NCH, KVL)
    emit_wload(w_dq, wdq_b, 1, D)

    with nc.Block() as blk:
        b1 = s1.v

        @blk.tensor
        def _(te):
            for i in range(NCH):
                te.matmul(pb[0][:, 0:KVL], wAux_b[:, i * KVL:(i + 1) * KVL],
                          wuk_b[:, i * KVL:(i + 1) * KVL],
                          start=(i == 0), stop=(i == NCH - 1)).then_inc(s1.h, 1)
            s1.v += NCH

        @blk.vector
        def _(ve):
            ve.wait_ge(s1.h, b1 + NCH)
            ve.tensor_copy(c1_b[:, :], pb[0][:, 0:KVL])

    with nc.Block() as blk:
        b1, b2 = s1.v, s2.v

        @blk.tensor
        def _(te):
            for c in range(NCH):
                if c >= 2:
                    te.wait_ge(s2.h, b2 + c - 1)
                te.matmul(pb[c % 2][:, 0:KVL], wdq_b[:, c * KVL:(c + 1) * KVL],
                          c1_b[:, :], start=True, stop=True).then_inc(s1.h, 1)
            s1.v += NCH

        @blk.vector
        def _(ve):
            for c in range(NCH):
                ve.wait_ge(s1.h, b1 + c + 1)
                ve.tensor_copy(keff_b[:, c * KVL:(c + 1) * KVL],
                               pb[c % 2][:, 0:KVL]).then_inc(s2.h, 1)
            s2.v += NCH

    emit_wload(w_uv, wAux_b, NCH, KVL)
    Bb.push()
    woT_b = Bb.take(NCH * D)
    emit_wload(w_oT, woT_b, NCH, D)
    with nc.Block() as blk:
        b1, b2 = s1.v, s2.v

        @blk.tensor
        def _(te):
            for j in range(NCH):
                if j >= 2:
                    te.wait_ge(s2.h, b2 + j - 1)
                for c in range(NCH):
                    te.matmul(pb[j % 2][:, 0:KVL], wAux_b[:, c * KVL:(c + 1) * KVL],
                              woT_b[:, c * D + j * 128: c * D + j * 128 + 128],
                              start=(c == 0), stop=(c == NCH - 1)).then_inc(s1.h, 1)
            s1.v += NCH * NCH

        @blk.vector
        def _(ve):
            for j in range(NCH):
                ve.wait_ge(s1.h, b1 + NCH * (j + 1))
                ve.tensor_copy(v1_b[:, j * KVL:(j + 1) * KVL],
                               pb[j % 2][:, 0:KVL]).then_inc(s2.h, 1)
            s2.v += NCH
    Bb.pop()

    if upto == "P2":
        tap_fm("keff", keff_b, nch=1, width=NCH * KVL, dtype=BF16)
        tap_fm("v1", v1_b, nch=1, width=NCH * KVL, dtype=BF16)
        emit_store_y()
        return nc, tap_tensors

    Dd.push()
    ckv_own = Dd.take(NTOK)
    ckv_own_tm = Dd.take(NTOK)
    with nc.Block() as blk:
        b1, b2 = s1.v, s2.v

        @blk.tensor
        def _(te):
            for qb in range(NQB):
                if qb >= 2:
                    te.wait_ge(s2.h, b2 + qb - 1)
                for c in range(NCH):
                    te.matmul(pb[qb % 2][:, :], wdkvT_b[:, c * KVL:(c + 1) * KVL],
                              xn[:, c * NTOK + qb * QB: c * NTOK + qb * QB + QB],
                              start=(c == 0), stop=(c == NCH - 1)).then_inc(s1.h, 1)
            s1.v += NQB * NCH

        @blk.vector
        def _(ve):
            for qb in range(NQB):
                ve.wait_ge(s1.h, b1 + NCH * (qb + 1))
                ve.tensor_copy(ckv_own[:, qb * QB:(qb + 1) * QB],
                               pb[qb % 2][:, :]).then_inc(s2.h, 1)
            s2.v += NQB

    with nc.Block() as blk:
        b1, b2 = s1.v, s2.v

        @blk.tensor
        def _(te):
            for i in range(16):
                if i >= 1:
                    te.wait_ge(s2.h, b2 + i)
                te.transpose(pbh[:, (i % 2) * 128:(i % 2) * 128 + 128],
                             ckv_own[:, i * 128:(i + 1) * 128], idb_s[:]).then_inc(s1.h, 1)
            s1.v += 16

        @blk.vector
        def _(ve):
            for i in range(16):
                ve.wait_ge(s1.h, b1 + i + 1)
                ve.tensor_copy(ckv_own_tm[:, i * 128:(i + 1) * 128],
                               pbh[:, (i % 2) * 128:(i % 2) * 128 + 128]).then_inc(s2.h, 1)
            s2.v += 16

    with nc.Block() as blk:
        @blk.sync
        def _(sy):
            fmv = bass.AP(ckv_contrib, 0, [[NTOK, 128], [1, NTOK]])
            sy.dma_start(out=fmv, in_=ckv_own[:, :]).then_inc(sd.h, 16)
            for i in range(16):
                sy.dma_start(out=ckv_contrib.ap()[NTOK + i * 128: NTOK + (i + 1) * 128, :],
                             in_=ckv_own_tm[:, i * 128:(i + 1) * 128]).then_inc(sd.h, 16)
            sd.v += 17 * 16
            sy.wait_ge(sd.h, sd.v)
    Dd.pop()

    if upto == "P3a":
        emit_store_y()
        return nc, tap_tensors

    groups4 = [[0, 1, 2, 3], [4, 5, 6, 7]]
    with nc.Block() as blk:
        @blk.gpsimd
        def _(gp):
            gp.collective_compute("AllGather", OP.bypass, replica_groups=groups4,
                                  ins=[ckv_contrib.ap()], outs=[ckv_ag.ap()]).then_inc(sco.h, 1)
            sco.v += 1
            gp.wait_ge(sco.h, sco.v)

    if upto == "P3b":
        emit_store_y()
        return nc, tap_tensors

    Bb.push()
    qT = Bb.take(NCH * NTOK)
    Dd.push()
    t1_b = Dd.take(NTOK)
    with nc.Block() as blk:
        b1, b2 = s1.v, s2.v

        @blk.tensor
        def _(te):
            for qb in range(NQB):
                if qb >= 2:
                    te.wait_ge(s2.h, b2 + qb - 1)
                for c in range(NCH):
                    te.matmul(pb[qb % 2][:, :], wdqT_b[:, c * KVL:(c + 1) * KVL],
                              xn[:, c * NTOK + qb * QB: c * NTOK + qb * QB + QB],
                              start=(c == 0), stop=(c == NCH - 1)).then_inc(s1.h, 1)
            s1.v += NQB * NCH

        @blk.vector
        def _(ve):
            for qb in range(NQB):
                ve.wait_ge(s1.h, b1 + NCH * (qb + 1))
                ve.tensor_copy(t1_b[:, qb * QB:(qb + 1) * QB],
                               pb[qb % 2][:, :]).then_inc(s2.h, 1)
            s2.v += NQB

    with nc.Block() as blk:
        b1, b2 = s1.v, s2.v

        @blk.tensor
        def _(te):
            n = 0
            for c in range(NCH):
                for qb in range(NQB):
                    if n >= 2:
                        te.wait_ge(s2.h, b2 + n - 1)
                    te.matmul(pb[n % 2][:, :], wuqT_b[:, c * 128:(c + 1) * 128],
                              t1_b[:, qb * QB:(qb + 1) * QB],
                              start=True, stop=True).then_inc(s1.h, 1)
                    n += 1
            s1.v += NCH * NQB

        @blk.vector
        def _(ve):
            n = 0
            for c in range(NCH):
                for qb in range(NQB):
                    ve.wait_ge(s1.h, b1 + n + 1)
                    ve.tensor_copy(qT[:, c * NTOK + qb * QB: c * NTOK + qb * QB + QB],
                                   pb[n % 2][:, :]).then_inc(s2.h, 1)
                    n += 1
            s2.v += NCH * NQB
    Dd.pop()
    C.pop()

    A.pop()
    A.push()
    ckv_fm = A.take(8 * 1024)
    ckv_tm = A.take(8 * 1024)
    with nc.Block() as blk:
        @blk.sync
        def _(sy):
            for gg in range(16):
                cc, slot = _block_owner(gg)
                cc = cc % 4
                base = cc * 2 * NTOK
                fmv = bass.AP(ckv_ag, base * KVL + slot * QB, [[NTOK, 128], [1, QB]])
                sy.dma_start(out=ckv_fm[:, gg * QB:(gg + 1) * QB], in_=fmv).then_inc(sd.h, 16)
                for j in range(4):
                    kt = gg * 4 + j
                    r0 = base + NTOK + slot * QB + j * 128
                    sy.dma_start(out=ckv_tm[:, kt * 128:(kt + 1) * 128],
                                 in_=ckv_ag.ap()[r0:r0 + 128, :]).then_inc(sd.h, 16)
            sd.v += 16 * 5 * 16
            sy.wait_ge(sd.h, sd.v)

    if upto == "P3":
        tap_fm("ckv_fm", ckv_fm, nch=1, width=8192, dtype=BF16)
        tap_fm("ckv_tm", ckv_tm, nch=1, width=8192, dtype=BF16)
        tap_fm("qT", qT, nch=1, width=NCH * NTOK, dtype=BF16)
        emit_store_y()
        return nc, tap_tensors

    Dd.push()
    qlh = Dd.take(2 * NTOK)
    E.push()
    mrun = E.take(8 * QB)
    permax = E.take(8)
    negm8 = E.take(8)
    biast = E.take(8)
    biastx = E.take(512)
    recipf = E.take(1024)
    rects = E.take(1024)
    C.push()
    pt = [C.take(QB), C.take(QB)]
    ptm = [C.take(QB), C.take(QB)]
    ctxe = [C.take(QB), C.take(QB)]
    scrb = C.take(128)
    negmrow = C.take(16)

    UB = [16, 32, 48, 64]
    MB = [(0, 16), (16, 32), (32, 48), (48, 64)]

    for hp in range(NH // 2):
        g0, g1 = 2 * hp, 2 * hp + 1
        with nc.Block() as blk:
            b1, b2, bv = s1.v, s2.v, sv.v
            akts = {qb: list(range(0, UB[qb], 4)) for qb in range(NQB)}
            ntile = 2 * sum(len(v) for v in akts.values())

            @blk.tensor
            def _(te):
                n = 0
                for h01 in range(2):
                    for qb in range(NQB):
                        if n >= 2:
                            te.wait_ge(s2.h, b2 + n - 1)
                        te.matmul(pb[n % 2][:, :],
                                  keff_b[h01 * 64:(h01 + 1) * 64, hp * 128:(hp + 1) * 128],
                                  qT[h01 * 64:(h01 + 1) * 64,
                                     hp * NTOK + qb * QB: hp * NTOK + (qb + 1) * QB],
                                  start=True, stop=True).then_inc(s1.h, 1)
                        n += 1
                a = 0
                for qb in range(NQB):
                    for kt in akts[qb]:
                        for h01 in range(2):
                            if a < 4:
                                te.wait_ge(s2.h, b2 + 8)
                            else:
                                te.wait_ge(s2.h, b2 + 8 + a - 3)
                            te.matmul(pb[a % 4][:, :],
                                      ckv_fm[:, kt * 128:(kt + 1) * 128],
                                      qlh[:, h01 * NTOK + qb * QB: h01 * NTOK + (qb + 1) * QB],
                                      start=True, stop=True).then_inc(s1.h, 1)
                            a += 1
                s1.v += 8 + ntile
                te.wait_ge(s2.h, b2 + 8 + ntile + 1)
                te.transpose(pbh[:, 0:128], scrb[:, 0:128], idb_s[:]).then_inc(s1.h, 1)
                te.wait_ge(s2.h, b2 + 8 + ntile + 2)
                te.transpose(pbh[:, 128:256], scrb[:, 0:128], idb_s[:]).then_inc(s1.h, 1)
                te.wait_ge(s2.h, b2 + 8 + ntile + 3)
                te.matmul(pb[4][:, 0:16], or_b[0:1, :], negmrow[0:1, 0:16],
                          start=True, stop=True).then_inc(s1.h, 1)
                s1.v += 3

            @blk.vector
            def _(ve):
                for n in range(8):
                    h01, qb = divmod(n, NQB)
                    ve.wait_ge(s1.h, b1 + n + 1)
                    ve.tensor_copy(qlh[:, h01 * NTOK + qb * QB: h01 * NTOK + (qb + 1) * QB],
                                   pb[n % 2][:, :]).then_inc(s2.h, 1)
                a = 0
                for qb in range(NQB):
                    for kt in akts[qb]:
                        for h01 in range(2):
                            slot = h01 * 4 + qb
                            ve.wait_ge(s1.h, b1 + 8 + a + 1)
                            if kt == 0:
                                ve.tensor_copy(mrun[:, slot * QB:(slot + 1) * QB],
                                               pb[a % 4][:, :]).then_inc(s2.h, 1)
                            else:
                                ve.tensor_tensor(mrun[:, slot * QB:(slot + 1) * QB],
                                                 pb[a % 4][:, :],
                                                 mrun[:, slot * QB:(slot + 1) * QB],
                                                 OP.max).then_inc(s2.h, 1)
                            a += 1
                s2.v += 8 + ntile
                for k in range(8):
                    bi = ve.reduce_max(permax[:, k:k + 1], mrun[:, k * QB:(k + 1) * QB], axis=AX.X)
                bi.then_inc(sv.h, 1)
                ve.wait_ge(sv.h, bv + 1)
                ve.tensor_scalar(scrb[:, 0:8], permax[:, 0:8], 0.125, None, OP.mult).then_inc(s2.h, 1)
                ve.wait_ge(s1.h, b1 + 8 + ntile + 1)
                ve.reduce_max(negm8[0:8, 0:1], pbh[0:8, 0:128], axis=AX.X, negate=True).then_inc(sv.h, 1)
                ve.wait_ge(sv.h, bv + 2)
                ve.tensor_copy(scrb[0:8, 0:1], negm8[0:8, 0:1]).then_inc(s2.h, 1)
                ve.wait_ge(s1.h, b1 + 8 + ntile + 2)
                ve.tensor_copy(negmrow[0:1, 0:16], pbh[0:1, 128:144]).then_inc(s2.h, 1)
                ve.wait_ge(s1.h, b1 + 8 + ntile + 3)
                ve.tensor_copy(biast[:, 0:8], pb[4][:, 0:8]).then_inc(sv.h, 1)
                ve.wait_ge(sv.h, bv + 3)
                for h01 in range(2):
                    for qb in range(NQB):
                        slot = h01 * 4 + qb
                        ve.tensor_scalar(biastx[:, slot * 64: slot * 64 + UB[qb]],
                                         ktpen_s[:, qb * 64: qb * 64 + UB[qb]],
                                         biast[:, slot:slot + 1], None, OP.add)
                for qb in range(NQB):
                    ve.tensor_scalar(mrun[:, qb * QB:(qb + 1) * QB], colmap_s[:, :],
                                     q0s_s[:, qb:qb + 1], None, OP.add)
                ve.engine_nop().then_inc(s2.h, 1)
                s2.v += 4
                sv.v += 3

        with nc.Block() as blk:
            b1, b2, b3, b4 = s1.v, s2.v, s3.v, s4.v
            tiles = []
            for qb in range(NQB):
                for kt in range(UB[qb]):
                    for h01 in range(2):
                        tiles.append((qb, kt, h01, MB[qb][0] <= kt < MB[qb][1]))
            ntl = len(tiles)
            pe_after_B = [0] * ntl
            pe_after_sum = [0] * ntl
            act_after = [0] * ntl
            pool_after = [0] * ntl
            epi_sum_free = {}
            epi_ctx_free = {}
            epi_out_free = {}
            pe_c = act_c = pool_c = dve_c = 0
            for t, (qb, kt, h01, mskd) in enumerate(tiles):
                act_c += 1
                act_after[t] = act_c
                if mskd:
                    pool_c += 1
                    pool_after[t] = pool_c
                if t + 1 == ntl or tiles[t + 1][0] != qb:
                    epi_sum_free[qb] = dve_c + 1
                    epi_ctx_free[qb] = (dve_c + 4, dve_c + 5)
                    epi_out_free[qb] = dve_c + 6
                    dve_c += 6
            for step in range(ntl + 1):
                if step < ntl:
                    pe_c += 1
                    pe_after_B[step] = pe_c
                u = step - 1 if step < ntl else ntl - 1
                if u >= 0:
                    pe_c += 2
                    pe_after_sum[u] = pe_c
                    if u + 1 == ntl or tiles[u + 1][0] != tiles[u][0]:
                        pe_c += 4

            @blk.tensor
            def _(te):
                pe_n = [0]

                def emit_pv(u):
                    qb, kt, h01, mskd = tiles[u]
                    if mskd:
                        te.wait_ge(s4.h, b4 + pool_after[u])
                    else:
                        te.wait_ge(s3.h, b3 + act_after[u])
                    if kt == 0 and qb > 0:
                        te.wait_ge(s2.h, b2 + epi_ctx_free[qb - 1][h01])
                    rhs = ptm[u % 2] if mskd else pt[u % 2]
                    te.matmul(pb[h01][:, :], ckv_tm[:, kt * 128:(kt + 1) * 128], rhs,
                              start=(kt == 0), stop=(kt == UB[qb] - 1)).then_inc(s1.h, 1)
                    if kt == 0 and qb > 0 and h01 == 0:
                        te.wait_ge(s2.h, b2 + epi_sum_free[qb - 1])
                    te.matmul(pb[2][0:1, :] if h01 == 0 else pb[2][32:33, :],
                              oc_b[:, 0:1], rhs,
                              start=(kt == 0), stop=(kt == UB[qb] - 1),
                              tile_position=(0, 32) if h01 == 1 else None).then_inc(s1.h, 1)
                    pe_n[0] += 2
                    if u + 1 == ntl or tiles[u + 1][0] != qb:
                        te.wait_ge(s2.h, b2 + epi_sum_free[qb] + 0)
                        te.matmul(pb[3][:, :], or_f[0:1, :], recipf[0:1, 0:QB],
                                  start=True, stop=True).then_inc(s1.h, 1)
                        te.wait_ge(s2.h, b2 + epi_sum_free[qb] + 1)
                        te.matmul(pb[3][:, :], or_f[0:1, :], recipf[0:1, QB:2 * QB],
                                  start=True, stop=True).then_inc(s1.h, 1)
                        te.wait_ge(s2.h, b2 + epi_ctx_free[qb][0])
                        te.matmul(pb[6][0:64, :], v1_b[:, g0 * 64:(g0 + 1) * 64], ctxe[0],
                                  start=True, stop=True).then_inc(s1.h, 1)
                        te.wait_ge(s2.h, b2 + epi_ctx_free[qb][1])
                        te.matmul(pb[6][64:128, :], v1_b[:, g1 * 64:(g1 + 1) * 64], ctxe[1],
                                  start=True, stop=True,
                                  tile_position=(0, 64)).then_inc(s1.h, 1)
                        pe_n[0] += 4

                for t, (qb, kt, h01, mskd) in enumerate(tiles):
                    if t >= 2:
                        te.wait_ge(s3.h, b3 + act_after[t - 2])
                    te.matmul(pb[4 + t % 2][:, :],
                              ckv_fm[:, kt * 128:(kt + 1) * 128],
                              qlh[:, h01 * NTOK + qb * QB: h01 * NTOK + (qb + 1) * QB],
                              start=True, stop=True).then_inc(s1.h, 1)
                    pe_n[0] += 1
                    if t >= 1:
                        emit_pv(t - 1)
                emit_pv(ntl - 1)
                s1.v += pe_n[0]

            @blk.scalar
            def _(sc):
                for t, (qb, kt, h01, mskd) in enumerate(tiles):
                    slot = h01 * 4 + qb
                    sc.wait_ge(s1.h, b1 + pe_after_B[t])
                    sc.activation(pt[t % 2], pb[4 + t % 2][:, :], AF.Exp,
                                  bias=biastx[:, slot * 64 + kt: slot * 64 + kt + 1],
                                  scale=0.125).then_inc(s3.h, 1)
                s3.v += ntl

            @blk.vector
            def _(ve):
                for t, (qb, kt, h01, mskd) in enumerate(tiles):
                    if mskd:
                        ve.wait_ge(s3.h, b3 + act_after[t])
                        ve.scalar_tensor_tensor(ptm[t % 2], mrun[:, qb * QB:(qb + 1) * QB],
                                                rowmap_s[:, kt:kt + 1], pt[t % 2],
                                                OP.is_ge, OP.mult).then_inc(s4.h, 1)
                    if t + 1 == ntl or tiles[t + 1][0] != qb:
                        ve.wait_ge(s1.h, b1 + pe_after_sum[t])
                        ve.reciprocal(recipf[0:1, 0:QB], pb[2][0:1, :])
                        ve.reciprocal(recipf[0:1, QB:2 * QB], pb[2][32:33, :]).then_inc(sv.h, 1)
                        ve.wait_ge(sv.h, sv.v + 1)
                        sv.v += 1
                        ve.engine_nop().then_inc(s2.h, 1)
                        mm_base = b1 + pe_after_sum[t]
                        ve.wait_ge(s1.h, mm_base + 1)
                        ve.tensor_copy(rects.tensor if False else rects[0:128, 0:QB], pb[3][:, :]).then_inc(s2.h, 1)
                        ve.wait_ge(s1.h, mm_base + 2)
                        ve.tensor_copy(rects[0:128, QB:2 * QB], pb[3][:, :]).then_inc(s2.h, 1)
                        ve.tensor_tensor(ctxe[0], pb[0][:, :], rects[0:128, 0:QB], OP.mult).then_inc(s2.h, 1)
                        ve.tensor_tensor(ctxe[1], pb[1][:, :], rects[0:128, QB:2 * QB], OP.mult).then_inc(s2.h, 1)
                        ve.wait_ge(s1.h, mm_base + 4)
                        ve.tensor_tensor(xres[:, hp * NTOK + qb * QB: hp * NTOK + (qb + 1) * QB],
                                         pb[6][:, :],
                                         xres[:, hp * NTOK + qb * QB: hp * NTOK + (qb + 1) * QB],
                                         OP.add).then_inc(s2.h, 1)
                s2.v += dve_c
                s4.v += pool_c

    C.pop()
    Dd.pop()
    E.pop()

    if upto == "P5":
        tap_fm("xattn", xres[:, :], dtype=F32)
        emit_store_y()
        return nc, tap_tensors

    A.pop()
    A.push()
    xn2 = A.take(NCH * NTOK)
    emit_ln(xres[:, :], xn2)

    if upto == "P6":
        tap_fm("xn2", xn2, dtype=BF16)
        emit_store_y()
        return nc, tap_tensors

    Dd.push()
    stg2 = Dd.take(2 * 128)
    with nc.Block() as blk:
        b1, b2, bd = s1.v, s2.v, sd.v

        @blk.tensor
        def _(te):
            n = 0
            for i in range(16):
                for c in range(NCH):
                    if n >= 1:
                        te.wait_ge(s2.h, b2 + n)
                    te.transpose(pbh[:, (n % 2) * 128:(n % 2) * 128 + 128],
                                 xn2[:, c * NTOK + i * 128: c * NTOK + (i + 1) * 128],
                                 idb_s[:]).then_inc(s1.h, 1)
                    n += 1
            s1.v += 128

        @blk.vector
        def _(ve):
            n = 0
            for i in range(16):
                for c in range(NCH):
                    ve.wait_ge(s1.h, b1 + n + 1)
                    if n >= 2:
                        ve.wait_ge(sd.h, bd + 16 * (n - 1))
                    ve.tensor_copy(stg2[:, (n % 2) * 128:(n % 2) * 128 + 128],
                                   pbh[:, (n % 2) * 128:(n % 2) * 128 + 128]).then_inc(s2.h, 1)
                    n += 1
            s2.v += 128

        @blk.sync
        def _(sy):
            n = 0
            for i in range(16):
                for c in range(NCH):
                    sy.wait_ge(s2.h, b2 + n + 1)
                    sy.dma_start(out=xm_tm_dram.ap()[i * 128:(i + 1) * 128, c * 128:(c + 1) * 128],
                                 in_=stg2[:, (n % 2) * 128:(n % 2) * 128 + 128]).then_inc(sd.h, 16)
                    n += 1
            sd.v += 128 * 16
    Dd.pop()

    if upto == "P7a":
        emit_store_y()
        return nc, tap_tensors

    Bb.pop()
    Bb.push()
    sh_pre = Bb.take(4 * NTOK)
    sh = Bb.take(4 * NTOK)
    C.off = 0
    C.marks.clear()
    C.push()
    sw1T_b = C.take(NCH * MOE_H)
    sw2T_b = C.take(NCH * MOE_H)
    abb2 = C.take(2 * NTOK)
    sqb = C.take(512)
    t1s = C.take(512)
    emit_wload(sw1T, sw1T_b, NCH, MOE_H)
    emit_wload(sw2T, sw2T_b, NCH, MOE_H)
    sb12_s = nc.alloc_sbuf_tensor("sb12_s", [128, 8], F32)
    slng_s = nc.alloc_sbuf_tensor("slng_s", [128, 8], F32)
    with nc.Block() as blk:
        @blk.sync
        def _(sy):
            for hh in range(4):
                sy.dma_start(out=sb12_s[:, hh:hh + 1], in_=sb1.ap()[hh * 128:(hh + 1) * 128, :]).then_inc(sd.h, 16)
                sy.dma_start(out=sb12_s[:, 4 + hh:5 + hh], in_=sb2.ap()[hh * 128:(hh + 1) * 128, :]).then_inc(sd.h, 16)
                sy.dma_start(out=slng_s[:, hh:hh + 1], in_=s_ln_g.ap()[hh * 128:(hh + 1) * 128, :]).then_inc(sd.h, 16)
                sy.dma_start(out=slng_s[:, 4 + hh:5 + hh], in_=s_ln_b.ap()[hh * 128:(hh + 1) * 128, :]).then_inc(sd.h, 16)
            sd.v += 16 * 16
            sy.wait_ge(sd.h, sd.v)

    with nc.Block() as blk:
        b1, b2, b3 = s1.v, s2.v, s3.v

        @blk.tensor
        def _(te):
            n = 0
            for hh in range(4):
                for qb in range(NQB):
                    if n > 0:
                        te.wait_ge(s2.h, b2 + n)
                    for c in range(NCH):
                        te.matmul(pb[0][:, :], sw1T_b[:, c * MOE_H + hh * 128: c * MOE_H + (hh + 1) * 128],
                                  xn2[:, c * NTOK + qb * QB: c * NTOK + qb * QB + QB],
                                  start=(c == 0), stop=(c == NCH - 1)).then_inc(s1.h, 1)
                    for c in range(NCH):
                        te.matmul(pb[1][:, :], sw2T_b[:, c * MOE_H + hh * 128: c * MOE_H + (hh + 1) * 128],
                                  xn2[:, c * NTOK + qb * QB: c * NTOK + qb * QB + QB],
                                  start=(c == 0), stop=(c == NCH - 1)).then_inc(s1.h, 1)
                    n += 1
            s1.v += 16 * 16

        @blk.scalar
        def _(sc):
            n = 0
            for hh in range(4):
                for qb in range(NQB):
                    sc.wait_ge(s1.h, b1 + 16 * n + 8)
                    if n > 0:
                        sc.wait_ge(s2.h, b2 + n)
                    sc.activation(t1s[:, :], pb[0][:, :], AF.Silu,
                                  bias=sb12_s[:, hh:hh + 1]).then_inc(s3.h, 1)
                    n += 1
            s3.v += 16

        @blk.vector
        def _(ve):
            n = 0
            for hh in range(4):
                for qb in range(NQB):
                    ve.wait_ge(s1.h, b1 + 16 * (n + 1))
                    ve.wait_ge(s3.h, b3 + n + 1)
                    ve.scalar_tensor_tensor(sh_pre[:, hh * NTOK + qb * QB: hh * NTOK + qb * QB + QB],
                                            pb[1][:, :], sb12_s[:, 4 + hh:5 + hh], t1s[:, :],
                                            OP.add, OP.mult).then_inc(s2.h, 1)
                    n += 1
            s2.v += 16

    E.push()
    r0t = E.take(NTOK)
    r1t = E.take(NTOK)
    r2t = E.take(NTOK)
    sqq = E.take(512)
    sqq2 = E.take(512)
    sr0, sr1, sr2 = r0t[0:1, :], r1t[0:1, :], r2t[0:1, :]
    with nc.Block() as blk:
        b1, b2, b3 = s1.v, s2.v, s3.v

        @blk.scalar
        def _(sc):
            for qb in range(NQB):
                for hh in range(4):
                    i = qb * 4 + hh
                    if i > 0:
                        sc.wait_ge(s1.h, b1 + 2 * i)
                    sc.activation(sqb[:, :], sh_pre[:, hh * NTOK + qb * QB: hh * NTOK + qb * QB + QB],
                                  AF.Square).then_inc(s3.h, 1)
            s3.v += 16

        @blk.tensor
        def _(te):
            for qb in range(NQB):
                for hh in range(4):
                    i = qb * 4 + hh
                    if hh == 0 and qb > 0:
                        te.wait_ge(s2.h, b2 + 2 * qb)
                    te.matmul(pb[0][0:1, :], oc_b[:, 0:1],
                              sh_pre[:, hh * NTOK + qb * QB: hh * NTOK + qb * QB + QB],
                              start=(hh == 0), stop=(hh == 3)).then_inc(s1.h, 1)
                    te.wait_ge(s3.h, b3 + i + 1)
                    te.matmul(pb[1][0:1, :], oc_b[:, 0:1], sqb[:, :],
                              start=(hh == 0), stop=(hh == 3)).then_inc(s1.h, 1)
            s1.v += 32

        @blk.vector
        def _(ve):
            for qb in range(NQB):
                ve.wait_ge(s1.h, b1 + 8 * (qb + 1))
                ve.tensor_copy(sr0[0:1, qb * QB:(qb + 1) * QB], pb[0][0:1, :]).then_inc(s2.h, 1)
                ve.tensor_copy(sr1[0:1, qb * QB:(qb + 1) * QB], pb[1][0:1, :]).then_inc(s2.h, 1)
            s2.v += 8

    ddh = float(MOE_H)
    with nc.Block() as blk:
        @blk.vector
        def _(ve):
            ve.tensor_scalar(sr2, sr1, 1.0 / ddh, float(EPS), OP.mult, OP.add)
    with nc.Block() as blk:
        @blk.vector
        def _(ve):
            ve.tensor_scalar(sr1, sr0, 1.0 / ddh, None, OP.mult)
    with nc.Block() as blk:
        @blk.vector
        def _(ve):
            ve.tensor_tensor(sr0, sr1, sr1, OP.mult)
    with nc.Block() as blk:
        @blk.vector
        def _(ve):
            ve.tensor_tensor(sr0, sr2, sr0, OP.subtract)
    with nc.Block() as blk:
        @blk.scalar
        def _(sc):
            sc.activation(sr2, sr0, AF.Sqrt)
    with nc.Block() as blk:
        @blk.vector
        def _(ve):
            ve.reciprocal(sr0, sr2)
    with nc.Block() as blk:
        @blk.vector
        def _(ve):
            ve.tensor_tensor(sr2, sr1, sr0, OP.mult)
    with nc.Block() as blk:
        @blk.vector
        def _(ve):
            ve.tensor_copy(abb2[0:1, 0:NTOK], sr0)
            ve.tensor_copy(abb2[0:1, NTOK:2 * NTOK], sr2)
    with nc.Block() as blk:
        b1, b2 = s1.v, s2.v

        @blk.tensor
        def _(te):
            for qb in range(NQB):
                if qb > 0:
                    te.wait_ge(s2.h, b2 + qb)
                te.matmul(pb[2][:, :], or_b[0:1, :], abb2[0:1, qb * QB:(qb + 1) * QB],
                          start=True, stop=True).then_inc(s1.h, 1)
                te.matmul(pb[3][:, :], or_b[0:1, :], abb2[0:1, NTOK + qb * QB: NTOK + (qb + 1) * QB],
                          start=True, stop=True).then_inc(s1.h, 1)
            s1.v += 8

        @blk.vector
        def _(ve):
            for qb in range(NQB):
                ve.wait_ge(s1.h, b1 + 2 * qb + 2)
                for hh in range(4):
                    sl = slice(hh * NTOK + qb * QB, hh * NTOK + qb * QB + QB)
                    ve.tensor_tensor(sqq[:, :], sh_pre[:, sl], pb[2][:, :], OP.mult)
                    ve.tensor_tensor(sqq2[:, :], sqq[:, :], pb[3][:, :], OP.subtract)
                    bi = ve.tensor_scalar(sh[:, sl], sqq2[:, :], slng_s[:, hh:hh + 1],
                                          slng_s[:, 4 + hh:5 + hh], OP.mult, OP.add)
                    if hh == 3:
                        bi.then_inc(s2.h, 1)
            s2.v += NQB
    E.pop()
    C.pop()

    if upto == "P7g":
        tap_fm("sh", sh, nch=4, width=NTOK, dtype=BF16)
        emit_store_y()
        return nc, tap_tensors

    lidx = Si.take(16)
    itmp = Si.take(16 * 5)
    ib_t = itmp[:, 0:16]
    it_t = itmp[:, 16:32]
    gs_t = itmp[:, 32:48]
    w_t = itmp[:, 48:64]
    e_t = itmp[:, 64:80]
    with nc.Block() as blk:
        @blk.sync
        def _(sy):
            for i in range(16):
                sy.dma_start(out=ib_t[:, i:i + 1], in_=bass.AP(idx_b, i * 128, [[1, 128], [1, 1]])).then_inc(sd.h, 16)
                sy.dma_start(out=it_t[:, i:i + 1], in_=bass.AP(idx_t, i * 128, [[1, 128], [1, 1]])).then_inc(sd.h, 16)
            sd.v += 32 * 16
            sy.wait_ge(sd.h, sd.v)

    def vstep(fn):
        with nc.Block() as blk:
            @blk.vector
            def _(ve):
                fn(ve)

    vstep(lambda ve: ve.tensor_scalar(gs_t, ib_t, 16, None, OP.mult))
    vstep(lambda ve: ve.tensor_scalar(w_t, it_t, 9, None, OP.arith_shift_right))
    vstep(lambda ve: ve.tensor_tensor(e_t, gs_t, w_t, OP.add))
    vstep(lambda ve: ve.tensor_scalar(w_t, w_t, 9, None, OP.arith_shift_left))
    vstep(lambda ve: ve.tensor_tensor(w_t, it_t, w_t, OP.subtract))
    fscr = E.take(16 * 4)
    gsf = fscr[:, 0:16]
    accf = fscr[:, 16:32]
    ef = fscr[:, 32:48]
    sumf = fscr[:, 48:64]
    vstep(lambda ve: ve.tensor_copy(gsf, e_t))
    vstep(lambda ve: ve.tensor_copy(accf, w_t))
    vstep(lambda ve: ve.tensor_scalar(sumf, accf, 0.0, None, OP.mult))
    for j in range(4):
        def fj(ve, j=j):
            ve.tensor_scalar(ef, gsf, metaf_s[:, j:j + 1], None, OP.is_equal)
        vstep(fj)
        def fj2(ve, j=j):
            ve.tensor_tensor(sumf, ef, sumf, OP.add)
        vstep(fj2)
        if j > 0:
            def fj3(ve, j=j):
                ve.tensor_scalar(ef, ef, float(j * 512), None, OP.mult)
            vstep(fj3)
            def fj4(ve, j=j):
                ve.tensor_tensor(accf, ef, accf, OP.add)
            vstep(fj4)
    vstep(lambda ve: ve.tensor_scalar(ef, sumf, -float(1 << 20), float(1 << 20), OP.mult, OP.add))
    vstep(lambda ve: ve.tensor_tensor(accf, ef, accf, OP.add))
    vstep(lambda ve: ve.tensor_copy(lidx, accf))
    lidxc = Si.take(16)
    vstep(lambda ve: ve.tensor_scalar(lidxc, lidx, NTOK, None, OP.min))

    Dd.push()
    gbuf = Dd.take(1024)
    gbuf2 = Dd.take(1024)
    E.push()
    zpadf = E.take(1024)
    with nc.Block() as blk:
        @blk.vector
        def _(ve):
            ve.memset(gbuf2[:, :], 0.0)
            ve.memset(zpadf[:, :], 0.0).then_inc(s2.h, 1)
            s2.v += 1

        @blk.sync
        def _(sy):
            sy.wait_ge(s2.h, s2.v)
            sy.dma_start(out=xm_tm_dram.ap()[NTOK:NTOK + 128, :],
                         in_=gbuf2[:, :]).then_inc(sd.h, 16)
            sy.dma_start(out=delta_all.ap()[MOE_M:MOE_M + 128, :],
                         in_=zpadf[:, :]).then_inc(sd.h, 16)
            sd.v += 32
            sy.wait_ge(sd.h, sd.v)
    E.pop()
    with nc.Block() as blk:
        b2, bg, bd = s2.v, sg.v, sd.v

        @blk.gpsimd
        def _(gp):
            for i in range(16):
                if i > 0:
                    gp.wait_ge(s2.h, b2 + i)
                gp.indirect_dma_start(
                    gbuf[:, :], None,
                    xm_tm_dram.ap(), IndirectOffsetOnAxis(lidxc[:, i:i + 1], 0),
                ).then_inc(sg.h, 16)
                sg.v += 16
                gp.wait_ge(sg.h, sg.v)

        @blk.vector
        def _(ve):
            for i in range(16):
                ve.wait_ge(sg.h, bg + 16 * (i + 1))
                if i > 0:
                    ve.wait_ge(sd.h, bd + 16 * i)
                ve.tensor_scalar(gbuf2[:, :], gbuf[:, :], sumf[:, i:i + 1],
                                 None, OP.mult).then_inc(s2.h, 1)
            s2.v += 16

        @blk.sync
        def _(sy):
            for i in range(16):
                sy.wait_ge(s2.h, b2 + i + 1)
                sy.dma_start(out=xmoe_contrib.ap()[i * 128:(i + 1) * 128, :],
                             in_=gbuf2[:, :]).then_inc(sd.h, 16)
            sd.v += 16 * 16
            sy.wait_ge(sd.h, sd.v)
    Dd.pop()

    groups8 = [[0, 1, 2, 3, 4, 5, 6, 7]]
    with nc.Block() as blk:
        @blk.gpsimd
        def _(gp):
            gp.collective_compute("AllReduce", OP.add, replica_groups=groups8,
                                  ins=[xmoe_contrib.ap()], outs=[xmoe_all.ap()]).then_inc(sco.h, 1)
            sco.v += 1
            gp.wait_ge(sco.h, sco.v)

    if upto == "P7c":
        tap_copy_dram("xmoe", xmoe_all, [MOE_M, D], BF16)
        emit_store_y()
        return nc, tap_tensors

    xm_fm = C.take(NCH * MOE_M)
    Dd.push()
    stg3 = Dd.take(2 * 1024)
    with nc.Block() as blk:
        b1, b2, bd = s1.v, s2.v, sd.v

        @blk.sync
        def _(sy):
            for i in range(16):
                if i >= 2:
                    sy.wait_ge(s1.h, b1 + 8 * (i - 1))
                sy.dma_start(out=stg3[:, (i % 2) * 1024:(i % 2) * 1024 + 1024],
                             in_=xmoe_all.ap()[i * 128:(i + 1) * 128, :]).then_inc(sd.h, 16)
            sd.v += 16 * 16

        @blk.tensor
        def _(te):
            n = 0
            for i in range(16):
                te.wait_ge(sd.h, bd + 16 * (i + 1))
                for c in range(NCH):
                    if n >= 1:
                        te.wait_ge(s2.h, b2 + n)
                    te.transpose(pbh[:, (n % 2) * 128:(n % 2) * 128 + 128],
                                 stg3[:, (i % 2) * 1024 + c * 128: (i % 2) * 1024 + (c + 1) * 128],
                                 idb_s[:]).then_inc(s1.h, 1)
                    n += 1
            s1.v += 128

        @blk.vector
        def _(ve):
            n = 0
            for i in range(16):
                for c in range(NCH):
                    ve.wait_ge(s1.h, b1 + n + 1)
                    ve.tensor_copy(xm_fm[:, c * MOE_M + i * 128: c * MOE_M + (i + 1) * 128],
                                   pbh[:, (n % 2) * 128:(n % 2) * 128 + 128]).then_inc(s2.h, 1)
                    n += 1
            s2.v += 128
    Dd.pop()

    A.pop()
    A.push()
    vb8t = A.take(MOE_M)
    gw_b = A.take(NCH * 16)
    ebrow_b = A.take(2 * MOE_H + D)
    sw3T_b = A.take(4 * D)
    ew3T_b = A.take(4 * D)
    E.push()
    sigf = E.take(MOE_M)
    gmisc = E.take(128)
    swv_t = E.take(16)
    ftile_t = E.take(16)
    dbuf = E.take(1024)
    selg_b = E.take(512)
    selb_b = E.take(512)
    erow_stg = E.take(1024)
    scb = E.take(512)
    sb3_s = nc.alloc_sbuf_tensor("sb3_s", [128, 8], F32)
    gb_s = nc.alloc_sbuf_tensor("gb_s", [16, 1], F32)
    jcols_s = nc.alloc_sbuf_tensor("jcols_s", [128, 16], F32)
    bigc_s = nc.alloc_sbuf_tensor("bigc_s", [128, 1], F32)
    invx = nc.alloc_sbuf_tensor("invx", [128, 16], I32)
    invf = nc.alloc_sbuf_tensor("invf", [128, 16], F32)
    expb_s = nc.alloc_sbuf_tensor("expb_s", [8, 1], F32)

    emit_wload(gwT, gw_b, NCH, 16)
    emit_wload(sw3T, sw3T_b, 4, D)
    emit_wload(ew3T, ew3T_b, 4, D)
    ew1T_b = Bb.t[:, 0:NCH * MOE_H]
    ew2T_b = Bb.t[:, NCH * MOE_H:2 * NCH * MOE_H]
    emit_wload(ew1T, ew1T_b, NCH, MOE_H)
    emit_wload(ew2T, ew2T_b, NCH, MOE_H)

    with nc.Block() as blk:
        bd = sd.v

        @blk.sync
        def _(sy):
            sy.dma_start(out=gb_s[:], in_=gb.ap()).then_inc(sd.h, 16)
            sy.dma_start(out=expb_s[:], in_=exp_bias.ap()).then_inc(sd.h, 16)
            sy.dma_start(out=erow_stg[0:1, 0:512], in_=eb1.ap()).then_inc(sd.h, 16)
            sy.dma_start(out=erow_stg[0:1, 512:1024], in_=eb2.ap()).then_inc(sd.h, 16)
            sy.dma_start(out=dbuf[0:1, 0:1024], in_=eb3.ap()).then_inc(sd.h, 16)
            sy.dma_start(out=selg_b[0:1, 0:512], in_=sel_ln_g.ap()).then_inc(sd.h, 16)
            sy.dma_start(out=selb_b[0:1, 0:512], in_=sel_ln_b.ap()).then_inc(sd.h, 16)
            for oc in range(8):
                sy.dma_start(out=sb3_s[:, oc:oc + 1],
                             in_=sb3.ap()[oc * 128:(oc + 1) * 128, :]).then_inc(sd.h, 16)
            sd.v += 15 * 16
            sy.wait_ge(sd.h, sd.v)

        @blk.vector
        def _(ve):
            ve.wait_ge(sd.h, bd + 15 * 16)
            ve.tensor_copy(ebrow_b[0:1, 0:1024], erow_stg[0:1, 0:1024])
            ve.tensor_copy(ebrow_b[0:1, 1024:2048], dbuf[0:1, 0:1024])

    with nc.Block() as blk:
        b1 = s1.v

        @blk.tensor
        def _(te):
            te.matmul(pb[2][:, :], or_f[0:1, :], selg_b[0:1, 0:512], start=True, stop=True).then_inc(s1.h, 1)
            te.matmul(pb[3][:, :], or_f[0:1, :], selb_b[0:1, 0:512], start=True, stop=True).then_inc(s1.h, 1)
            s1.v += 2

        @blk.vector
        def _(ve):
            ve.wait_ge(s1.h, b1 + 2)
            ve.tensor_copy(selg_b[:, :], pb[2][:, :])
            ve.tensor_copy(selb_b[:, :], pb[3][:, :])

    with nc.Block() as blk:
        b1, b3 = s1.v, s3.v

        @blk.tensor
        def _(te):
            for qb in range(NQB):
                if qb >= 2:
                    te.wait_ge(s3.h, b3 + qb - 1)
                for c in range(NCH):
                    te.matmul(pb[qb % 2][0:16, :], gw_b[:, c * 16:(c + 1) * 16],
                              xm_fm[:, c * MOE_M + qb * QB: c * MOE_M + qb * QB + QB],
                              start=(c == 0), stop=(c == NCH - 1)).then_inc(s1.h, 1)
            s1.v += 32

        @blk.scalar
        def _(sc):
            for qb in range(NQB):
                sc.wait_ge(s1.h, b1 + 8 * (qb + 1))
                sc.activation(sigf[0:16, qb * QB:(qb + 1) * QB], pb[qb % 2][0:16, :],
                              AF.Sigmoid, bias=gb_s[0:16, 0:1]).then_inc(s3.h, 1)
            s3.v += 4

    vf = E.take(MOE_M)
    with nc.Block() as blk:
        @blk.vector
        def _(ve):
            ve.tensor_copy(vf[0:16, :], sigf[0:16, :]).then_inc(sv.h, 1)
            sv.v += 1
            ve.wait_ge(sv.h, sv.v)
            ve.tensor_copy(vf[32:40, :], sigf[0:8, :]).then_inc(sv.h, 1)
            sv.v += 1
            ve.wait_ge(sv.h, sv.v)
            ve.tensor_scalar(vf[0:8, :], sigf[0:8, :], expb_s[0:8, 0:1], None, OP.add)

    gt = gmisc
    with nc.Block() as blk:
        b1, b2, b3 = s1.v, s2.v, s3.v

        @blk.tensor
        def _(te):
            for i in range(16):
                if i >= 1:
                    te.wait_ge(s2.h, b2 + 3 * (i - 1) + 1)
                te.transpose(pb[5][:, 0:128],
                             vf[:, i * 128:(i + 1) * 128], idf_s[:]).then_inc(s1.h, 1)
            s1.v += 16

        @blk.vector
        def _(ve):
            for i in range(16):
                ve.wait_ge(s1.h, b1 + i + 1)
                ve.tensor_copy(gt[:, 0:48], pb[5][:, 0:48]).then_inc(s2.h, 1)
                ve.engine_nop().then_inc(sv.h, 1)
                ve.wait_ge(sv.h, sv.v + 1); sv.v += 1
                ve.reduce_max(gt[:, 48:49], gt[:, 0:8], axis=AX.X).then_inc(sv.h, 1)
                ve.wait_ge(sv.h, sv.v + 1); sv.v += 1
                ve.tensor_scalar(gt[:, 49:57], gt[:, 0:8], gt[:, 48:49], None, OP.is_equal).then_inc(sv.h, 1)
                ve.wait_ge(sv.h, sv.v + 1); sv.v += 1
                ve.tensor_tensor(gt[:, 57:65], gt[:, 49:57], gt[:, 32:40], OP.mult).then_inc(sv.h, 1)
                ve.wait_ge(sv.h, sv.v + 1); sv.v += 1
                ve.reduce_sum(gt[:, 65:66], gt[:, 57:65], axis=AX.X).then_inc(sv.h, 1)
                ve.wait_ge(sv.h, sv.v + 1); sv.v += 1
                ve.tensor_tensor(gt[:, 66:74], gt[:, 49:57], onehot_s[:, :], OP.mult).then_inc(sv.h, 1)
                ve.wait_ge(sv.h, sv.v + 1); sv.v += 1
                ve.reduce_sum(gt[:, 74:75], gt[:, 66:74], axis=AX.X).then_inc(sv.h, 1)
                ve.wait_ge(sv.h, sv.v + 1); sv.v += 1
                ve.tensor_tensor(gt[:, 75:76], gt[:, 8:9], gt[:, 65:66], OP.subtract).then_inc(s2.h, 1)
                ve.wait_ge(s3.h, b3 + 2 * (i + 1))
                ve.tensor_tensor(ftile_t[:, i:i + 1], gt[:, 76:77], gt[:, 74:75], OP.mult).then_inc(s2.h, 1)
            s2.v += 48

        @blk.scalar
        def _(sc):
            for i in range(16):
                sc.wait_ge(s2.h, b2 + 3 * i + 2)
                sc.activation(swv_t[:, i:i + 1], gt[:, 75:76], AF.Sigmoid).then_inc(s3.h, 1)
                sc.activation(gt[:, 76:77], gt[:, 75:76], AF.Sigmoid, scale=-1.0).then_inc(s3.h, 1)
            s3.v += 32

    if upto == "P7f":
        tap_fm("gating", swv_t, nch=1, width=16, dtype=F32)
        tap_fm("gatingf", ftile_t, nch=1, width=16, dtype=F32)
        emit_store_y()
        return nc, tap_tensors

    Dd.push()
    t4 = Dd.take(512)
    z1 = Dd.take(512)
    sqb4 = Dd.take(512)
    z2 = Dd.take(512)
    z3 = Dd.take(512)
    z4 = Dd.take(512)
    zTt = Dd.take(512)
    est = E.take(8)
    for i in range(16):
        with nc.Block() as blk:
            b1, b2, b3, bv = s1.v, s2.v, s3.v, sv.v

            @blk.tensor
            def _(te):
                for c in range(NCH):
                    te.matmul(pb[0][:, :], xm_fm[:, c * MOE_M + i * 128: c * MOE_M + (i + 1) * 128],
                              ew1T_b[:, c * MOE_H:(c + 1) * MOE_H],
                              start=(c == 0), stop=False).then_inc(s1.h, 1)
                te.matmul(pb[0][:, :], or_b[0:1, :], ebrow_b[0:1, 0:512],
                          start=False, stop=True).then_inc(s1.h, 1)
                for c in range(NCH):
                    te.matmul(pb[1][:, :], xm_fm[:, c * MOE_M + i * 128: c * MOE_M + (i + 1) * 128],
                              ew2T_b[:, c * MOE_H:(c + 1) * MOE_H],
                              start=(c == 0), stop=False).then_inc(s1.h, 1)
                te.matmul(pb[1][:, :], or_b[0:1, :], ebrow_b[0:1, 512:1024],
                          start=False, stop=True).then_inc(s1.h, 1)
                s1.v += 18
                for k in range(4):
                    te.wait_ge(s2.h, b2 + 11 + k)
                    te.transpose(pbh[:, (k % 2) * 128:(k % 2) * 128 + 128],
                                 z3[:, k * 128:(k + 1) * 128], idb_s[:]).then_inc(s1.h, 1)
                s1.v += 4
                te.wait_ge(s2.h, b2 + 15)
                for oh in range(2):
                    for k in range(4):
                        te.matmul(pb[2 + oh][:, :], zTt[:, k * 128:(k + 1) * 128],
                                  ew3T_b[:, k * D + oh * 512: k * D + oh * 512 + 512],
                                  start=(k == 0), stop=False).then_inc(s1.h, 1)
                    te.matmul(pb[2 + oh][:, :], or_b[0:1, :],
                              ebrow_b[0:1, 1024 + oh * 512: 1024 + (oh + 1) * 512],
                              start=False, stop=True).then_inc(s1.h, 1)
                s1.v += 10

            @blk.scalar
            def _(sc):
                sc.wait_ge(s1.h, b1 + 9)
                sc.activation(t4[:, :], pb[0][:, :], AF.Silu).then_inc(s3.h, 1)
                sc.wait_ge(s2.h, b2 + 1)
                sc.activation(sqb4[:, :], z1[:, :], AF.Square).then_inc(s3.h, 1)
                sc.wait_ge(s2.h, b2 + 7)
                sc.activation(est[:, 6:7], est[:, 5:6], AF.Sqrt).then_inc(s3.h, 1)
                s3.v += 3

            @blk.vector
            def _(ve):
                ve.wait_ge(s1.h, b1 + 18)
                ve.wait_ge(s3.h, b3 + 1)
                ve.tensor_tensor(z1[:, :], t4[:, :], pb[1][:, :], OP.mult).then_inc(s2.h, 1)
                ve.reduce_sum(est[:, 1:2], z1[:, :], axis=AX.X).then_inc(s2.h, 1)
                ve.wait_ge(s2.h, b2 + 2)
                ve.tensor_scalar(est[:, 2:3], est[:, 1:2], 1.0 / 512.0, None, OP.mult).then_inc(s2.h, 1)
                ve.wait_ge(s3.h, b3 + 2)
                ve.reduce_sum(est[:, 0:1], sqb4[:, :], axis=AX.X).then_inc(s2.h, 1)
                ve.wait_ge(s2.h, b2 + 4)
                ve.tensor_scalar(est[:, 3:4], est[:, 0:1], 1.0 / 512.0, float(EPS), OP.mult, OP.add).then_inc(s2.h, 1)
                ve.wait_ge(s2.h, b2 + 5)
                ve.tensor_tensor(est[:, 4:5], est[:, 2:3], est[:, 2:3], OP.mult).then_inc(s2.h, 1)
                ve.wait_ge(s2.h, b2 + 6)
                ve.tensor_tensor(est[:, 5:6], est[:, 3:4], est[:, 4:5], OP.subtract).then_inc(s2.h, 1)
                ve.wait_ge(s3.h, b3 + 3)
                ve.reciprocal(est[:, 7:8], est[:, 6:7]).then_inc(s2.h, 1)
                ve.wait_ge(s2.h, b2 + 8)
                ve.tensor_scalar(z2[:, :], z1[:, :], est[:, 2:3], est[:, 7:8],
                                 OP.subtract, OP.mult).then_inc(s2.h, 1)
                ve.wait_ge(s2.h, b2 + 9)
                ve.tensor_tensor(z4[:, :], z2[:, :], selg_b[:, :], OP.mult).then_inc(s2.h, 1)
                ve.wait_ge(s2.h, b2 + 10)
                ve.tensor_tensor(z3[:, :], z4[:, :], selb_b[:, :], OP.add).then_inc(s2.h, 1)
                for k in range(4):
                    ve.wait_ge(s1.h, b1 + 18 + k + 1)
                    ve.tensor_copy(zTt[:, k * 128:(k + 1) * 128],
                                   pbh[:, (k % 2) * 128:(k % 2) * 128 + 128]).then_inc(s2.h, 1)
                for oh in range(2):
                    ve.wait_ge(s1.h, b1 + 22 + 5 * (oh + 1))
                    ve.tensor_scalar(dbuf[:, oh * 512:(oh + 1) * 512], pb[2 + oh][:, :],
                                     ftile_t[:, i:i + 1], None, OP.mult).then_inc(s2.h, 1)
                s2.v += 17

            @blk.sync
            def _(sy):
                sy.wait_ge(s2.h, b2 + 17)
                sy.dma_start(out=delta_contrib.ap()[i * 128:(i + 1) * 128, :],
                             in_=dbuf[:, :]).then_inc(sd.h, 16)
                sd.v += 16
                sy.wait_ge(sd.h, sd.v)
    Dd.pop()

    with nc.Block() as blk:
        @blk.gpsimd
        def _(gp):
            gp.collective_compute("AllReduce", OP.add, replica_groups=groups8,
                                  ins=[delta_contrib.ap()], outs=[delta_all.ap()[0:MOE_M, :]]).then_inc(sco.h, 1)
            sco.v += 1
            gp.wait_ge(sco.h, sco.v)

    if upto == "P7h":
        tap_copy_dram("delta", delta_all, [MOE_M, D], F32)
        emit_store_y()
        return nc, tap_tensors

    with nc.Block() as blk:
        @blk.vector
        def _(ve):
            ve.memset(dbuf[:, :], 0.0)

        @blk.sync
        def _(sy):
            for k in range(16):
                sy.dma_start(out=own_scale.ap()[k * 128:(k + 1) * 128, :],
                             in_=oc_f[:, 0:1]).then_inc(sd.h, 16)
            sd.v += 16 * 16
            sy.wait_ge(sd.h, sd.v)
    with nc.Block() as blk:
        @blk.sync
        def _(sy):
            sy.dma_start(out=jcols_s[:], in_=jcols0.ap()).then_inc(sd.h, 16)
            sy.dma_start(out=bigc_s[:], in_=bigcol0.ap()).then_inc(sd.h, 16)
            sd.v += 2 * 16
            sy.wait_ge(sd.h, sd.v)
            for k in range(17):
                sy.dma_start(out=own_inv.ap()[k * 128:(k + 1) * 128, :],
                             in_=bigc_s[:, :]).then_inc(sd.h, 16)
            sd.v += 17 * 16
            sy.wait_ge(sd.h, sd.v)

    with nc.Block() as blk:
        @blk.gpsimd
        def _(gp):
            for i in range(16):
                gp.indirect_dma_start(
                    own_scale.ap(), IndirectOffsetOnAxis(lidxc[:, i:i + 1], 0),
                    swv_t[:, i:i + 1], None,
                ).then_inc(sg.h, 16)
                sg.v += 16
                gp.wait_ge(sg.h, sg.v)

    jvals_e = fscr[:, 0:16]
    with nc.Block() as blk:
        @blk.vector
        def _(ve):
            ve.tensor_copy(jvals_e, jcols_s[:, :])

    with nc.Block() as blk:
        @blk.gpsimd
        def _(gp):
            for i in range(16):
                gp.indirect_dma_start(
                    own_inv.ap(), IndirectOffsetOnAxis(lidxc[:, i:i + 1], 0),
                    jvals_e[:, i:i + 1], None,
                ).then_inc(sg.h, 16)
                sg.v += 16
                gp.wait_ge(sg.h, sg.v)

    with nc.Block() as blk:
        @blk.sync
        def _(sy):
            for i in range(16):
                sy.dma_start(out=invf[:, i:i + 1],
                             in_=own_inv.ap()[i * 128:(i + 1) * 128, :]).then_inc(sd.h, 16)
            sd.v += 16 * 16
            sy.wait_ge(sd.h, sd.v)



    dmask = fscr[:, 32:48]
    dbuf2v = vf[:, 0:1024]
    with nc.Block() as blk:
        @blk.vector
        def _(ve):
            ve.tensor_scalar(invx[:, :], invf[:, :], float(MOE_M), None, OP.min).then_inc(sv.h, 1)
            sv.v += 1
            ve.wait_ge(sv.h, sv.v)
            ve.tensor_scalar(dmask[:, :], invf[:, :], float(MOE_M) - 0.5, None, OP.is_lt)

    with nc.Block() as blk:
        b2g, bgg, bdg = s2.v, sg.v, sd.v

        @blk.gpsimd
        def _(gp):
            for i in range(16):
                if i > 0:
                    gp.wait_ge(s2.h, b2g + i)
                gp.indirect_dma_start(
                    dbuf[:, :], None,
                    delta_all.ap(), IndirectOffsetOnAxis(invx[:, i:i + 1], 0),
                ).then_inc(sg.h, 16)
                sg.v += 16
                gp.wait_ge(sg.h, sg.v)

        @blk.vector
        def _(ve):
            for i in range(16):
                ve.wait_ge(sg.h, bgg + 16 * (i + 1))
                if i > 0:
                    ve.wait_ge(sd.h, bdg + 16 * i)
                ve.tensor_scalar(dbuf2v[:, :], dbuf[:, :], dmask[:, i:i + 1],
                                 None, OP.mult).then_inc(s2.h, 1)
            s2.v += 16

        @blk.sync
        def _(sy):
            for i in range(16):
                sy.wait_ge(s2.h, b2g + i + 1)
                sy.dma_start(out=own_delta.ap()[i * 128:(i + 1) * 128, :],
                             in_=dbuf2v[:, :]).then_inc(sd.h, 16)
            sd.v += 16 * 16
            sy.wait_ge(sd.h, sd.v)

    with nc.Block() as blk:
        @blk.sync
        def _(sy):
            sy.dma_start(out=sigf[0:1, 0:NTOK],
                         in_=bass.AP(own_scale, 0, [[NTOK, 1], [1, NTOK]])).then_inc(sd.h, 16)
            sd.v += 16
            sy.wait_ge(sd.h, sd.v)

    if upto == "P7i2":
        tap_fm("gating", swv_t, nch=1, width=16, dtype=F32)
        tap_fm("gatingf", ftile_t, nch=1, width=16, dtype=F32)
        tap_fm("sigf", sigf, nch=1, width=MOE_M, dtype=F32)
        tap_fm("gwb", gw_b, nch=1, width=NCH * 16, dtype=BF16)
        tap_fm("vf", vf, nch=1, width=MOE_M, dtype=F32)
        tap_fm("gmisc", gmisc, nch=1, width=128, dtype=F32)
        tap_copy_dram("oscale", own_scale, [NTOK + 128, 1], F32)
        tap_copy_dram("odelta", own_delta, [NTOK, D], F32)
        tap_copy_dram("dall", delta_all, [MOE_M + 128, D], F32)
        tap_copy_dram("oinv", own_inv, [NTOK + 128, 1], F32)
        tap_copy_dram("dcon", delta_contrib, [MOE_M, D], F32)
        emit_store_y()
        return nc, tap_tensors

    with nc.Block() as blk:
        b1, b2, b3 = s1.v, s2.v, s3.v

        @blk.tensor
        def _(te):
            n = 0
            for qb in range(NQB):
                if qb > 0:
                    te.wait_ge(s2.h, b2 + 9 * qb)
                te.matmul(pb[4][:, :], or_f[0:1, :], sigf[0:1, qb * QB:(qb + 1) * QB],
                          start=True, stop=True).then_inc(s1.h, 1)
                s1.v += 1
                for oc in range(NCH):
                    if oc >= 2:
                        te.wait_ge(s2.h, b2 + 9 * qb + oc)
                    for hh in range(4):
                        te.matmul(pb[oc % 2][:, :],
                                  sw3T_b[:, hh * D + oc * 128: hh * D + (oc + 1) * 128],
                                  sh[:, hh * NTOK + qb * QB: hh * NTOK + qb * QB + QB],
                                  start=(hh == 0), stop=(hh == 3)).then_inc(s1.h, 1)
                    s1.v += 4

        @blk.vector
        def _(ve):
            n1 = b1
            for qb in range(NQB):
                ve.wait_ge(s1.h, n1 + 1)
                ve.tensor_copy(scb[:, :], pb[4][:, :]).then_inc(s2.h, 1)
                n1 += 1
                for oc in range(NCH):
                    ve.wait_ge(s1.h, n1 + 4)
                    ve.scalar_tensor_tensor(dbuf[:, 0:512], pb[oc % 2][:, :],
                                            sb3_s[:, oc:oc + 1], scb[:, :],
                                            OP.add, OP.mult).then_inc(sv.h, 1)
                    sv.v += 1
                    ve.wait_ge(sv.h, sv.v)
                    ve.tensor_tensor(xres[:, oc * NTOK + qb * QB: oc * NTOK + qb * QB + QB],
                                     dbuf[:, 0:512],
                                     xres[:, oc * NTOK + qb * QB: oc * NTOK + qb * QB + QB],
                                     OP.add).then_inc(s2.h, 1)
                    n1 += 4
            s2.v += NQB * 9

    if upto == "P7i3":
        tap_fm("xcomb", xres[:, :], dtype=F32)
        tap_copy_dram("oscale", own_scale, [NTOK, 1], F32)
        emit_store_y()
        return nc, tap_tensors

    with nc.Block() as blk:
        b1, b2, bd = s1.v, s2.v, sd.v

        @blk.sync
        def _(sy):
            n = 0
            for qb in range(NQB):
                for oc in range(NCH):
                    for k in range(4):
                        if n >= 2:
                            sy.wait_ge(s1.h, b1 + n - 1)
                        sy.dma_start(out=dbuf[:, (n % 2) * 512:(n % 2) * 512 + 128],
                                     in_=own_delta.ap()[qb * QB + k * 128: qb * QB + (k + 1) * 128,
                                                        oc * 128:(oc + 1) * 128]).then_inc(sd.h, 16)
                        n += 1
            sd.v += 128 * 16

        @blk.tensor
        def _(te):
            n = 0
            for qb in range(NQB):
                for oc in range(NCH):
                    for k in range(4):
                        te.wait_ge(sd.h, bd + 16 * (n + 1))
                        if n >= 1:
                            te.wait_ge(s2.h, b2 + n)
                        te.transpose(pb[5][:, 0:128],
                                     dbuf[:, (n % 2) * 512:(n % 2) * 512 + 128], idf_s[:]).then_inc(s1.h, 1)
                        n += 1
            s1.v += 128

        @blk.vector
        def _(ve):
            n = 0
            for qb in range(NQB):
                for oc in range(NCH):
                    for k in range(4):
                        ve.wait_ge(s1.h, b1 + n + 1)
                        cols = slice(oc * NTOK + qb * QB + k * 128, oc * NTOK + qb * QB + (k + 1) * 128)
                        ve.tensor_tensor(xres[:, cols], pb[5][:, 0:128],
                                         xres[:, cols], OP.add).then_inc(s2.h, 1)
                        n += 1
            s2.v += 128
    E.pop()
    Dd.pop() if False else None
    A.pop()
    Bb.pop()
    C.pop() if False else None
    C.off = 0
    C.marks.clear()

    if upto == "P7":
        tap_fm("xffn", xres[:, :], dtype=F32)
        tap_copy_dram("odelta", own_delta, [NTOK, D], F32)
        tap_copy_dram("oscale", own_scale, [NTOK + 128, 1], F32)
        emit_store_y()
        return nc, tap_tensors

    E.off = 0
    E.marks.clear()

    A.push()
    xn3 = A.take(NCH * NTOK)
    emit_ln(xres[:, :], xn3)

    C.off = 0
    C.marks.clear()
    wqkv_b = C.take(NCH * D)
    emit_wload(w_qkvT, wqkv_b, NCH, D)
    Bb.push()
    qkv_own = Bb.take(NCH * NTOK)
    with nc.Block() as blk:
        b1, b2 = s1.v, s2.v

        @blk.tensor
        def _(te):
            n = 0
            for oc in range(NCH):
                for qb in range(NQB):
                    if n >= 2:
                        te.wait_ge(s2.h, b2 + n - 1)
                    for c in range(NCH):
                        te.matmul(pb[n % 2][:, :], wqkv_b[:, c * D + oc * 128: c * D + (oc + 1) * 128],
                                  xn3[:, c * NTOK + qb * QB: c * NTOK + qb * QB + QB],
                                  start=(c == 0), stop=(c == NCH - 1)).then_inc(s1.h, 1)
                    n += 1
            s1.v += 32 * NCH

        @blk.vector
        def _(ve):
            n = 0
            for oc in range(NCH):
                for qb in range(NQB):
                    ve.wait_ge(s1.h, b1 + NCH * (n + 1))
                    ve.tensor_copy(qkv_own[:, oc * NTOK + qb * QB: oc * NTOK + qb * QB + QB],
                                   pb[n % 2][:, :]).then_inc(s2.h, 1)
                    n += 1
            s2.v += 32

    Dd.push()
    stg4 = Dd.take(2 * 128)
    with nc.Block() as blk:
        b1, b2, bd = s1.v, s2.v, sd.v

        @blk.tensor
        def _(te):
            n = 0
            for i in range(16):
                for c in range(NCH):
                    if n >= 1:
                        te.wait_ge(s2.h, b2 + n)
                    te.transpose(pbh[:, (n % 2) * 128:(n % 2) * 128 + 128],
                                 qkv_own[:, c * NTOK + i * 128: c * NTOK + (i + 1) * 128],
                                 idb_s[:]).then_inc(s1.h, 1)
                    n += 1
            s1.v += 128

        @blk.vector
        def _(ve):
            n = 0
            for i in range(16):
                for c in range(NCH):
                    ve.wait_ge(s1.h, b1 + n + 1)
                    if n >= 2:
                        ve.wait_ge(sd.h, bd + 16 * (n - 1))
                    ve.tensor_copy(stg4[:, (n % 2) * 128:(n % 2) * 128 + 128],
                                   pbh[:, (n % 2) * 128:(n % 2) * 128 + 128]).then_inc(s2.h, 1)
                    n += 1
            s2.v += 128

        @blk.sync
        def _(sy):
            n = 0
            for i in range(16):
                for c in range(NCH):
                    sy.wait_ge(s2.h, b2 + n + 1)
                    sy.dma_start(out=qkv_contrib.ap()[NTOK + i * 128: NTOK + (i + 1) * 128,
                                                      c * 128:(c + 1) * 128],
                                 in_=stg4[:, (n % 2) * 128:(n % 2) * 128 + 128]).then_inc(sd.h, 16)
                    n += 1
            sd.v += 128 * 16
    Dd.pop()
    with nc.Block() as blk:
        @blk.sync
        def _(sy):
            for c in range(NCH):
                fmv = bass.AP(qkv_contrib, c * 128 * NTOK, [[NTOK, 128], [1, NTOK]])
                sy.dma_start(out=fmv, in_=qkv_own[:, c * NTOK:(c + 1) * NTOK]).then_inc(sd.h, 16)
            sd.v += NCH * 16
            sy.wait_ge(sd.h, sd.v)

    with nc.Block() as blk:
        @blk.gpsimd
        def _(gp):
            gp.collective_compute("AllGather", OP.bypass, replica_groups=groups4,
                                  ins=[qkv_contrib.ap()], outs=[qkv_ag.ap()]).then_inc(sco.h, 1)
            sco.v += 1
            gp.wait_ge(sco.h, sco.v)

    if upto == "P8a":
        tap_copy_dram("qkvag", qkv_ag, [4 * 2 * NTOK, D], BF16)
        emit_store_y()
        return nc, tap_tensors

    A.pop()
    A.push()
    ctxbuf = A.take(NCH * NTOK)
    C.off = 0
    C.marks.clear()
    kb = C.take(8192)
    vb_c = C.take(62 * 132)
    Dd.push()
    vb_d = Dd.take(2 * 132)
    ctmp = Dd.take(2 * QB)
    ptl = [Dd.take(QB), Dd.take(QB)]
    E.push()
    mrun2 = E.take(8 * QB)
    permax2 = E.take(8)
    negm82 = E.take(8)
    biast2 = E.take(8)
    recipf2 = E.take(1024)
    rects2 = E.take(1024)
    scrb2 = Dd.take(128)
    negmrow2 = Dd.take(16)

    with nc.Block() as blk:
        @blk.vector
        def _(ve):
            ve.memset(vb_c[:, :], 1.0)
            ve.memset(vb_d[:, :], 1.0)

    for hp in range(NH // 2):
        with nc.Block() as blk:
            @blk.sync
            def _(sy):
                for r in range(4):
                    fmv = bass.AP(qkv_ag, (r * 2 * NTOK) * D + hp * 128 * NTOK,
                                  [[NTOK, 128], [1, NTOK]])
                    sy.dma_start(out=kb[:, r * NTOK:(r + 1) * NTOK], in_=fmv).then_inc(sd.h, 16)
                    tmv = bass.AP(qkv_ag, (r * 2 * NTOK + NTOK) * D + hp * 128,
                                  [[D, 128], [128 * D, 16], [1, 128]])
                    dstv = bass.AP(vb.tensor, vb.offset + r * 16 * 128,
                                   [[vb.tensor.shape[1], 128], [128, 16], [1, 128]]) if False else None
                    sy.dma_start(out=bass.AP(kb.tensor, kb.offset, [[1, 1], [1, 1]]), in_=bass.AP(qkv_ag, 0, [[1, 1], [1, 1]])) if False else None
                    n0 = r * 16
                    for kt in range(16):
                        K = n0 + kt
                        src = bass.AP(qkv_ag, (r * 2 * NTOK + NTOK + kt * 128) * D + hp * 128,
                                      [[D, 128], [64, 2], [1, 64]])
                        if K < 62:
                            dst = bass.AP(vb_c.tensor, vb_c.offset + K * 132,
                                          [[vb_c.tensor.shape[1], 128], [66, 2], [1, 64]])
                        else:
                            dst = bass.AP(vb_d.tensor, vb_d.offset + (K - 62) * 132,
                                          [[vb_d.tensor.shape[1], 128], [66, 2], [1, 64]])
                        sy.dma_start(out=dst, in_=src).then_inc(sd.h, 16)
                sd.v += 4 * 17 * 16
                sy.wait_ge(sd.h, sd.v)

        with nc.Block() as blk:
            b1, b2, bv = s1.v, s2.v, sv.v
            aktsL = list(range(0, 64, 4))
            ntile = 2 * 4 * len(aktsL)

            @blk.tensor
            def _(te):
                a = 0
                for qb in range(NQB):
                    for kt in aktsL:
                        for h01 in range(2):
                            if a >= 4:
                                te.wait_ge(s2.h, b2 + a - 3)
                            te.matmul(pb[a % 4][:, :],
                                      kb[h01 * 64:(h01 + 1) * 64, kt * 128:(kt + 1) * 128],
                                      qkv_own[h01 * 64:(h01 + 1) * 64,
                                              hp * NTOK + qb * QB: hp * NTOK + (qb + 1) * QB],
                                      start=True, stop=True).then_inc(s1.h, 1)
                            a += 1
                s1.v += ntile
                te.wait_ge(s2.h, b2 + ntile + 1)
                te.transpose(pbh[:, 0:128], scrb2[:, 0:128], idb_s[:]).then_inc(s1.h, 1)
                te.wait_ge(s2.h, b2 + ntile + 2)
                te.transpose(pbh[:, 128:256], scrb2[:, 0:128], idb_s[:]).then_inc(s1.h, 1)
                te.wait_ge(s2.h, b2 + ntile + 3)
                te.matmul(pb[4][:, 0:16], or_b[0:1, :], negmrow2[0:1, 0:16],
                          start=True, stop=True).then_inc(s1.h, 1)
                s1.v += 3

            @blk.vector
            def _(ve):
                a = 0
                for qb in range(NQB):
                    for kt in aktsL:
                        for h01 in range(2):
                            slot = h01 * 4 + qb
                            ve.wait_ge(s1.h, b1 + a + 1)
                            if kt == 0:
                                ve.tensor_copy(mrun2[:, slot * QB:(slot + 1) * QB],
                                               pb[a % 4][:, :]).then_inc(s2.h, 1)
                            else:
                                ve.tensor_tensor(mrun2[:, slot * QB:(slot + 1) * QB],
                                                 pb[a % 4][:, :],
                                                 mrun2[:, slot * QB:(slot + 1) * QB],
                                                 OP.max).then_inc(s2.h, 1)
                            a += 1
                s2.v += ntile
                for k in range(8):
                    bi = ve.reduce_max(permax2[:, k:k + 1], mrun2[:, k * QB:(k + 1) * QB], axis=AX.X)
                bi.then_inc(sv.h, 1)
                ve.wait_ge(sv.h, bv + 1)
                ve.tensor_scalar(scrb2[:, 0:8], permax2[:, 0:8], 0.125, None, OP.mult).then_inc(s2.h, 1)
                ve.wait_ge(s1.h, b1 + ntile + 1)
                ve.reduce_max(negm82[0:8, 0:1], pbh[0:8, 0:128], axis=AX.X, negate=True).then_inc(sv.h, 1)
                ve.wait_ge(sv.h, bv + 2)
                ve.tensor_copy(scrb2[0:8, 0:1], negm82[0:8, 0:1]).then_inc(s2.h, 1)
                ve.wait_ge(s1.h, b1 + ntile + 2)
                ve.tensor_copy(negmrow2[0:1, 0:16], pbh[0:1, 128:144]).then_inc(s2.h, 1)
                ve.wait_ge(s1.h, b1 + ntile + 3)
                ve.tensor_copy(biast2[:, 0:8], pb[4][:, 0:8])
                ve.engine_nop().then_inc(s2.h, 1)
                s2.v += 4
                sv.v += 2

        with nc.Block() as blk:
            b1, b2, b3 = s1.v, s2.v, s3.v
            tiles = []
            for qb in range(NQB):
                for kt in range(64):
                    for h01 in range(2):
                        tiles.append((qb, kt, h01))
            ntl = len(tiles)
            bd = sd.v
            pe_after_B = [0] * ntl
            pe_after_pv = [0] * ntl
            act_after = [0] * ntl
            epi_recip = {}
            epi_free = {}
            e5_pos = {}
            pe_c = act_c = dve_c = 0
            for t, (qb, kt, h01) in enumerate(tiles):
                act_c += 1
                act_after[t] = act_c
                if t + 1 == ntl or tiles[t + 1][0] != qb:
                    epi_recip[qb] = dve_c + 1
                    epi_free[qb] = (dve_c + 4, dve_c + 5)
                    e5_pos[qb] = dve_c + 5
                    dve_c += 5
            for step in range(ntl + 1):
                if step < ntl:
                    pe_c += 1
                    pe_after_B[step] = pe_c
                u = step - 1 if step < ntl else ntl - 1
                if u >= 0:
                    pe_c += 1
                    pe_after_pv[u] = pe_c
                    if u + 1 == ntl or tiles[u + 1][0] != tiles[u][0]:
                        pe_c += 2

            def vtile(kt, h01):
                if kt < 62:
                    return vb_c[:, kt * 132 + h01 * 66: kt * 132 + h01 * 66 + 66]
                return vb_d[:, (kt - 62) * 132 + h01 * 66: (kt - 62) * 132 + h01 * 66 + 66]

            @blk.tensor
            def _(te):
                def emit_pv(u):
                    qb, kt, h01 = tiles[u]
                    te.wait_ge(s3.h, b3 + act_after[u])
                    if kt == 0 and qb > 0:
                        te.wait_ge(s2.h, b2 + epi_free[qb - 1][h01])
                    te.matmul(pb[h01][0:66, :], vtile(kt, h01), ptl[u % 2],
                              start=(kt == 0), stop=(kt == 63)).then_inc(s1.h, 1)
                    if u + 1 == ntl or tiles[u + 1][0] != qb:
                        te.wait_ge(s2.h, b2 + epi_recip[qb])
                        te.matmul(pb[3][0:64, :], or_f[0:1, 0:64], recipf2[0:1, 0:QB],
                                  start=True, stop=True).then_inc(s1.h, 1)
                        te.matmul(pb[2][0:64, :], or_f[0:1, 0:64], recipf2[0:1, QB:2 * QB],
                                  start=True, stop=True).then_inc(s1.h, 1)

                for t, (qb, kt, h01) in enumerate(tiles):
                    if t >= 2:
                        te.wait_ge(s3.h, b3 + act_after[t - 2])
                    te.matmul(pb[4 + t % 2][:, :],
                              kb[h01 * 64:(h01 + 1) * 64, kt * 128:(kt + 1) * 128],
                              qkv_own[h01 * 64:(h01 + 1) * 64,
                                      hp * NTOK + qb * QB: hp * NTOK + (qb + 1) * QB],
                              start=True, stop=True).then_inc(s1.h, 1)
                    if t >= 1:
                        emit_pv(t - 1)
                emit_pv(ntl - 1)
                s1.v += pe_c

            @blk.scalar
            def _(sc):
                for t, (qb, kt, h01) in enumerate(tiles):
                    slot = h01 * 4 + qb
                    sc.wait_ge(s1.h, b1 + pe_after_B[t])
                    sc.activation(ptl[t % 2], pb[4 + t % 2][:, :], AF.Exp,
                                  bias=biast2[:, slot:slot + 1], scale=0.125).then_inc(s3.h, 1)
                s3.v += ntl

            @blk.vector
            def _(ve):
                for t, (qb, kt, h01) in enumerate(tiles):
                    if t + 1 == ntl or tiles[t + 1][0] != qb:
                        ve.wait_ge(s1.h, b1 + pe_after_pv[t])
                        ve.reciprocal(recipf2[0:1, 0:QB], pb[0][64:65, :])
                        ve.reciprocal(recipf2[0:1, QB:2 * QB], pb[1][64:65, :]).then_inc(sv.h, 1)
                        ve.wait_ge(sv.h, sv.v + 1)
                        sv.v += 1
                        ve.engine_nop().then_inc(s2.h, 1)
                        mm_base = b1 + pe_after_pv[t]
                        ve.wait_ge(s1.h, mm_base + 1)
                        ve.tensor_copy(rects2[0:64, 0:QB], pb[3][0:64, :]).then_inc(s2.h, 1)
                        ve.wait_ge(s1.h, mm_base + 2)
                        ve.tensor_copy(rects2[0:64, QB:2 * QB], pb[2][0:64, :]).then_inc(s2.h, 1)
                        ve.tensor_tensor(ctxbuf[0:64, hp * NTOK + qb * QB: hp * NTOK + (qb + 1) * QB],
                                         pb[0][0:64, :], rects2[0:64, 0:QB],
                                         OP.mult).then_inc(s2.h, 1)
                        if qb >= 2:
                            ve.wait_ge(sd.h, bd + 16 * (qb - 1))
                        ve.tensor_tensor(ctmp[0:64, (qb % 2) * QB:(qb % 2 + 1) * QB],
                                         pb[1][0:64, :], rects2[0:64, QB:2 * QB],
                                         OP.mult).then_inc(s2.h, 1)
                s2.v += dve_c

            @blk.sync
            def _(sy):
                for qb in range(NQB):
                    sy.wait_ge(s2.h, b2 + e5_pos[qb])
                    sy.dma_start(out=ctxbuf[64:128, hp * NTOK + qb * QB: hp * NTOK + (qb + 1) * QB],
                                 in_=ctmp[0:64, (qb % 2) * QB:(qb % 2 + 1) * QB]).then_inc(sd.h, 16)
                sd.v += NQB * 16
                sy.wait_ge(sd.h, sd.v)
    Dd.pop()
    E.pop()

    if upto == "P8b":
        tap_fm("ctxb", ctxbuf, dtype=BF16)
        emit_store_y()
        return nc, tap_tensors

    C.off = 0
    C.marks.clear()
    wo_b = C.take(NCH * D)
    emit_wload(w_l1oT, wo_b, NCH, D)
    with nc.Block() as blk:
        b1, b2 = s1.v, s2.v

        @blk.tensor
        def _(te):
            n = 0
            for oc in range(NCH):
                for qb in range(NQB):
                    if n >= 2:
                        te.wait_ge(s2.h, b2 + n - 1)
                    for c in range(NCH):
                        te.matmul(pb[n % 2][:, :], wo_b[:, c * D + oc * 128: c * D + (oc + 1) * 128],
                                  ctxbuf[:, c * NTOK + qb * QB: c * NTOK + qb * QB + QB],
                                  start=(c == 0), stop=(c == NCH - 1)).then_inc(s1.h, 1)
                    n += 1
            s1.v += 32 * NCH

        @blk.vector
        def _(ve):
            n = 0
            for oc in range(NCH):
                for qb in range(NQB):
                    ve.wait_ge(s1.h, b1 + NCH * (n + 1))
                    sl = slice(oc * NTOK + qb * QB, oc * NTOK + qb * QB + QB)
                    ve.tensor_tensor(xres[:, sl], pb[n % 2][:, :], xres[:, sl],
                                     OP.add).then_inc(s2.h, 1)
                    n += 1
            s2.v += 32

    if upto == "P8":
        tap_fm("xl1a", xres[:, :], dtype=F32)
        emit_store_y()
        return nc, tap_tensors

    E.push()
    sstg = E.take(D)
    fstg = E.take(D)
    f2stg = E.take(D)
    Dd.push()
    wstg = Dd.take(D)
    with nc.Block() as blk:
        bd, bg2, b4 = sd.v, sg.v, s4.v

        @blk.sync
        def _(sy):
            n = 0
            for i in range(6):
                for c in range(NCH):
                    if n >= 1:
                        sy.wait_ge(s4.h, b4 + 2 * n)
                    sy.dma_start(out=sstg[:, :], in_=scT.ap()[i, c * 128:(c + 1) * 128, :]).then_inc(sd.h, 16)
                    n += 1
            sd.v += 48 * 16

        @blk.gpsimd
        def _(gp):
            n = 0
            for i in range(6):
                for c in range(NCH):
                    if n >= 1:
                        gp.wait_ge(s4.h, b4 + 2 * n - 1)
                    gp.dma_start(out=fstg[:, :], in_=qwT.ap()[i, c * 128:(c + 1) * 128, :]).then_inc(sg.h, 16)
                    gp.wait_ge(sg.h, bg2 + 16 * (2 * n + 1))
                    gp.wait_ge(sd.h, bd + 16 * (n + 1))
                    gp.tensor_scalar(f2stg[:, :], fstg[:, :], 1.0 / 7.5, -1.0,
                                     OP.mult, OP.add).then_inc(s4.h, 1)
                    gp.tensor_tensor(wstg[:, :], f2stg[:, :], sstg[:, :], OP.mult).then_inc(s4.h, 1)
                    gp.dma_start(out=wq_dram.ap()[i, c * 128:(c + 1) * 128, :],
                                 in_=wstg[:, :]).then_inc(sg.h, 16)
                    gp.wait_ge(sg.h, bg2 + 16 * (2 * n + 2))
                    n += 1
            s4.v += 96
            sg.v += 96 * 16
    Dd.pop()
    E.pop()

    if upto == "P9a":
        tap_copy_dram("wq", wq_dram, [6, D, D], BF16)
        emit_store_y()
        return nc, tap_tensors

    l1b_s = nc.alloc_sbuf_tensor("l1b_s", [128, 48], F32)
    lng_s = nc.alloc_sbuf_tensor("lng_s", [128, 24], F32)
    lnb_s = nc.alloc_sbuf_tensor("lnb_s", [128, 24], F32)
    with nc.Block() as blk:
        @blk.sync
        def _(sy):
            for i in range(6):
                for oc in range(NCH):
                    sy.dma_start(out=l1b_s[:, i * 8 + oc: i * 8 + oc + 1],
                                 in_=bass.AP(l1_biases, i * D + oc * 128, [[1, 128], [1, 1]])).then_inc(sd.h, 16)
            for blki in range(3):
                for oc in range(NCH):
                    sy.dma_start(out=lng_s[:, blki * 8 + oc: blki * 8 + oc + 1],
                                 in_=bass.AP(ln_g, blki * D + oc * 128, [[1, 128], [1, 1]])).then_inc(sd.h, 16)
                    sy.dma_start(out=lnb_s[:, blki * 8 + oc: blki * 8 + oc + 1],
                                 in_=bass.AP(ln_b, blki * D + oc * 128, [[1, 128], [1, 1]])).then_inc(sd.h, 16)
            sd.v += (48 + 48) * 16
            sy.wait_ge(sd.h, sd.v)

    A.pop()
    A.push()
    hb_t = A.take(4096)
    ybf_t = A.take(4096)
    g_t = A.take(4096)
    sqs_t = A.take(512)
    abb3 = A.take(1024)
    Dd.push()
    t1b_t = Dd.take(512)
    E.push()
    h_t = E.take(4096)
    lr0 = E.take(512)
    lr1 = E.take(512)
    lr2 = E.take(512)
    tb_t = A.take(512)

    C.off = 0
    C.marks.clear()
    wt_b = C.take(NCH * D)
    lab_b = C.take(NCH * 32)
    lbb_b = C.take(D)

    def l1_linear(i, rhs_t, out_cb):
        emit_wload(wq_dram.ap()[i], wt_b, NCH, D)
        emit_wload(laT.ap()[i], lab_b, NCH, 32)
        emit_wload(lbT.ap()[i], lbb_b, 1, D, rows=32)
        with nc.Block() as blk:
            b1, b2 = s1.v, s2.v

            @blk.tensor
            def _(te):
                for c in range(NCH):
                    te.matmul(pb[5][0:32, :], lab_b[:, c * 32:(c + 1) * 32],
                              rhs_t[:, c * 512:(c + 1) * 512],
                              start=(c == 0), stop=(c == NCH - 1)).then_inc(s1.h, 1)
                s1.v += NCH
                for oc in range(NCH):
                    if oc >= 2:
                        te.wait_ge(s2.h, b2 + oc)
                    for c in range(NCH):
                        te.matmul(pb[oc % 2][:, :], wt_b[:, c * D + oc * 128: c * D + (oc + 1) * 128],
                                  rhs_t[:, c * 512:(c + 1) * 512],
                                  start=(c == 0), stop=False).then_inc(s1.h, 1)
                    te.wait_ge(s2.h, b2 + 1)
                    te.matmul(pb[oc % 2][:, :], lbb_b[0:32, oc * 128:(oc + 1) * 128],
                              t1b_t[0:32, :], start=False, stop=True).then_inc(s1.h, 1)
                s1.v += NCH * (NCH + 1)

            @blk.vector
            def _(ve):
                ve.wait_ge(s1.h, b1 + NCH)
                ve.tensor_copy(t1b_t[0:32, :], pb[5][0:32, :]).then_inc(s2.h, 1)
                for oc in range(NCH):
                    ve.wait_ge(s1.h, b1 + NCH + (NCH + 1) * (oc + 1))
                    out_cb(ve, oc)
                s2.v += 1 + NCH

    for qb in range(NQB):
        emit_ln(xres[:, :], hb_t, w=QB, dst_f=h_t, nch=NCH, cs=NTOK, t0=qb * QB)
        for blki in range(3):
            i0, i1 = 2 * blki, 2 * blki + 1

            def cb0(ve, oc, i0=i0):
                ve.tensor_scalar(ybf_t[:, oc * 512:(oc + 1) * 512], pb[oc % 2][:, :],
                                 l1b_s[:, i0 * 8 + oc: i0 * 8 + oc + 1], None,
                                 OP.add).then_inc(s2.h, 1)
            l1_linear(i0, hb_t, cb0)

            with nc.Block() as blk:
                b1, b2, b3 = s1.v, s2.v, s3.v

                @blk.scalar
                def _(sc):
                    for c in range(NCH):
                        if c > 0:
                            sc.wait_ge(s1.h, b1 + 2 * c)
                        sc.activation(sqs_t[:, :], ybf_t[:, c * 512:(c + 1) * 512],
                                      AF.Square).then_inc(s3.h, 1)
                    s3.v += NCH

                @blk.tensor
                def _(te):
                    for c in range(NCH):
                        te.matmul(pb[2][0:1, :], oc_b[:, 0:1],
                                  ybf_t[:, c * 512:(c + 1) * 512],
                                  start=(c == 0), stop=(c == NCH - 1)).then_inc(s1.h, 1)
                        te.wait_ge(s3.h, b3 + c + 1)
                        te.matmul(pb[3][0:1, :], oc_b[:, 0:1], sqs_t[:, :],
                                  start=(c == 0), stop=(c == NCH - 1)).then_inc(s1.h, 1)
                    s1.v += 2 * NCH

                @blk.vector
                def _(ve):
                    ve.wait_ge(s1.h, b1 + 2 * NCH)
                    ve.tensor_copy(lr0[0:1, :], pb[2][0:1, :]).then_inc(sv.h, 1)
                    sv.v += 1
                    ve.wait_ge(sv.h, sv.v)
                    ve.tensor_copy(lr1[0:1, :], pb[3][0:1, :]).then_inc(sv.h, 1)
                    sv.v += 1
                    ve.wait_ge(sv.h, sv.v)
                    ve.tensor_scalar(lr2[0:1, :], lr1[0:1, :], 1.0 / D, float(EPS), OP.mult, OP.add).then_inc(sv.h, 1)
                    sv.v += 1
                    ve.wait_ge(sv.h, sv.v)
                    ve.tensor_scalar(lr1[0:1, :], lr0[0:1, :], 1.0 / D, None, OP.mult).then_inc(sv.h, 1)
                    sv.v += 1
                    ve.wait_ge(sv.h, sv.v)
                    ve.tensor_tensor(lr0[0:1, :], lr1[0:1, :], lr1[0:1, :], OP.mult).then_inc(sv.h, 1)
                    sv.v += 1
                    ve.wait_ge(sv.h, sv.v)
                    ve.tensor_tensor(lr0[0:1, :], lr2[0:1, :], lr0[0:1, :], OP.subtract).then_inc(s2.h, 1)
                    s2.v += 1

                @blk.scalar
                def _(sc):
                    sc.wait_ge(s2.h, b2 + 1)
                    sc.activation(lr2[0:1, :], lr0[0:1, :], AF.Sqrt).then_inc(s3.h, 1)
                    s3.v += 1

            with nc.Block() as blk:
                @blk.vector
                def _(ve):
                    ve.reciprocal(lr0[0:1, :], lr2[0:1, :]).then_inc(sv.h, 1)
                    sv.v += 1
                    ve.wait_ge(sv.h, sv.v)
                    ve.tensor_tensor(lr2[0:1, :], lr1[0:1, :], lr0[0:1, :], OP.mult).then_inc(sv.h, 1)
                    sv.v += 1
                    ve.wait_ge(sv.h, sv.v)
                    ve.tensor_copy(abb3[0:1, 0:512], lr0[0:1, :])
                    ve.tensor_copy(abb3[0:1, 512:1024], lr2[0:1, :])

            with nc.Block() as blk:
                b1, b2, b3 = s1.v, s2.v, s3.v

                @blk.tensor
                def _(te):
                    te.matmul(pb[2][:, :], or_b[0:1, :], abb3[0:1, 0:512],
                              start=True, stop=True).then_inc(s1.h, 1)
                    te.matmul(pb[3][:, :], or_b[0:1, :], abb3[0:1, 512:1024],
                              start=True, stop=True).then_inc(s1.h, 1)
                    s1.v += 2

                @blk.vector
                def _(ve):
                    ve.wait_ge(s1.h, b1 + 2)
                    for c in range(NCH):
                        sl = slice(c * 512, (c + 1) * 512)
                        ve.tensor_tensor(sqs_t[:, :], ybf_t[:, sl], pb[2][:, :], OP.mult)
                        ve.tensor_tensor(tb_t[:, :], sqs_t[:, :], pb[3][:, :], OP.subtract)
                        ve.tensor_scalar(ybf_t[:, sl], tb_t[:, :],
                                         lng_s[:, blki * 8 + c: blki * 8 + c + 1],
                                         lnb_s[:, blki * 8 + c: blki * 8 + c + 1],
                                         OP.mult, OP.add).then_inc(s2.h, 1)
                    s2.v += NCH

                @blk.scalar
                def _(sc):
                    for c in range(NCH):
                        sc.wait_ge(s2.h, b2 + c + 1)
                        sc.activation(g_t[:, c * 512:(c + 1) * 512],
                                      ybf_t[:, c * 512:(c + 1) * 512], AF.Gelu).then_inc(s3.h, 1)
                    s3.v += NCH

            def cb1(ve, oc, i1=i1):
                sl = slice(oc * 512, (oc + 1) * 512)
                ve.scalar_tensor_tensor(h_t[:, sl], pb[oc % 2][:, :],
                                        l1b_s[:, i1 * 8 + oc: i1 * 8 + oc + 1],
                                        h_t[:, sl], OP.add, OP.add).then_inc(sv.h, 1)
                sv.v += 1
                ve.wait_ge(sv.h, sv.v)
                ve.tensor_copy(hb_t[:, sl], h_t[:, sl]).then_inc(s2.h, 1)
            l1_linear(i1, g_t, cb1)

        with nc.Block() as blk:
            @blk.vector
            def _(ve):
                for c in range(NCH):
                    sl = slice(c * NTOK + qb * QB, c * NTOK + qb * QB + QB)
                    ve.tensor_tensor(xres[:, sl], h_t[:, c * 512:(c + 1) * 512],
                                     xres[:, sl], OP.add)
    E.pop()
    Dd.pop()
    A.pop()

    emit_store_y()
    return nc, tap_tensors


def make_in_maps(inputs):
    xnp = np.asarray(inputs["L0_attn_x"])
    TT = lambda a: np.ascontiguousarray(np.asarray(a).T)
    Cc = lambda a: np.ascontiguousarray(np.asarray(a))

    gw = np.zeros((16, 1024), np.float32)
    gw[0:8] = np.asarray(inputs["L0_ffn_eg_w"])
    gw[8] = np.asarray(inputs["L0_ffn_sg_w"])[0]
    gbv = np.zeros((16, 1), np.float32)
    gbv[0:8, 0] = np.asarray(inputs["L0_ffn_eg_b"])
    gbv[8, 0] = np.asarray(inputs["L0_ffn_sg_b"])[0]

    shared = {
        "w_dqT": TT(inputs["L0_attn_W_dq"]),
        "w_dq": Cc(inputs["L0_attn_W_dq"]),
        "w_uqT": TT(inputs["L0_attn_W_uq"]),
        "w_uq": Cc(inputs["L0_attn_W_uq"]),
        "w_dkvT": TT(inputs["L0_attn_W_dkv"]),
        "w_uk": Cc(inputs["L0_attn_W_uk"]),
        "w_uv": Cc(inputs["L0_attn_W_uv"]),
        "w_oT": TT(inputs["L0_attn_W_o"]),
        "sw1T": TT(inputs["L0_ffn_sw1"]),
        "sw2T": TT(inputs["L0_ffn_sw2"]),
        "sw3T": TT(inputs["L0_ffn_sw3"]),
        "sb1": Cc(inputs["L0_ffn_sb1"]).reshape(-1, 1),
        "sb2": Cc(inputs["L0_ffn_sb2"]).reshape(-1, 1),
        "sb3": Cc(inputs["L0_ffn_sb3"]).reshape(-1, 1),
        "s_ln_g": Cc(inputs["L0_ffn_s_ln_g"]).reshape(-1, 1),
        "s_ln_b": Cc(inputs["L0_ffn_s_ln_b"]).reshape(-1, 1),
        "gwT": TT(gw),
        "gb": gbv,
        "exp_bias": Cc(inputs["L0_ffn_exp_bias"]).reshape(-1, 1),
        "sel_ln_g": Cc(inputs["L0_ffn_sel_ln_g"]).reshape(1, -1),
        "sel_ln_b": Cc(inputs["L0_ffn_sel_ln_b"]).reshape(1, -1),
        "idx_b": np.asarray(inputs["L0_ffn_index_b"], dtype=np.int32),
        "idx_t": np.asarray(inputs["L0_ffn_index_t"], dtype=np.int32),
        "w_qkvT": TT(inputs["L1_attn_W_qkv"]),
        "w_l1oT": TT(inputs["L1_attn_W_o"]),
        "qwT": Cc(np.asarray(inputs["L1_ffn_qweights"], dtype=np.int32).transpose(0, 2, 1)),
        "scT": Cc(np.repeat(np.asarray(inputs["L1_ffn_scales"]), 16, axis=2).transpose(0, 2, 1)),
        "l1_biases": Cc(inputs["L1_ffn_biases"]),
        "laT": Cc(np.asarray(inputs["L1_ffn_lora_a"]).transpose(0, 2, 1)),
        "lbT": Cc(np.asarray(inputs["L1_ffn_lora_b"]).transpose(0, 2, 1)),
        "ln_g": Cc(inputs["L1_ffn_ln_gamma"]),
        "ln_b": Cc(inputs["L1_ffn_ln_beta"]),
    }
    ew1 = np.asarray(inputs["L0_ffn_ew1"])
    ew2 = np.asarray(inputs["L0_ffn_ew2"])
    ew3 = np.asarray(inputs["L0_ffn_ew3"])
    eb1 = np.asarray(inputs["L0_ffn_eb1"])
    eb2 = np.asarray(inputs["L0_ffn_eb2"])
    eb3 = np.asarray(inputs["L0_ffn_eb3"])

    in_maps = []
    for c in range(8):
        gs = _core_blocks(c)
        bb = c // 4
        xo = np.concatenate([xnp[bb, (g % 16) * QB:(g % 16) * QB + QB] for g in gs], axis=0)
        m = dict(shared)
        m["xT"] = np.ascontiguousarray(xo.T)
        m["ew1T"] = TT(ew1[c])
        m["ew2T"] = TT(ew2[c])
        m["ew3T"] = TT(ew3[c])
        m["eb1"] = Cc(eb1[c]).reshape(1, -1)
        m["eb2"] = Cc(eb2[c]).reshape(1, -1)
        m["eb3"] = Cc(eb3[c]).reshape(1, -1)
        meta = np.zeros((128, 4), np.int32)
        for j, g in enumerate(gs):
            meta[:, j] = g
        m["core_meta"] = meta
        oh = np.zeros((128, 8), np.float32)
        oh[:, c] = 1.0
        m["my_onehot"] = oh
        q0v = np.zeros((128, 4), np.float32)
        ktp = np.zeros((128, 256), np.float32)
        for j, g in enumerate(gs):
            q0v[:, j] = (g % 16) * QB
            v = 4 * ((g % 16) + 1)
            ktp[:, j * 64 + v: j * 64 + 64] = -30000.0
        m["q0s"] = q0v
        m["ktpen"] = ktp
        m["meta_f"] = meta.astype(np.float32)
        in_maps.append(m)
    return in_maps


def assemble_output(results):
    y = np.zeros((B, T, D), dtype=np.float32)
    for c in range(8):
        gs = _core_blocks(c)
        bb = c // 4
        yt = results[c]["y"].T
        for j, g in enumerate(gs):
            y[bb, (g % 16) * QB:(g % 16) * QB + QB] = yt[j * QB:(j + 1) * QB]
    return y


def kernel(_taps=(), _upto=None, **inputs):
    nc, tap_tensors = build_program(taps=_taps, upto=_upto)
    in_maps = make_in_maps(inputs)
    res = run_bass_kernel_spmd(nc, in_maps, core_ids=list(range(8)))
    if _taps or _upto:
        return res
    return assemble_output(res.results)
